# version 01
# baseline (speedup 1.0000x reference)
import contextlib
import os
import numpy as np
import ml_dtypes
import concourse.bass as bass
import concourse.mybir as mybir
from concourse.bass_utils import run_bass_kernel_spmd

F32 = mybir.dt.float32
BF16 = mybir.dt.bfloat16
AF = mybir.ActivationFunctionType
ALU = mybir.AluOpType
AX = mybir.AxisListType

NCORES = 8
BATCH = 4
TFULL = 4096
T = 2048
HALO = 8
D = 1024
DI = 2048
DG = 4096
NST = 16
NH = 8
EPS = 1e-5
NPT = DI // 128
NTT = T // 128
BF = ml_dtypes.bfloat16


class Prog:

    ENGINES = ("sync", "vector", "scalar", "tensor", "gpsimd")

    AUTO_SEM = {"vector": "v", "scalar": "s", "gpsimd": "p"}

    def __init__(self):
        self.streams = {e: [] for e in self.ENGINES}
        self.counts = {}
        self.last_op = {}

    def op(self, eng, fn, waits=(), inc=None, inc_by=1):
        w = [x for x in waits if x[1] > 0]
        if eng in self.AUTO_SEM:
            if inc is None:
                inc = self.AUTO_SEM[eng]
            prev = self.last_op.get(eng)
            if prev is not None:
                w.append(prev)
        self._gidx = getattr(self, "_gidx", 0) + 1
        self.streams[eng].append((fn, w, inc, inc_by, self._gidx))
        if inc is not None:
            self.counts[inc] = self.counts.get(inc, 0) + inc_by
            self.last_op[eng] = (inc, self.counts[inc])
        return self.counts.get(inc, 0)

    def dma(self, eng, fn, waits=(), inc=None):
        w = list(waits)
        if inc is not None:
            prev = self.counts.get(inc, 0)
            if prev:
                w.append((inc, prev))
        return self.op(eng, fn, w, inc, 16)

    def val(self, sem):
        return self.counts.get(sem, 0)

    def bar(self, sem):
        return (sem, self.val(sem))

    def emit(self, block, sems):
        import os as _os
        K = int(_os.environ.get("BISECT_K", "10**9".replace("*", "")) or 10**9)
        K = int(_os.environ.get("BISECT_K", str(10**9)))
        for name in self.ENGINES:
            stream = [t for t in self.streams[name] if t[4] <= K]
            if not stream:
                continue

            def make(ops):
                def body(eng):
                    for fn, waits, inc, by, gi in ops:
                        for (s, v) in waits:
                            eng.wait_ge(sems[s], v)
                        instr = fn(eng)
                        if inc is not None:
                            instr.then_inc(sems[inc], by)
                return body

            getattr(block, name)(make(stream))


def make_sems(ctx, nc, names):
    return {k: ctx.enter_context(nc.semaphore(k)) for k in names}


def build_l1():
    nc = bass.Bass("TRN2", target_bir_lowering=False, debug=False,
                   detect_race_conditions=False)
    TH = T + HALO

    xin = nc.dram_tensor("xin", [TH, D], F32, kind="ExternalInput").ap()
    w_in = nc.dram_tensor("w_in", [D, DG], F32, kind="ExternalInput").ap()
    b_in = nc.dram_tensor("b_in", [1, DG], F32, kind="ExternalInput").ap()
    ln_w = nc.dram_tensor("ln_w", [D, 1], F32, kind="ExternalInput").ap()
    ln_b = nc.dram_tensor("ln_b", [D, 1], F32, kind="ExternalInput").ap()
    conv_w = nc.dram_tensor("conv_w", [DI, 4], F32, kind="ExternalInput").ap()
    conv_b = nc.dram_tensor("conv_b", [DI, 1], F32, kind="ExternalInput").ap()
    w_x = nc.dram_tensor("w_x", [DI, 2 * NST], F32, kind="ExternalInput").ap()
    b_xv = nc.dram_tensor("b_xv", [2 * NST, 1], F32, kind="ExternalInput").ap()
    identb = nc.dram_tensor("identb", [128, 128], BF16, kind="ExternalInput").ap()
    ones1 = nc.dram_tensor("ones1", [1, 128], F32, kind="ExternalInput").ap()
    hmask = nc.dram_tensor("hmask", [128, 1], F32, kind="ExternalInput").ap()

    o_xc = nc.dram_tensor("o_xc", [DI, T], BF16, kind="ExternalOutput").ap()
    o_sg = nc.dram_tensor("o_sg", [DI, T], BF16, kind="ExternalOutput").ap()
    o_b = nc.dram_tensor("o_b", [NST, T], F32, kind="ExternalOutput").ap()
    o_c = nc.dram_tensor("o_c", [NST, T], F32, kind="ExternalOutput").ap()

    with contextlib.ExitStack() as ctx:
        sb = lambda n, s, d=F32: ctx.enter_context(nc.sbuf_tensor(n, s, d))
        ps = lambda n, s, d=F32: ctx.enter_context(nc.psum_tensor(n, s, d))

        s_x = [sb(f"s_x{i}", [128, D]) for i in range(2)]
        s_xh = sb("s_xh", [HALO, D])
        s_sq = sb("s_sq", [128, D])
        s_stat = sb("s_stat", [128, 2 * (NTT + 1)])
        s_mu = sb("s_mu", [128, NTT + 1])
        s_nv = sb("s_nv", [128, NTT + 1])
        s_sc = sb("s_sc", [128, NTT + 1])
        s_bcs = sb("s_bcs", [128, NTT + 1])
        s_xn = [sb(f"s_xn{i}", [128, D], BF16) for i in range(2)]
        s_xnh = sb("s_xnh", [HALO, D], BF16)
        s_id = sb("s_id", [128, 128], BF16)
        s_ones1 = sb("s_ones1", [1, 128])
        s_hm = sb("s_hm", [128, 1])
        s_xnT = [sb(f"s_xnT{k}", [128, HALO + T], BF16) for k in range(8)]
        SINGLE_WK = os.environ.get("L1_1WK", "0") == "1"
        s_wk = ([sb("s_wk0", [128, DG])] * 2) if SINGLE_WK else [sb(f"s_wk{k}", [128, DG]) for k in range(2)]
        s_wb = [sb(f"s_wb{k}", [128, DG], BF16) for k in range(8)]
        s_lnw = sb("s_lnw", [128, 8])
        s_lnb = sb("s_lnb", [128, 8])
        s_lnbb = sb("s_lnbb", [128, 8], BF16)
        s_binrow = sb("s_binrow", [1, DG])
        s_bias = sb("s_bias", [128, 32])
        s_cw = sb("s_cw", [128, 4 * NPT])
        s_cb = sb("s_cb", [128, NPT])
        s_xm = sb("s_xm", [128, HALO + T])
        s_xmh = sb("s_xmh", [128, NPT * HALO])
        s_acc = sb("s_acc", [128, T])
        s_xcp = sb("s_xcp", [128, T], BF16)
        s_sgp = [sb(f"s_sgp{i}", [128, T], BF16) for i in range(2)]
        s_wxb = sb("s_wxb", [128, 2 * NST * NPT], BF16)
        s_wxf = sb("s_wxf", [128, 2 * NST * NPT])
        s_bxv = sb("s_bxv", [2 * NST, 1])
        s_bcout = sb("s_bcout", [2 * NST, T])

        p_tr = ps("p_tr", [128, 512], BF16)
        p_mm = [ps(f"p_mm{i}", [128, 512]) for i in range(2)]
        p_misc = ps("p_misc", [128, 512])
        p_bias = p_misc[:, 0:32]
        p_bc = [ps(f"p_bc{i}", [2 * NST, 512]) for i in range(4)]

        sems = make_sems(ctx, nc, ("ld", "wld", "v", "s", "t", "g", "p"))
        block = ctx.enter_context(nc.Block())
        P = Prog()
        TRUNC = int(os.environ.get("L1_TRUNC", "99"))

        NC_MASK = int(os.environ.get("L1_NC", "31"))
        if NC_MASK & 1:
            P.dma("sync", lambda e: e.dma_start(out=s_id[:], in_=identb), inc="ld")
            P.dma("sync", lambda e: e.dma_start(out=s_ones1[:], in_=ones1), inc="ld")
            P.dma("sync", lambda e: e.dma_start(out=s_hm[:], in_=hmask), inc="ld")
        if NC_MASK & 2:
            for k in range(8):
                P.dma("sync", lambda e, k=k: e.dma_start(
                    out=s_lnw[:, k:k + 1], in_=ln_w[k * 128:(k + 1) * 128, :]), inc="ld")
                P.dma("sync", lambda e, k=k: e.dma_start(
                    out=s_lnb[:, k:k + 1], in_=ln_b[k * 128:(k + 1) * 128, :]), inc="ld")
        if NC_MASK & 4:
            P.dma("sync", lambda e: e.dma_start(out=s_binrow[:], in_=b_in), inc="ld")
        if NC_MASK & 8:
            for pt in range(NPT):
                P.dma("sync", lambda e, pt=pt: e.dma_start(
                    out=s_cw[:, 4 * pt:4 * pt + 4],
                    in_=conv_w[pt * 128:(pt + 1) * 128, :]), inc="ld")
                P.dma("sync", lambda e, pt=pt: e.dma_start(
                    out=s_cb[:, pt:pt + 1],
                    in_=conv_b[pt * 128:(pt + 1) * 128, :]), inc="ld")
                P.dma("sync", lambda e, pt=pt: e.dma_start(
                    out=s_wxf[:, 32 * pt:32 * pt + 32],
                    in_=w_x[pt * 128:(pt + 1) * 128, :]), inc="ld")
        if NC_MASK & 16:
            P.dma("sync", lambda e: e.dma_start(out=s_bxv[:], in_=b_xv), inc="ld")
            P.dma("sync", lambda e: e.dma_start(out=s_xh[:], in_=xin[0:HALO, :]), inc="ld")
        ld_const = P.val("ld")

        if os.environ.get("L1_NOWXB"):
            v_wx = 0
        else:
            P.op("vector", lambda e: e.tensor_copy(out=s_wxb[:], in_=s_wxf[:]),
                 waits=[("ld", ld_const)], inc="v")
            v_wx = P.val("v")

        if TRUNC < 2:
            P.dma("gpsimd", lambda e: e.dma_start(out=o_b, in_=s_stat[0:16, 0:T] if False else s_sc[0:16, 0:1].broadcast_to([16, T])), inc="g") if False else None
            P.emit(block, sems)
            return nc
        v_lnbb = 0
        cvt_done = []
        for k in (range(0) if os.environ.get("L1_NOW") else range(8)):
            stage = s_wk[k % 2]
            w = [("ld", ld_const)]
            kgap = 1 if SINGLE_WK else 2
            if k >= kgap:
                w.append(("v", cvt_done[k - kgap]))
            P.dma("sync", lambda e, k=k, st_=stage: e.dma_start(
                out=st_[:], in_=w_in[k * 128:(k + 1) * 128, :]), waits=w, inc="wld")
            P.op("vector", lambda e, k=k, st_=stage: e.tensor_scalar(
                out=s_wb[k][:], in0=st_[:], scalar1=s_lnw[:, k:k + 1], scalar2=None,
                op0=ALU.mult),
                waits=[("wld", 16 * (k + 1))], inc="v")
            for j in (range(0) if os.environ.get("L1_NOBROW") else range(8)):
                wj = [("wld", 16 * (k + 1))]
                if k == 0 and j == 0:
                    pass
                P.op("tensor", lambda e, k=k, j=j, st_=stage: e.matmul(
                    out=p_misc[0:1, :], lhsT=s_lnb[:, k:k + 1],
                    rhs=st_[:, j * 512:(j + 1) * 512],
                    start=True, stop=True), waits=wj + [("v", P.val("v"))], inc="t")
                P.op("vector", lambda e, j=j: e.tensor_tensor(
                    out=s_binrow[0:1, j * 512:(j + 1) * 512],
                    in0=s_binrow[0:1, j * 512:(j + 1) * 512],
                    in1=p_misc[0:1, :], op=ALU.add),
                    waits=[("t", P.val("t")), ("ld", ld_const)], inc="v")
            cvt_done.append(P.val("v"))
        v_wcvt = P.val("v")
        v_brow = v_wcvt
        if os.environ.get("L1_NOW"):
            cvt_done = [0] * 8

        for m in (range(0) if os.environ.get("L1_NOBIAS") else range(32)):
            P.op("tensor", lambda e, m=m: e.matmul(
                out=p_bias[:, m:m + 1], lhsT=s_binrow[0:1, m * 128:(m + 1) * 128],
                rhs=s_ones1[0:1, 0:1], start=True, stop=True),
                waits=[("v", v_brow)], inc="t")
        t_bias = P.val("t")
        if os.environ.get("L1_NOBIAS"):
            s_bias_done = 0
        else:
            s_bias_done = P.op("scalar", lambda e: e.activation(
                out=s_bias[:], in_=p_bias[:], func=AF.Identity),
                waits=[("t", t_bias)], inc="s")

        if TRUNC < 3:
            P.emit(block, sems)
            return nc
        L1_HALO = int(os.environ.get("L1_HALO", "1"))
        L1_TILES = int(os.environ.get("L1_TILES", str(NTT)))
        tile_tr_done = {}
        tile_evac_done = {}
        n_tr = 0

        def stats_ops(tile_idx, xtile, prows, ld_wait):
            si = tile_idx
            c_sq = P.op("scalar", lambda e: e.activation(
                out=s_sq[0:prows, :], in_=xtile, func=AF.Square,
                accum_out=s_stat[0:prows, 2 * si:2 * si + 1]),
                waits=[ld_wait], inc="s")
            c_sm = P.op("scalar", lambda e: e.activation(
                out=s_sq[0:prows, :], in_=xtile, func=AF.Identity,
                accum_out=s_stat[0:prows, 2 * si + 1:2 * si + 2]), inc="s")
            P.op("vector", lambda e: e.tensor_scalar(
                out=s_mu[0:prows, si:si + 1],
                in0=s_stat[0:prows, 2 * si + 1:2 * si + 2],
                scalar1=1.0 / D, scalar2=None, op0=ALU.mult),
                waits=[("s", c_sm)], inc="v")
            P.op("vector", lambda e: e.tensor_scalar(
                out=s_nv[0:prows, si:si + 1],
                in0=s_stat[0:prows, 2 * si:2 * si + 1],
                scalar1=1.0 / D, scalar2=None, op0=ALU.mult), inc="v")
            P.op("vector", lambda e: e.scalar_tensor_tensor(
                out=s_nv[0:prows, si:si + 1], in0=s_mu[0:prows, si:si + 1],
                scalar=s_mu[0:prows, si:si + 1], in1=s_nv[0:prows, si:si + 1],
                op0=ALU.mult, op1=ALU.subtract), inc="v")
            v_nv = P.val("v")
            P.op("vector", lambda e: e.tensor_scalar(
                out=s_sc[0:prows, si:si + 1], in0=s_nv[0:prows, si:si + 1],
                scalar1=-(1.0 + EPS), scalar2=EPS * EPS,
                op0=ALU.mult, op1=ALU.add), inc="v")
            v_nv2 = P.val("v")
            c_rt = P.op("scalar", lambda e: e.activation(
                out=s_sc[0:prows, si:si + 1], in_=s_sc[0:prows, si:si + 1],
                func=AF.Sqrt), waits=[("v", v_nv2)], inc="s")
            P.op("vector", lambda e: e.reciprocal(
                out=s_sc[0:prows, si:si + 1], in_=s_sc[0:prows, si:si + 1]),
                waits=[("s", c_rt)], inc="v")
            P.op("vector", lambda e: e.scalar_tensor_tensor(
                out=s_bcs[0:prows, si:si + 1], in0=s_mu[0:prows, si:si + 1],
                scalar=-1.0, in1=s_sc[0:prows, si:si + 1],
                op0=ALU.mult, op1=ALU.mult), inc="v")
            return P.val("v")

        if not L1_HALO:
            v_h = 0
        v_h = stats_ops(NTT, s_xh[:], HALO, ("ld", ld_const)) if L1_HALO else 0
        if L1_HALO:
            c_xnh = P.op("scalar", lambda e: e.activation(
                out=s_xnh[:], in_=s_xh[:], func=AF.Identity,
                scale=s_sc[0:HALO, NTT:NTT + 1], bias=s_bcs[0:HALO, NTT:NTT + 1]),
                waits=[("v", v_h)], inc="s")
            for k in range(8):
                n_tr += 1
                P.op("tensor", lambda e, k=k: e.transpose(
                    out=p_tr[:, 0:HALO], in_=s_xnh[:, k * 128:(k + 1) * 128],
                    identity=s_id[0:HALO, 0:HALO]),
                    waits=[("s", c_xnh), ("s", P.val("s"))], inc="t")
                P.op("scalar", lambda e, k=k: e.activation(
                    out=s_xnT[k][:, 0:HALO], in_=p_tr[:, 0:HALO], func=AF.Identity),
                    waits=[("t", P.val("t"))], inc="s")

        for i in range(L1_TILES):
            xb = s_x[i % 2]
            w = [("s", 0)]
            if i >= 2:
                w = [("s", tile_evac_done[i - 2])]
            P.dma("sync", lambda e, i=i, xb=xb: e.dma_start(
                out=xb[:], in_=xin[HALO + i * 128:HALO + (i + 1) * 128, :]),
                waits=w, inc="ld")
            L1_PARTS = int(os.environ.get("L1_PARTS", "9"))
            if L1_PARTS < 1:
                continue
            v_i = stats_ops(i, xb[:], 128, ("ld", P.val("ld")))
            if L1_PARTS < 2:
                continue
            xnb = s_xn[i % 2]
            wx = [("v", v_i)]
            if i >= 2:
                wx.append(("t", tile_tr_done[i - 2]))
            XNMODE = int(os.environ.get("L1_XNMODE", "0"))
            if XNMODE == 1:
                c_xn = P.op("scalar", lambda e, i=i, xb=xb: e.activation(
                    out=s_sq[:], in_=xb[:], func=AF.Identity,
                    scale=s_sc[:, i:i + 1], bias=s_bcs[:, i:i + 1]),
                    waits=wx, inc="s")
            elif XNMODE == 2:
                c_xn = P.op("scalar", lambda e, i=i, xb=xb, xnb=xnb: e.activation(
                    out=xnb[:], in_=xb[:], func=AF.Identity),
                    waits=wx, inc="s")
            elif XNMODE == 3:
                P.op("vector", lambda e, i=i, xb=xb: e.scalar_tensor_tensor(
                    out=s_sq[:], in0=xb[:], scalar=s_sc[:, i:i + 1],
                    in1=s_bcs[:, i:i + 1].broadcast_to([128, D]),
                    op0=ALU.mult, op1=ALU.add), waits=wx)
                c_xn = P.op("vector", lambda e, xnb=xnb: e.tensor_copy(
                    out=xnb[:], in_=s_sq[:]))
            else:
                c_xn = P.op("scalar", lambda e, i=i, xb=xb, xnb=xnb: e.activation(
                    out=xnb[:], in_=xb[:], func=AF.Identity,
                    scale=s_sc[:, i:i + 1], bias=s_bcs[:, i:i + 1]),
                    waits=wx, inc="s")
            for k in range(8):
                sl = 0
                wt = [("s", c_xn)]
                if n_tr >= 1:
                    wt.append(("s", tile_evac_done.get(("tr", n_tr - 1), 0)))
                n_tr += 1
                ct = P.op("tensor", lambda e, k=k, xnb=xnb, sl=sl: e.transpose(
                    out=p_tr[:, sl:sl + 128], in_=xnb[:, k * 128:(k + 1) * 128],
                    identity=s_id[:]), waits=wt, inc="t")
                ce = P.op("scalar", lambda e, k=k, i=i, sl=sl: e.activation(
                    out=s_xnT[k][:, HALO + i * 128:HALO + (i + 1) * 128],
                    in_=p_tr[:, sl:sl + 128], func=AF.Identity),
                    waits=[("t", ct)], inc="s")
                tile_evac_done[("tr", n_tr - 1)] = ce
            tile_tr_done[i] = P.val("t")
            tile_evac_done[i] = P.val("s")

        v_phaseA = P.val("v")
        s_phaseA = P.val("s")
        t_phaseA = P.val("t")

        mm_group = 0
        evac_group_done = []
        mm_group_t = []

        def mm_tile(m, c0, c1c, dest_fn, act_func, bias_col, extra_waits=()):
            nonlocal mm_group
            g = mm_group
            mm_group += 1
            pb = p_mm[g % 2]
            n = c1c - c0
            for k in range(8):
                w = [("s", s_phaseA), ("v", v_wcvt)]
                if k == 0 and g >= 2:
                    w.append(("s", evac_group_done[g - 2]))
                P.op("tensor", lambda e, k=k, m=m, pb=pb, c0=c0, c1c=c1c, n=n: e.matmul(
                    out=pb[:, 0:n], lhsT=s_wb[k][:, m * 128:(m + 1) * 128],
                    rhs=s_xnT[k][:, c0:c1c],
                    start=(k == 0), stop=(k == 7)), waits=w, inc="t")
            tmm = P.val("t")
            mm_group_t.append(tmm)
            ce = P.op("scalar", lambda e, pb=pb, n=n: e.activation(
                out=dest_fn(), in_=pb[:, 0:n], func=act_func,
                bias=bias_col), waits=[("t", tmm), ("s", s_bias_done),
                                       *extra_waits], inc="s")
            evac_group_done.append(ce)
            return ce

        for pt in range(NPT):
            mm_tile(pt, 0, HALO,
                    (lambda pt=pt: s_xmh[:, pt * HALO:(pt + 1) * HALO]),
                    AF.Identity, s_bias[:, pt:pt + 1])
        vmaskall = P.op("vector", lambda e: e.tensor_scalar(
            out=s_xmh[:], in0=s_xmh[:], scalar1=s_hm[:, 0:1], scalar2=None,
            op0=ALU.mult),
            waits=[("s", P.val("s")), ("ld", ld_const)], inc="v")

        if TRUNC < 4:
            P.emit(block, sems)
            return nc
        sg_spill_done = {}
        for mi, m in enumerate(range(16, 32)):
            pt = m - 16
            buf = s_sgp[pt % 2]
            ew = []
            if pt >= 2:
                ew.append(("g", sg_spill_done[pt - 2]))
            last = None
            for tb in range(4):
                last = mm_tile(m, HALO + tb * 512, HALO + (tb + 1) * 512,
                               (lambda buf=buf, tb=tb: buf[:, tb * 512:(tb + 1) * 512]),
                               AF.Silu, s_bias[:, m:m + 1], ew)
            cg = P.dma("gpsimd", lambda e, pt=pt, buf=buf: e.dma_start(
                out=o_sg[pt * 128:(pt + 1) * 128, :], in_=buf[:]),
                waits=[("s", last)], inc="g")
            sg_spill_done[pt] = cg

        if TRUNC < 5:
            P.emit(block, sems)
            return nc
        xc_spill_done = {}
        v_conv_done = {}
        wx_done = {}
        for pt in range(NPT):
            m = pt
            last = None
            for tb in range(4):
                ew = []
                if pt >= 1:
                    ew.append(("v", v_conv_done[pt - 1]))
                last = mm_tile(m, HALO + tb * 512, HALO + (tb + 1) * 512,
                               (lambda tb=tb: s_xm[:, HALO + tb * 512:HALO + (tb + 1) * 512]),
                               AF.Identity, s_bias[:, m:m + 1], ew)
            cmask = P.op("vector", lambda e, pt=pt: e.tensor_copy(
                out=s_xm[:, 0:HALO], in_=s_xmh[:, pt * HALO:(pt + 1) * HALO]),
                waits=[("v", vmaskall)] + ([("v", v_conv_done[pt - 1])] if pt >= 1 else []),
                inc="v")
            cw = s_cw
            c1 = P.op("vector", lambda e, pt=pt: e.scalar_tensor_tensor(
                out=s_acc[:], in0=s_xm[:, HALO:HALO + T],
                scalar=cw[:, 4 * pt + 3:4 * pt + 4],
                in1=s_cb[:, pt:pt + 1].broadcast_to([128, T]),
                op0=ALU.mult, op1=ALU.add),
                waits=[("s", last), ("v", cmask)], inc="v")
            for kk in range(3):
                c1 = P.op("vector", lambda e, pt=pt, kk=kk: e.scalar_tensor_tensor(
                    out=s_acc[:], in0=s_xm[:, 5 + kk:5 + kk + T],
                    scalar=cw[:, 4 * pt + kk:4 * pt + kk + 1],
                    in1=s_acc[:], op0=ALU.mult, op1=ALU.add), inc="v")
            v_conv_done[pt] = c1
            ws = [("v", c1)]
            if pt >= 1:
                ws.append(("g", xc_spill_done[pt - 1]))
                ws.append(("t", wx_done[pt - 1]))
            csil = P.op("scalar", lambda e: e.activation(
                out=s_xcp[:], in_=s_acc[:], func=AF.Silu), waits=ws, inc="s")
            cspill = P.dma("gpsimd", lambda e, pt=pt: e.dma_start(
                out=o_xc[pt * 128:(pt + 1) * 128, :], in_=s_xcp[:]),
                waits=[("s", csil)], inc="g")
            xc_spill_done[pt] = cspill
            for tb in (range(0) if os.environ.get("L1_NOWX") else range(4)):
                P.op("tensor", lambda e, pt=pt, tb=tb: e.matmul(
                    out=p_bc[tb][:, :],
                    lhsT=s_wxb[:, 32 * pt:32 * pt + 32],
                    rhs=s_xcp[:, tb * 512:(tb + 1) * 512],
                    start=(pt == 0), stop=(pt == NPT - 1)),
                    waits=[("s", csil), ("v", v_wx)], inc="t")
            wx_done[pt] = P.val("t")

        t_all = P.val("t")
        ev = 0
        for tb in (range(0) if os.environ.get("L1_NOWX") else range(4)):
            ev = P.op("scalar", lambda e, tb=tb: e.activation(
                out=s_bcout[:, tb * 512:(tb + 1) * 512], in_=p_bc[tb][:, :],
                func=AF.Identity, bias=s_bxv[:, 0:1]),
                waits=[("t", t_all), ("ld", ld_const)], inc="s")
        if os.environ.get("L1_NOWX"):
            ev = P.val("s")
        P.dma("gpsimd", lambda e: e.dma_start(out=o_b, in_=s_bcout[0:16, :]),
              waits=[("s", ev)], inc="g")
        P.dma("gpsimd", lambda e: e.dma_start(out=o_c, in_=s_bcout[16:32, :]),
              inc="g")
        g_final = P.val("g")
        P.op("gpsimd", lambda e: e.engine_nop(), waits=[("g", g_final)])

        P.emit(block, sems)
    return nc




def build_l2():
    nc = bass.Bass("TRN2", target_bir_lowering=False, debug=False,
                   detect_race_conditions=False)
    TH2 = TFULL // 2

    xc = nc.dram_tensor("xc", [DI, TFULL], BF16, kind="ExternalInput").ap()
    bh = nc.dram_tensor("bh", [NH, TFULL], F32, kind="ExternalInput").ap()
    ch = nc.dram_tensor("ch", [NH, TFULL], F32, kind="ExternalInput").ap()
    alog = nc.dram_tensor("alog", [DI, NH], F32, kind="ExternalInput").ap()
    oh = nc.dram_tensor("oh", [NH, NH * 128], BF16, kind="ExternalInput").ap()

    o_y = nc.dram_tensor("o_y", [DI, TFULL], BF16, kind="ExternalOutput").ap()

    with contextlib.ExitStack() as ctx:
        sb = lambda n, s, d=F32: ctx.enter_context(nc.sbuf_tensor(n, s, d))
        ps = lambda n, s, d=F32: ctx.enter_context(nc.psum_tensor(n, s, d))

        s_bh = sb("s_bh", [NH, TFULL])
        s_ch = sb("s_ch", [NH, TFULL])
        s_bhb = sb("s_bhb", [NH, TFULL], BF16)
        s_chb = sb("s_chb", [NH, TFULL], BF16)
        s_oh = sb("s_oh", [NH, NH * 128], BF16)
        s_alog = sb("s_alog", [128, NPT * NH])
        s_A = sb("s_A", [128, NPT * NH])
        s_bb = sb("s_bb", [128, NH * TH2], BF16)
        s_cc = sb("s_cc", [128, NH * TH2], BF16)
        s_xcp = sb("s_xcp", [128, TH2], BF16)
        s_u = sb("s_u", [128, TH2], BF16)
        s_ha = sb("s_ha", [128, TH2])
        s_g = [sb(f"s_g{i}", [128, TH2], BF16) for i in range(2)]
        s_y = sb("s_y", [128, TH2])
        s_yb = sb("s_yb", [128, TH2], BF16)
        s_hc = sb("s_hc", [128, NPT * NH])

        p_dup = [ps(f"p_dup{i}", [128, 512]) for i in range(2)]

        sems = make_sems(ctx, nc, ("ld", "xld", "v", "s", "t", "g", "p"))
        block = ctx.enter_context(nc.Block())
        P = Prog()

        P.dma("sync", lambda e: e.dma_start(out=s_bh[:], in_=bh), inc="ld")
        P.dma("sync", lambda e: e.dma_start(out=s_ch[:], in_=ch), inc="ld")
        P.dma("sync", lambda e: e.dma_start(out=s_oh[:], in_=oh), inc="ld")
        for pt in range(NPT):
            P.dma("sync", lambda e, pt=pt: e.dma_start(
                out=s_alog[:, pt * NH:(pt + 1) * NH],
                in_=alog[pt * 128:(pt + 1) * 128, :]), inc="ld")
        ld0 = P.val("ld")
        P.op("scalar", lambda e: e.activation(
            out=s_A[:], in_=s_alog[:], func=AF.Exp), waits=[("ld", ld0)], inc="s")
        sA = P.op("scalar", lambda e: e.activation(
            out=s_A[:], in_=s_A[:], func=AF.Exp, scale=-1.0), inc="s")
        P.op("vector", lambda e: e.tensor_copy(out=s_bhb[:], in_=s_bh[:]),
             waits=[("ld", ld0)], inc="v")
        vBC = P.op("vector", lambda e: e.tensor_copy(out=s_chb[:], in_=s_ch[:]),
                   inc="v")

        dup_evac = {}
        gdup = 0
        pool_done = {}
        gctr = 0
        vy_prev = 0
        dma_prev = 0

        for half in range(2):
            t0 = half * TH2
            v_before_dups = P.val("v")
            for n in range(NH):
                for which, src, dst in (("b", s_bhb, s_bb), ("c", s_chb, s_cc)):
                    for bk in range(4):
                        pd = p_dup[gdup % 2]
                        w = [("v", vBC)]
                        if gdup >= 2:
                            w.append(("s", dup_evac[gdup - 2]))
                        P.op("tensor", lambda e, n=n, bk=bk, pd=pd, t0=t0, src=src: e.matmul(
                            out=pd[:], lhsT=s_oh[:, n * 128:(n + 1) * 128],
                            rhs=src[:, t0 + bk * 512:t0 + (bk + 1) * 512],
                            start=True, stop=True), waits=w, inc="t")
                        we = [("t", P.val("t"))]
                        if half == 1:
                            we.append(("v", v_before_dups))
                        c1 = P.op("scalar", lambda e, n=n, bk=bk, pd=pd, dst=dst: e.activation(
                            out=dst[:, n * TH2 + bk * 512:n * TH2 + (bk + 1) * 512],
                            in_=pd[:], func=AF.Identity), waits=we, inc="s")
                        dup_evac[gdup] = c1
                        gdup += 1
            dup_done = P.val("s")

            for pt in range(NPT):
                P.dma("sync", lambda e, pt=pt, t0=t0: e.dma_start(
                    out=s_xcp[:], in_=xc[pt * 128:(pt + 1) * 128, t0:t0 + TH2]),
                    waits=[("v", P.val("v"))], inc="xld")
                xld = P.val("xld")
                for n in range(NH):
                    ci = pt * NH + n
                    wu = [("xld", xld), ("s", dup_done)]
                    P.op("vector", lambda e, n=n: e.tensor_tensor(
                        out=s_u[:], in0=s_xcp[:],
                        in1=s_bb[:, n * TH2:(n + 1) * TH2], op=ALU.mult),
                        waits=wu, inc="v")
                    if half == 0:
                        init = 0.0
                    else:
                        init = s_hc[:, ci:ci + 1]
                    P.op("vector", lambda e, ci=ci, init=init: e.tensor_tensor_scan(
                        out=s_ha[:], data0=s_A[:, ci:ci + 1].broadcast_to([128, TH2]),
                        data1=s_u[:], initial=init,
                        op0=ALU.mult, op1=ALU.add),
                        waits=[("s", sA)], inc="v")
                    if half == 0:
                        P.op("vector", lambda e, ci=ci: e.tensor_copy(
                            out=s_hc[:, ci:ci + 1], in_=s_ha[:, TH2 - 1:TH2]), inc="v")
                    gb = s_g[gctr % 2]
                    wg = []
                    if gctr >= 2:
                        wg.append(("p", pool_done[gctr - 2]))
                    vg = P.op("vector", lambda e, n=n, gb=gb: e.tensor_tensor(
                        out=gb[:], in0=s_ha[:],
                        in1=s_cc[:, n * TH2:(n + 1) * TH2], op=ALU.mult),
                        waits=wg, inc="v")
                    wp = [("v", vg)]
                    if n == 0:
                        wp.append(("v", vy_prev))
                    if n == 0:
                        pc = P.op("gpsimd", lambda e, gb=gb: e.tensor_copy(
                            out=s_y[:], in_=gb[:]), waits=wp, inc="p")
                    else:
                        pc = P.op("gpsimd", lambda e, gb=gb: e.tensor_tensor(
                            out=s_y[:], in0=s_y[:], in1=gb[:], op=ALU.add),
                            waits=wp, inc="p")
                    pool_done[gctr] = pc
                    gctr += 1
                wy = [("p", pool_done[gctr - 1])]
                if dma_prev:
                    wy.append(("g", dma_prev))
                vy_prev = P.op("vector", lambda e: e.tensor_copy(
                    out=s_yb[:], in_=s_y[:]), waits=wy, inc="v")
                dma_prev = P.dma("gpsimd", lambda e, pt=pt, t0=t0: e.dma_start(
                    out=o_y[pt * 128:(pt + 1) * 128, t0:t0 + TH2], in_=s_yb[:]),
                    waits=[("v", vy_prev)], inc="g")

        g_final = P.val("g")
        P.op("gpsimd", lambda e: e.engine_nop(), waits=[("g", g_final)])
        P.emit(block, sems)
    return nc


def build_l3():
    nc = bass.Bass("TRN2", target_bir_lowering=False, debug=False,
                   detect_race_conditions=False)

    xin = nc.dram_tensor("xin", [T, D], F32, kind="ExternalInput").ap()
    p0 = nc.dram_tensor("p0", [DI, T], BF16, kind="ExternalInput").ap()
    p1 = nc.dram_tensor("p1", [DI, T], BF16, kind="ExternalInput").ap()
    xc = nc.dram_tensor("xc", [DI, T], BF16, kind="ExternalInput").ap()
    sg = nc.dram_tensor("sg", [DI, T], BF16, kind="ExternalInput").ap()
    dvec = nc.dram_tensor("dvec", [DI, 1], F32, kind="ExternalInput").ap()
    w_out = nc.dram_tensor("w_out", [DI, D], F32, kind="ExternalInput").ap()
    b_out = nc.dram_tensor("b_out", [1, D], F32, kind="ExternalInput").ap()
    g_row = nc.dram_tensor("g_row", [1, D], F32, kind="ExternalInput").ap()
    be_row = nc.dram_tensor("be_row", [1, D], F32, kind="ExternalInput").ap()
    ones1 = nc.dram_tensor("ones1", [1, 128], F32, kind="ExternalInput").ap()
    identf = nc.dram_tensor("identf", [128, 128], F32, kind="ExternalInput").ap()

    o_x1 = nc.dram_tensor("o_x1", [T, D], F32, kind="ExternalOutput").ap()
    o_lm = nc.dram_tensor("o_lm", [1, 1], F32, kind="ExternalOutput").ap()

    with contextlib.ExitStack() as ctx:
        sb = lambda n, s, d=F32: ctx.enter_context(nc.sbuf_tensor(n, s, d))
        ps = lambda n, s, d=F32: ctx.enter_context(nc.psum_tensor(n, s, d))

        s_p0 = sb("s_p0", [128, T], BF16)
        s_p1 = sb("s_p1", [128, T], BF16)
        s_xcp = sb("s_xcp", [128, T], BF16)
        s_sgp = sb("s_sgp", [128, T], BF16)
        s_d = sb("s_d", [128, NPT])
        s_y = sb("s_y", [128, T])
        s_yg = [sb(f"s_yg{i}", [128, T], BF16) for i in range(NPT)]
        s_wo = [sb(f"s_wo{i}", [128, D]) for i in range(2)]
        s_wob = [sb(f"s_wob{i}", [128, D], BF16) for i in range(NPT)]
        s_borow = sb("s_borow", [1, D])
        s_grow = sb("s_grow", [1, D])
        s_berow = sb("s_berow", [1, D])
        s_bo = sb("s_bo", [128, D])
        s_ga = sb("s_ga", [128, D])
        s_be = sb("s_be", [128, D])
        s_ones1 = sb("s_ones1", [1, 128])
        s_idf = sb("s_idf", [128, 128])
        s_x = [sb(f"s_x{i}", [128, D]) for i in range(2)]
        s_sq = sb("s_sq", [128, D])
        s_ln1 = sb("s_ln1", [128, D])
        s_x1 = [sb(f"s_x1{i}", [128, D]) for i in range(2)]
        s_z = sb("s_z", [128, D])
        s_stat = sb("s_stat", [128, 4 * NTT])
        s_mu = sb("s_mu", [128, 2 * NTT])
        s_nv = sb("s_nv", [128, 2 * NTT])
        s_sc = sb("s_sc", [128, 2 * NTT])
        s_bcs = sb("s_bcs", [128, 2 * NTT])
        s_mx = sb("s_mx", [128, 2])
        s_mrow = sb("s_mrow", [1, 128])
        s_lm = sb("s_lm", [1, 1])

        p_mm = [ps(f"p_mm{i}", [128, D]) for i in range(2)]
        p_b = ps("p_b", [128, D])
        p_r = ps("p_r", [1, 128])

        sems = make_sems(ctx, nc, ("ld", "wld", "xt", "v", "s", "t", "g", "p"))
        block = ctx.enter_context(nc.Block())
        P = Prog()

        P.dma("sync", lambda e: e.dma_start(out=s_borow[:], in_=b_out), inc="ld")
        P.dma("sync", lambda e: e.dma_start(out=s_grow[:], in_=g_row), inc="ld")
        P.dma("sync", lambda e: e.dma_start(out=s_berow[:], in_=be_row), inc="ld")
        P.dma("sync", lambda e: e.dma_start(out=s_ones1[:], in_=ones1), inc="ld")
        P.dma("sync", lambda e: e.dma_start(out=s_idf[:], in_=identf), inc="ld")
        for pt in range(NPT):
            P.dma("sync", lambda e, pt=pt: e.dma_start(
                out=s_d[:, pt:pt + 1], in_=dvec[pt * 128:(pt + 1) * 128, :]), inc="ld")
        ld0 = P.val("ld")

        prev_ev = 0
        for (row, dst) in ((s_borow, s_bo), (s_grow, s_ga), (s_berow, s_be)):
            for bk in range(2):
                P.op("tensor", lambda e, row=row, bk=bk: e.matmul(
                    out=p_b[:, bk * 512:(bk + 1) * 512], lhsT=s_ones1[0:1, :],
                    rhs=row[0:1, bk * 512:(bk + 1) * 512], start=True, stop=True),
                    waits=[("ld", ld0), ("s", prev_ev)], inc="t")
            prev_ev = P.op("scalar", lambda e, dst=dst: e.activation(
                out=dst[:], in_=p_b[:], func=AF.Identity),
                waits=[("t", P.val("t"))], inc="s")
        s_bcast = P.val("s")

        cvt = []
        for k in range(NPT):
            stg = s_wo[k % 2]
            w = []
            if k >= 2:
                w.append(("v", cvt[k - 2]))
            P.dma("sync", lambda e, k=k, stg=stg: e.dma_start(
                out=stg[:], in_=w_out[k * 128:(k + 1) * 128, :]), waits=w, inc="wld")
            c = P.op("vector", lambda e, k=k, stg=stg: e.tensor_copy(
                out=s_wob[k][:], in_=stg[:]), waits=[("wld", 16 * (k + 1))], inc="v")
            cvt.append(c)
        v_wcvt = P.val("v")

        yg_done = []
        for pt in range(NPT):
            w = []
            if pt >= 1:
                w.append(("v", yg_done[pt - 1]))
            P.dma("sync", lambda e, pt=pt: e.dma_start(
                out=s_p0[:], in_=p0[pt * 128:(pt + 1) * 128, :]), waits=w, inc="ld")
            P.dma("sync", lambda e, pt=pt: e.dma_start(
                out=s_p1[:], in_=p1[pt * 128:(pt + 1) * 128, :]), inc="ld")
            P.dma("sync", lambda e, pt=pt: e.dma_start(
                out=s_xcp[:], in_=xc[pt * 128:(pt + 1) * 128, :]), inc="ld")
            P.dma("sync", lambda e, pt=pt: e.dma_start(
                out=s_sgp[:], in_=sg[pt * 128:(pt + 1) * 128, :]), inc="ld")
            ldp = P.val("ld")
            P.op("vector", lambda e: e.tensor_tensor(
                out=s_y[:], in0=s_p0[:], in1=s_p1[:], op=ALU.add),
                waits=[("ld", ldp)], inc="v")
            P.op("vector", lambda e, pt=pt: e.scalar_tensor_tensor(
                out=s_y[:], in0=s_xcp[:], scalar=s_d[:, pt:pt + 1],
                in1=s_y[:], op0=ALU.mult, op1=ALU.add), inc="v")
            c = P.op("vector", lambda e, pt=pt: e.tensor_tensor(
                out=s_yg[pt][:], in0=s_y[:], in1=s_sgp[:], op=ALU.mult), inc="v")
            yg_done.append(c)
        v_yg = P.val("v")

        def stats2(xtile, sl, base_wait):
            c_sq = P.op("scalar", lambda e: e.activation(
                out=s_sq[:], in_=xtile, func=AF.Square,
                accum_out=s_stat[:, 2 * sl:2 * sl + 1]), waits=[base_wait], inc="s")
            c_sm = P.op("scalar", lambda e: e.activation(
                out=s_sq[:], in_=xtile, func=AF.Identity,
                accum_out=s_stat[:, 2 * sl + 1:2 * sl + 2]), inc="s")
            P.op("vector", lambda e: e.tensor_scalar(
                out=s_mu[:, sl:sl + 1], in0=s_stat[:, 2 * sl + 1:2 * sl + 2],
                scalar1=1.0 / D, scalar2=None, op0=ALU.mult),
                waits=[("s", c_sm)], inc="v")
            P.op("vector", lambda e: e.tensor_scalar(
                out=s_nv[:, sl:sl + 1], in0=s_stat[:, 2 * sl:2 * sl + 1],
                scalar1=1.0 / D, scalar2=None, op0=ALU.mult), inc="v")
            P.op("vector", lambda e: e.scalar_tensor_tensor(
                out=s_nv[:, sl:sl + 1], in0=s_mu[:, sl:sl + 1],
                scalar=s_mu[:, sl:sl + 1], in1=s_nv[:, sl:sl + 1],
                op0=ALU.mult, op1=ALU.subtract), inc="v")
            return P.val("v")

        def finish_scale(sl, combined):
            scale = -(1.0 + EPS) if combined else -1.0
            bias = EPS * EPS if combined else EPS
            P.op("vector", lambda e, sl=sl, scale=scale, bias=bias: e.tensor_scalar(
                out=s_sc[:, sl:sl + 1], in0=s_nv[:, sl:sl + 1],
                scalar1=scale, scalar2=bias, op0=ALU.mult, op1=ALU.add), inc="v")
            c_rt = P.op("scalar", lambda e, sl=sl: e.activation(
                out=s_sc[:, sl:sl + 1], in_=s_sc[:, sl:sl + 1],
                func=AF.Sqrt), waits=[("v", P.val("v"))], inc="s")
            P.op("vector", lambda e, sl=sl: e.reciprocal(
                out=s_sc[:, sl:sl + 1], in_=s_sc[:, sl:sl + 1]),
                waits=[("s", c_rt)], inc="v")
            P.op("vector", lambda e, sl=sl: e.scalar_tensor_tensor(
                out=s_bcs[:, sl:sl + 1], in0=s_mu[:, sl:sl + 1],
                scalar=-1.0, in1=s_sc[:, sl:sl + 1],
                op0=ALU.mult, op1=ALU.mult), inc="v")
            return P.val("v")

        ev_done = {}
        x_free = {}
        x1_free = {}
        mxv = 0
        for i in range(NTT):
            pb = p_mm[i % 2]
            for k in range(NPT):
                w = [("v", v_yg)]
                if k == 0 and i >= 2:
                    w.append(("v", ev_done[i - 2]))
                for bk in range(2):
                    P.op("tensor", lambda e, k=k, i=i, bk=bk, pb=pb: e.matmul(
                        out=pb[:, bk * 512:(bk + 1) * 512],
                        lhsT=s_yg[k][:, i * 128:(i + 1) * 128],
                        rhs=s_wob[k][:, bk * 512:(bk + 1) * 512],
                        start=(k == 0), stop=(k == NPT - 1)), waits=w, inc="t")
            t_mm = P.val("t")
            xb = s_x[i % 2]
            w = []
            if i >= 2:
                w.append(("v", x_free[i - 2]))
            P.dma("sync", lambda e, i=i, xb=xb: e.dma_start(
                out=xb[:], in_=xin[i * 128:(i + 1) * 128, :]), waits=w, inc="ld")
            ldx = P.val("ld")
            sl1 = 2 * i
            stats2(xb[:], sl1, ("ld", ldx))
            finish_scale(sl1, False)
            c_ln1 = P.op("scalar", lambda e, xb=xb, sl1=sl1: e.activation(
                out=s_ln1[:], in_=xb[:], func=AF.Identity,
                scale=s_sc[:, sl1:sl1 + 1], bias=s_bcs[:, sl1:sl1 + 1]),
                waits=[("v", P.val("v"))], inc="s")
            x1b = s_x1[i % 2]
            w = [("s", c_ln1)]
            if i >= 2:
                w.append(("v", x1_free[i - 2]))
                w.append(("g", 16 * (i - 1)))
            P.op("vector", lambda e, xb=xb: e.tensor_tensor(
                out=s_ln1[:], in0=s_ln1[:], in1=xb[:], op=ALU.add), waits=w, inc="v")
            P.op("vector", lambda e, pb=pb, x1b=x1b: e.tensor_tensor(
                out=x1b[:], in0=s_ln1[:], in1=pb[:], op=ALU.add),
                waits=[("t", t_mm)], inc="v")
            c_x1 = P.op("vector", lambda e, x1b=x1b: e.tensor_tensor(
                out=x1b[:], in0=x1b[:], in1=s_bo[:], op=ALU.add),
                waits=[("s", s_bcast)], inc="v")
            ev_done[i] = c_x1
            x_free[i] = c_x1
            P.dma("gpsimd", lambda e, i=i, x1b=x1b: e.dma_start(
                out=o_x1[i * 128:(i + 1) * 128, :], in_=x1b[:]),
                waits=[("v", c_x1)], inc="g")
            sl2 = 2 * i + 1
            stats2(x1b[:], sl2, ("v", c_x1))
            finish_scale(sl2, True)
            c_z = P.op("scalar", lambda e, x1b=x1b, sl2=sl2: e.activation(
                out=s_z[:], in_=x1b[:], func=AF.Identity,
                scale=s_sc[:, sl2:sl2 + 1], bias=s_bcs[:, sl2:sl2 + 1]),
                waits=[("v", P.val("v"))], inc="s")
            x1_free[i] = P.val("v") + 2
            P.op("vector", lambda e: e.tensor_tensor(
                out=s_z[:], in0=s_z[:], in1=s_ga[:], op=ALU.mult),
                waits=[("s", c_z)], inc="v")
            P.op("vector", lambda e: e.tensor_tensor(
                out=s_z[:], in0=s_z[:], in1=s_be[:], op=ALU.add), inc="v")
            P.op("vector", lambda e: e.tensor_reduce(
                out=s_mx[:, 1:2], in_=s_z[:], axis=AX.X, op=ALU.max,
                apply_absolute_value=True), inc="v")
            if i == 0:
                mxv = P.op("vector", lambda e: e.tensor_copy(
                    out=s_mx[:, 0:1], in_=s_mx[:, 1:2]), inc="v")
            else:
                mxv = P.op("vector", lambda e: e.tensor_tensor(
                    out=s_mx[:, 0:1], in0=s_mx[:, 0:1], in1=s_mx[:, 1:2],
                    op=ALU.max), inc="v")
            x1_free[i] = mxv

        P.op("tensor", lambda e: e.matmul(
            out=p_r[:], lhsT=s_mx[:, 0:1], rhs=s_idf[:], start=True, stop=True),
            waits=[("v", mxv)], inc="t")
        P.op("scalar", lambda e: e.activation(
            out=s_mrow[:], in_=p_r[:], func=AF.Identity),
            waits=[("t", P.val("t"))], inc="s")
        vlm = P.op("vector", lambda e: e.tensor_reduce(
            out=s_lm[:], in_=s_mrow[:], axis=AX.X, op=ALU.max),
            waits=[("s", P.val("s"))], inc="v")
        P.dma("gpsimd", lambda e: e.dma_start(out=o_lm, in_=s_lm[:]),
              waits=[("v", vlm)], inc="g")
        g_final = P.val("g")
        P.op("gpsimd", lambda e: e.engine_nop(), waits=[("g", g_final)])
        P.emit(block, sems)
    return nc


def build_l4():
    nc = bass.Bass("TRN2", target_bir_lowering=False, debug=False,
                   detect_race_conditions=False)

    x1 = nc.dram_tensor("x1", [T, D], F32, kind="ExternalInput").ap()
    wff = nc.dram_tensor("wff", [D, D], F32, kind="ExternalInput").ap()
    lmax8 = nc.dram_tensor("lmax8", [1, NCORES], F32, kind="ExternalInput").ap()
    g_row = nc.dram_tensor("g_row", [1, D], F32, kind="ExternalInput").ap()
    be_row = nc.dram_tensor("be_row", [1, D], F32, kind="ExternalInput").ap()
    ones1 = nc.dram_tensor("ones1", [1, 128], F32, kind="ExternalInput").ap()
    identf = nc.dram_tensor("identf", [128, 128], F32, kind="ExternalInput").ap()
    identb = nc.dram_tensor("identb", [128, 128], BF16, kind="ExternalInput").ap()

    o_x2 = nc.dram_tensor("o_x2", [T, D], F32, kind="ExternalOutput").ap()

    NKB = D // 128
    with contextlib.ExitStack() as ctx:
        sb = lambda n, s, d=F32: ctx.enter_context(nc.sbuf_tensor(n, s, d))
        ps = lambda n, s, d=F32: ctx.enter_context(nc.psum_tensor(n, s, d))

        s_w = [sb(f"s_w{i}", [128, D]) for i in range(2)]
        s_absw = [sb(f"s_absw{i}", [128, D]) for i in range(NKB)]
        s_sgnb = [sb(f"s_sgnb{i}", [128, D], BF16) for i in range(NKB)]
        s_scr = sb("s_scr", [128, D])
        s_mask = sb("s_mask", [128, D])
        s_m0 = sb("s_m0", [128, D], BF16)
        s_mT = [sb(f"s_mT{i}", [128, D], BF16) for i in range(NKB)]
        s_cols = sb("s_cols", [128, 3 * NKB])
        s_red = sb("s_red", [128, 4])
        s_row = sb("s_row", [1, 128])
        s_sca = sb("s_sca", [1, 12])
        s_dcol = sb("s_dcol", [128, 1])
        s_qcol = sb("s_qcol", [128, 1])
        s_acol = sb("s_acol", [128, 1])
        s_lmax = sb("s_lmax", [1, NCORES])
        s_grow = sb("s_grow", [1, D])
        s_berow = sb("s_berow", [1, D])
        s_ga = sb("s_ga", [128, D])
        s_be = sb("s_be", [128, D])
        s_ones1 = sb("s_ones1", [1, 128])
        s_idf = sb("s_idf", [128, 128])
        s_idb = sb("s_idb", [128, 128], BF16)
        s_x1 = [sb(f"s_x1{i}", [128, D]) for i in range(2)]
        s_sq = sb("s_sq", [128, D])
        s_stat = sb("s_stat", [128, 2 * NTT])
        s_mu = sb("s_mu", [128, NTT])
        s_nv = sb("s_nv", [128, NTT])
        s_sc = sb("s_sc", [128, NTT])
        s_bcs = sb("s_bcs", [128, NTT])
        s_z = sb("s_z", [128, D])
        s_a0 = sb("s_a0", [128, D])
        s_sgn = sb("s_sgn", [128, D])
        s_b = sb("s_b", [128, D])
        s_qtm = sb("s_qtm", [128, D], BF16)
        s_qfm = [sb(f"s_qfm{i}", [128, T], BF16) for i in range(NKB)]
        s_t2 = sb("s_t2", [128, D])

        p_b = ps("p_b", [128, D])
        p_c1 = p_b[:, 0:1]
        p_tr = ps("p_tr", [128, 512], BF16)
        p_mm = [ps(f"p_mm{i}", [128, D]) for i in range(2)]
        p_r = p_b[0:1, 0:128]

        sems = make_sems(ctx, nc, ("ld", "wld", "v", "s", "t", "g", "p"))
        block = ctx.enter_context(nc.Block())
        P = Prog()

        P.dma("sync", lambda e: e.dma_start(out=s_lmax[:], in_=lmax8), inc="ld")
        P.dma("sync", lambda e: e.dma_start(out=s_grow[:], in_=g_row), inc="ld")
        P.dma("sync", lambda e: e.dma_start(out=s_berow[:], in_=be_row), inc="ld")
        P.dma("sync", lambda e: e.dma_start(out=s_ones1[:], in_=ones1), inc="ld")
        P.dma("sync", lambda e: e.dma_start(out=s_idf[:], in_=identf), inc="ld")
        P.dma("sync", lambda e: e.dma_start(out=s_idb[:], in_=identb), inc="ld")
        ld0 = P.val("ld")

        prev_ev = 0
        for (row, dst) in ((s_grow, s_ga), (s_berow, s_be)):
            for bk in range(2):
                P.op("tensor", lambda e, row=row, bk=bk: e.matmul(
                    out=p_b[:, bk * 512:(bk + 1) * 512], lhsT=s_ones1[0:1, :],
                    rhs=row[0:1, bk * 512:(bk + 1) * 512], start=True, stop=True),
                    waits=[("ld", ld0), ("s", prev_ev)], inc="t")
            prev_ev = P.op("scalar", lambda e, dst=dst: e.activation(
                out=dst[:], in_=p_b[:], func=AF.Identity),
                waits=[("t", P.val("t"))], inc="s")
        s_bcast = P.val("s")

        sgn_done = {}
        for t in range(NKB):
            stg = s_w[t % 2]
            w = []
            if t >= 2:
                w.append(("s", sgn_done[t - 2]))
            P.dma("sync", lambda e, t=t, stg=stg: e.dma_start(
                out=stg[:], in_=wff[t * 128:(t + 1) * 128, :]), waits=w, inc="wld")
            P.op("scalar", lambda e, t=t, stg=stg: e.activation(
                out=s_absw[t][:], in_=stg[:], func=AF.Abs,
                accum_out=s_cols[:, t:t + 1]),
                waits=[("wld", 16 * (t + 1))], inc="s")
            c = P.op("scalar", lambda e, t=t, stg=stg: e.activation(
                out=s_sgnb[t][:], in_=stg[:], func=AF.Sign), inc="s")
            sgn_done[t] = c
        s_wabs = P.val("s")

        def col_to_scalar(col_ap, dst_slot, op):
            ct = P.op("tensor", lambda e: e.matmul(
                out=p_r, lhsT=col_ap, rhs=s_idf[:], start=True, stop=True),
                waits=[("v", P.val("v")), ("s", P.val("s"))], inc="t")
            cs = P.op("scalar", lambda e: e.activation(
                out=s_row[:], in_=p_r, func=AF.Identity),
                waits=[("t", ct)], inc="s")
            return P.op("vector", lambda e: e.tensor_reduce(
                out=s_sca[0:1, dst_slot:dst_slot + 1], in_=s_row[:],
                axis=AX.X, op=op), waits=[("s", cs)], inc="v")

        def bcast_col(src_slot, dst):
            ct = P.op("tensor", lambda e: e.matmul(
                out=p_c1, lhsT=s_ones1[0:1, :],
                rhs=s_sca[0:1, src_slot:src_slot + 1], start=True, stop=True),
                waits=[("v", P.val("v")), ("s", P.val("s"))], inc="t")
            return P.op("scalar", lambda e, dst=dst: e.activation(
                out=dst[:], in_=p_c1[:], func=AF.Identity),
                waits=[("t", ct)], inc="s")

        P.op("vector", lambda e: e.tensor_reduce(
            out=s_red[:, 0:1], in_=s_cols[:, 0:NKB], axis=AX.X, op=ALU.add),
            waits=[("s", s_wabs)], inc="v")
        col_to_scalar(s_red[:, 0:1], 0, ALU.add)
        P.op("vector", lambda e: e.tensor_scalar(
            out=s_sca[0:1, 1:2], in0=s_sca[0:1, 0:1],
            scalar1=0.7 / (D * D), scalar2=None, op0=ALU.mult), inc="v")
        c_dcol = bcast_col(1, s_dcol)

        tr_ev = {}
        ntr = 0
        t_tr_prev = 0
        for t in range(NKB):
            vm = P.op("vector", lambda e, t=t: e.tensor_scalar(
                out=s_mask[:], in0=s_absw[t][:], scalar1=s_dcol[:, 0:1],
                scalar2=None, op0=ALU.is_gt),
                waits=[("s", c_dcol), ("t", t_tr_prev)], inc="v")
            P.op("vector", lambda e, t=t: e.tensor_tensor(
                out=s_scr[:], in0=s_absw[t][:], in1=s_mask[:], op=ALU.mult), inc="v")
            P.op("vector", lambda e, t=t: e.tensor_scalar(
                out=s_scr[:], in0=s_scr[:], scalar1=0.0, scalar2=0.0,
                op0=ALU.add, op1=ALU.add,
                accum_out=s_cols[:, NKB + t:NKB + t + 1]), inc="v")
            P.op("vector", lambda e, t=t: e.tensor_scalar(
                out=s_scr[:], in0=s_mask[:], scalar1=0.0, scalar2=0.0,
                op0=ALU.add, op1=ALU.add,
                accum_out=s_cols[:, 2 * NKB + t:2 * NKB + t + 1]), inc="v")
            vz = P.op("vector", lambda e, t=t: e.tensor_tensor(
                out=s_m0[:], in0=s_sgnb[t][:], in1=s_mask[:], op=ALU.mult), inc="v")
            for k in range(NKB):
                sl = 0
                wt = [("v", vz)]
                if ntr >= 1:
                    wt.append(("s", tr_ev[ntr - 1]))
                ct = P.op("tensor", lambda e, k=k, sl=sl: e.transpose(
                    out=p_tr[:, sl:sl + 128], in_=s_m0[:, k * 128:(k + 1) * 128],
                    identity=s_idb[:]), waits=wt, inc="t")
                ce = P.op("scalar", lambda e, k=k, t=t, sl=sl: e.activation(
                    out=s_mT[k][:, t * 128:(t + 1) * 128], in_=p_tr[:, sl:sl + 128],
                    func=AF.Identity), waits=[("t", ct)], inc="s")
                tr_ev[ntr] = ce
                ntr += 1
            t_tr_prev = P.val("t")
        s_mT_done = P.val("s")

        P.op("vector", lambda e: e.tensor_reduce(
            out=s_red[:, 1:2], in_=s_cols[:, NKB:2 * NKB], axis=AX.X, op=ALU.add),
            waits=[("v", P.val("v"))], inc="v")
        P.op("vector", lambda e: e.tensor_reduce(
            out=s_red[:, 2:3], in_=s_cols[:, 2 * NKB:3 * NKB], axis=AX.X,
            op=ALU.add), inc="v")
        col_to_scalar(s_red[:, 1:2], 2, ALU.add)
        col_to_scalar(s_red[:, 2:3], 3, ALU.add)
        P.op("vector", lambda e: e.tensor_scalar(
            out=s_sca[0:1, 3:4], in0=s_sca[0:1, 3:4], scalar1=1.0,
            scalar2=None, op0=ALU.max), inc="v")
        P.op("vector", lambda e: e.reciprocal(
            out=s_sca[0:1, 4:5], in_=s_sca[0:1, 3:4]), inc="v")
        P.op("vector", lambda e: e.tensor_tensor(
            out=s_sca[0:1, 5:6], in0=s_sca[0:1, 2:3], in1=s_sca[0:1, 4:5],
            op=ALU.mult), inc="v")
        P.op("vector", lambda e: e.tensor_reduce(
            out=s_sca[0:1, 6:7], in_=s_lmax[:], axis=AX.X, op=ALU.max),
            waits=[("ld", ld0)], inc="v")
        P.op("vector", lambda e: e.tensor_scalar(
            out=s_sca[0:1, 6:7], in0=s_sca[0:1, 6:7], scalar1=1e-8,
            scalar2=None, op0=ALU.max), inc="v")
        P.op("vector", lambda e: e.reciprocal(
            out=s_sca[0:1, 7:8], in_=s_sca[0:1, 6:7]), inc="v")
        P.op("vector", lambda e: e.tensor_scalar(
            out=s_sca[0:1, 8:9], in0=s_sca[0:1, 7:8], scalar1=127.0,
            scalar2=None, op0=ALU.mult), inc="v")
        P.op("vector", lambda e: e.tensor_tensor(
            out=s_sca[0:1, 9:10], in0=s_sca[0:1, 5:6], in1=s_sca[0:1, 6:7],
            op=ALU.mult), inc="v")
        P.op("vector", lambda e: e.tensor_scalar(
            out=s_sca[0:1, 9:10], in0=s_sca[0:1, 9:10], scalar1=1.0 / 127.0,
            scalar2=None, op0=ALU.mult), inc="v")
        c_qcol = bcast_col(8, s_qcol)
        c_acol = bcast_col(9, s_acol)

        def stats2(xtile, sl, base_wait):
            c_sq = P.op("scalar", lambda e: e.activation(
                out=s_sq[:], in_=xtile, func=AF.Square,
                accum_out=s_stat[:, 2 * sl:2 * sl + 1]), waits=[base_wait], inc="s")
            c_sm = P.op("scalar", lambda e: e.activation(
                out=s_sq[:], in_=xtile, func=AF.Identity,
                accum_out=s_stat[:, 2 * sl + 1:2 * sl + 2]), inc="s")
            P.op("vector", lambda e: e.tensor_scalar(
                out=s_mu[:, sl:sl + 1], in0=s_stat[:, 2 * sl + 1:2 * sl + 2],
                scalar1=1.0 / D, scalar2=None, op0=ALU.mult),
                waits=[("s", c_sm)], inc="v")
            P.op("vector", lambda e: e.tensor_scalar(
                out=s_nv[:, sl:sl + 1], in0=s_stat[:, 2 * sl:2 * sl + 1],
                scalar1=1.0 / D, scalar2=None, op0=ALU.mult), inc="v")
            P.op("vector", lambda e: e.scalar_tensor_tensor(
                out=s_nv[:, sl:sl + 1], in0=s_mu[:, sl:sl + 1],
                scalar=s_mu[:, sl:sl + 1], in1=s_nv[:, sl:sl + 1],
                op0=ALU.mult, op1=ALU.subtract), inc="v")
            P.op("vector", lambda e, sl=sl: e.tensor_scalar(
                out=s_sc[:, sl:sl + 1], in0=s_nv[:, sl:sl + 1],
                scalar1=-(1.0 + EPS), scalar2=EPS * EPS,
                op0=ALU.mult, op1=ALU.add), inc="v")
            c_rt = P.op("scalar", lambda e, sl=sl: e.activation(
                out=s_sc[:, sl:sl + 1], in_=s_sc[:, sl:sl + 1],
                func=AF.Sqrt), waits=[("v", P.val("v"))], inc="s")
            P.op("vector", lambda e, sl=sl: e.reciprocal(
                out=s_sc[:, sl:sl + 1], in_=s_sc[:, sl:sl + 1]),
                waits=[("s", c_rt)], inc="v")
            P.op("vector", lambda e, sl=sl: e.scalar_tensor_tensor(
                out=s_bcs[:, sl:sl + 1], in0=s_mu[:, sl:sl + 1],
                scalar=-1.0, in1=s_sc[:, sl:sl + 1],
                op0=ALU.mult, op1=ALU.mult), inc="v")
            return P.val("v")

        x1_free = {}
        mm_free = {}
        qtr = 0
        qtr_ev = {}
        q_tr_prev = 0
        for i in range(NTT):
            xb = s_x1[i % 2]
            w = []
            if i >= 2:
                w.append(("v", x1_free[i - 2]))
            P.dma("sync", lambda e, i=i, xb=xb: e.dma_start(
                out=xb[:], in_=x1[i * 128:(i + 1) * 128, :]), waits=w, inc="ld")
            v_st = stats2(xb[:], i, ("ld", P.val("ld")))
            c_z = P.op("scalar", lambda e, xb=xb, i=i: e.activation(
                out=s_z[:], in_=xb[:], func=AF.Identity,
                scale=s_sc[:, i:i + 1], bias=s_bcs[:, i:i + 1]),
                waits=[("v", v_st)], inc="s")
            P.op("vector", lambda e: e.tensor_tensor(
                out=s_z[:], in0=s_z[:], in1=s_ga[:], op=ALU.mult),
                waits=[("s", c_z), ("s", s_bcast)], inc="v")
            P.op("vector", lambda e: e.tensor_tensor(
                out=s_z[:], in0=s_z[:], in1=s_be[:], op=ALU.add), inc="v")
            P.op("vector", lambda e: e.tensor_scalar(
                out=s_z[:], in0=s_z[:], scalar1=s_qcol[:, 0:1], scalar2=None,
                op0=ALU.mult), waits=[("s", c_qcol)], inc="v")
            P.op("vector", lambda e: e.tensor_scalar(
                out=s_z[:], in0=s_z[:], scalar1=127.0, scalar2=None,
                op0=ALU.min), inc="v")
            vz = P.op("vector", lambda e: e.tensor_scalar(
                out=s_z[:], in0=s_z[:], scalar1=-127.0, scalar2=None,
                op0=ALU.max), inc="v")
            c_a0 = P.op("scalar", lambda e: e.activation(
                out=s_a0[:], in_=s_z[:], func=AF.Abs),
                waits=[("v", vz), ("t", q_tr_prev)], inc="s")
            c_sg = P.op("scalar", lambda e: e.activation(
                out=s_sgn[:], in_=s_z[:], func=AF.Sign), inc="s")
            P.op("vector", lambda e: e.tensor_copy(
                out=s_z[:], in_=s_a0[:]), waits=[("s", c_sg)], inc="v")
            for kb in range(6, -1, -1):
                P.op("vector", lambda e, kb=kb: e.tensor_scalar(
                    out=s_b[:], in0=s_z[:], scalar1=(2.0 ** kb) - 0.5,
                    scalar2=float(2 ** kb), op0=ALU.is_ge, op1=ALU.mult), inc="v")
                P.op("vector", lambda e: e.tensor_tensor(
                    out=s_z[:], in0=s_z[:], in1=s_b[:], op=ALU.subtract), inc="v")
            P.op("vector", lambda e: e.tensor_tensor(
                out=s_b[:], in0=s_a0[:], in1=s_z[:], op=ALU.subtract), inc="v")
            vq = P.op("vector", lambda e: e.tensor_tensor(
                out=s_qtm[:], in0=s_b[:], in1=s_sgn[:], op=ALU.mult),
                waits=[("t", q_tr_prev)], inc="v")
            for k in range(NKB):
                sl = 0
                wt = [("v", vq), ("s", s_mT_done)]
                if qtr >= 1:
                    wt.append(("s", qtr_ev[qtr - 1]))
                ct = P.op("tensor", lambda e, k=k, sl=sl: e.transpose(
                    out=p_tr[:, sl:sl + 128], in_=s_qtm[:, k * 128:(k + 1) * 128],
                    identity=s_idb[:]), waits=wt, inc="t")
                ce = P.op("scalar", lambda e, k=k, i=i, sl=sl: e.activation(
                    out=s_qfm[k][:, i * 128:(i + 1) * 128], in_=p_tr[:, sl:sl + 128],
                    func=AF.Identity), waits=[("t", ct)], inc="s")
                qtr_ev[qtr] = ce
                qtr += 1
            q_tr_prev = P.val("t")
            q_done = P.val("s")
            pb = p_mm[i % 2]
            for k in range(NKB):
                w = [("s", q_done)]
                if k == 0 and i >= 2:
                    w.append(("s", mm_free[i - 2]))
                for bk in range(2):
                    P.op("tensor", lambda e, k=k, i=i, bk=bk, pb=pb: e.matmul(
                        out=pb[:, bk * 512:(bk + 1) * 512],
                        lhsT=s_qfm[k][:, i * 128:(i + 1) * 128],
                        rhs=s_mT[k][:, bk * 512:(bk + 1) * 512],
                        start=(k == 0), stop=(k == NKB - 1)), waits=w, inc="t")
            t_mm = P.val("t")
            wt2 = [("t", t_mm), ("s", c_acol)]
            if i >= 1:
                wt2.append(("g", 16 * i))
            c_t2 = P.op("scalar", lambda e, pb=pb: e.activation(
                out=s_t2[:], in_=pb[:], func=AF.Identity,
                scale=s_acol[:, 0:1]), waits=wt2, inc="s")
            mm_free[i] = c_t2
            vx2 = P.op("vector", lambda e, xb=xb: e.tensor_tensor(
                out=s_t2[:], in0=s_t2[:], in1=xb[:], op=ALU.add),
                waits=[("s", c_t2)], inc="v")
            x1_free[i] = vx2
            P.dma("gpsimd", lambda e, i=i: e.dma_start(
                out=o_x2[i * 128:(i + 1) * 128, :], in_=s_t2[:]),
                waits=[("v", vx2)], inc="g")

        g_final = P.val("g")
        P.op("gpsimd", lambda e: e.engine_nop(), waits=[("g", g_final)])
        P.emit(block, sems)
    return nc


_NC_CACHE = {}


def _get(name, builder):
    if name not in _NC_CACHE:
        _NC_CACHE[name] = builder()
    return _NC_CACHE[name]


def kernel(L0_attn_x, L0_attn_ln_w, L0_attn_ln_b, L0_attn_W_in, L0_attn_b_in,
           L0_attn_conv_w, L0_attn_conv_b, L0_attn_A_log, L0_attn_D,
           L0_attn_W_x, L0_attn_b_x, L0_attn_W_out, L0_attn_b_out,
           L0_ffn_weight, L0_ffn_gamma, L0_ffn_beta):
    f32 = np.float32
    x = np.ascontiguousarray(np.asarray(L0_attn_x, f32))
    w_in = np.ascontiguousarray(np.asarray(L0_attn_W_in, f32))
    b_in = np.asarray(L0_attn_b_in, f32).reshape(1, DG)
    ln_w = np.asarray(L0_attn_ln_w, f32).reshape(D, 1)
    ln_b = np.asarray(L0_attn_ln_b, f32).reshape(D, 1)
    conv_w = np.ascontiguousarray(np.asarray(L0_attn_conv_w, f32).reshape(DI, 4))
    conv_b = np.asarray(L0_attn_conv_b, f32).reshape(DI, 1)
    w_x = np.ascontiguousarray(np.asarray(L0_attn_W_x, f32))
    b_x = np.asarray(L0_attn_b_x, f32).reshape(2 * NST, 1)
    a_log = np.ascontiguousarray(np.asarray(L0_attn_A_log, f32))
    dvec = np.asarray(L0_attn_D, f32).reshape(DI, 1)
    w_out = np.ascontiguousarray(np.asarray(L0_attn_W_out, f32))
    b_out = np.asarray(L0_attn_b_out, f32).reshape(1, D)
    wff = np.ascontiguousarray(np.asarray(L0_ffn_weight, f32))
    g_row = np.asarray(L0_ffn_gamma, f32).reshape(1, D)
    be_row = np.asarray(L0_ffn_beta, f32).reshape(1, D)

    identb = np.eye(128, dtype=BF)
    identf = np.eye(128, dtype=f32)
    ones1 = np.ones((1, 128), f32)
    oh = np.zeros((NH, NH * 128), BF)
    for n in range(NH):
        oh[n, n * 128:(n + 1) * 128] = 1
    zeros_halo = np.zeros((HALO, D), f32)

    cores = list(range(NCORES))

    in1 = []
    for c in cores:
        b, h = c // 2, c % 2
        if h == 0:
            xs = np.concatenate([zeros_halo, x[b, 0:T]], 0)
        else:
            xs = np.ascontiguousarray(x[b, T - HALO:2 * T])
        hm = np.ones((128, 1), f32) if h == 1 else np.zeros((128, 1), f32)
        in1.append(dict(xin=xs, w_in=w_in, b_in=b_in, ln_w=ln_w, ln_b=ln_b,
                        conv_w=conv_w, conv_b=conv_b, w_x=w_x,
                        b_xv=b_x, identb=identb, ones1=ones1, hmask=hm))
    r1 = run_bass_kernel_spmd(_get("l1", build_l1), in1, core_ids=cores).results

    xc_full = [np.concatenate([np.asarray(r1[2 * b]["o_xc"]),
                               np.asarray(r1[2 * b + 1]["o_xc"])], 1)
               for b in range(BATCH)]
    b_full = [np.concatenate([np.asarray(r1[2 * b]["o_b"]),
                              np.asarray(r1[2 * b + 1]["o_b"])], 1)
              for b in range(BATCH)]
    c_full = [np.concatenate([np.asarray(r1[2 * b]["o_c"]),
                              np.asarray(r1[2 * b + 1]["o_c"])], 1)
              for b in range(BATCH)]
    in2 = []
    for c in cores:
        b, nh = c // 2, c % 2
        in2.append(dict(xc=xc_full[b],
                        bh=np.ascontiguousarray(b_full[b][nh * NH:(nh + 1) * NH]),
                        ch=np.ascontiguousarray(c_full[b][nh * NH:(nh + 1) * NH]),
                        alog=np.ascontiguousarray(a_log[:, nh * NH:(nh + 1) * NH]),
                        oh=oh))
    r2 = run_bass_kernel_spmd(_get("l2", build_l2), in2, core_ids=cores).results

    in3 = []
    for c in cores:
        b, h = c // 2, c % 2
        sl = np.s_[:, h * T:(h + 1) * T]
        in3.append(dict(
            xin=np.ascontiguousarray(x[b, h * T:(h + 1) * T]),
            p0=np.ascontiguousarray(np.asarray(r2[2 * b]["o_y"])[sl]),
            p1=np.ascontiguousarray(np.asarray(r2[2 * b + 1]["o_y"])[sl]),
            xc=np.asarray(r1[c]["o_xc"]),
            sg=np.asarray(r1[c]["o_sg"]),
            dvec=dvec, w_out=w_out, b_out=b_out,
            g_row=g_row, be_row=be_row, ones1=ones1, identf=identf))
    r3 = run_bass_kernel_spmd(_get("l3", build_l3), in3, core_ids=cores).results

    lmax8 = np.concatenate([np.asarray(r3[c]["o_lm"]).reshape(1, 1)
                            for c in cores], 1)
    in4 = []
    for c in cores:
        in4.append(dict(x1=np.asarray(r3[c]["o_x1"]), wff=wff, lmax8=lmax8,
                        g_row=g_row, be_row=be_row, ones1=ones1,
                        identf=identf, identb=identb))
    r4 = run_bass_kernel_spmd(_get("l4", build_l4), in4, core_ids=cores).results

    out = np.empty((BATCH, TFULL, D), f32)
    for c in cores:
        b, h = c // 2, c % 2
        out[b, h * T:(h + 1) * T] = np.asarray(r4[c]["o_x2"])
    if os.environ.get("KDEBUG"):
        _DBG.update(r1=r1, r2=r2, r3=r3, r4=r4, lmax8=lmax8)
    return out


_DBG = {}


# version 03 (final)
# speedup vs baseline: 1.0373x; 1.0373x over previous
import contextlib
import os
import numpy as np
import ml_dtypes
import concourse.bass as bass
import concourse.mybir as mybir
from concourse.bass_utils import run_bass_kernel_spmd

F32 = mybir.dt.float32
BF16 = mybir.dt.bfloat16
AF = mybir.ActivationFunctionType
ALU = mybir.AluOpType
AX = mybir.AxisListType

NCORES = 8
BATCH = 4
TFULL = 4096
T = 2048
HALO = 8
D = 1024
DI = 2048
DG = 4096
NST = 16
NH = 8
EPS = 1e-5
NPT = DI // 128
NTT = T // 128
BF = ml_dtypes.bfloat16


class Prog:

    ENGINES = ("sync", "vector", "scalar", "tensor", "gpsimd")

    AUTO_SEM = {"vector": "v", "scalar": "s", "gpsimd": "p"}

    def __init__(self):
        self.streams = {e: [] for e in self.ENGINES}
        self.counts = {}
        self.last_op = {}

    def op(self, eng, fn, waits=(), inc=None, inc_by=1):
        w = [x for x in waits if x[1] > 0]
        if eng in self.AUTO_SEM:
            if inc is None:
                inc = self.AUTO_SEM[eng]
            prev = self.last_op.get(eng)
            if prev is not None:
                w.append(prev)
        self._gidx = getattr(self, "_gidx", 0) + 1
        self.streams[eng].append((fn, w, inc, inc_by, self._gidx))
        if inc is not None:
            self.counts[inc] = self.counts.get(inc, 0) + inc_by
            self.last_op[eng] = (inc, self.counts[inc])
        return self.counts.get(inc, 0)

    def dma(self, eng, fn, waits=(), inc=None):
        w = list(waits)
        if inc is not None:
            prev = self.counts.get(inc, 0)
            if prev:
                w.append((inc, prev))
        return self.op(eng, fn, w, inc, 16)

    def val(self, sem):
        return self.counts.get(sem, 0)

    def bar(self, sem):
        return (sem, self.val(sem))

    def emit(self, block, sems):
        import os as _os
        K = int(_os.environ.get("BISECT_K", "10**9".replace("*", "")) or 10**9)
        K = int(_os.environ.get("BISECT_K", str(10**9)))
        for name in self.ENGINES:
            stream = [t for t in self.streams[name] if t[4] <= K]
            if not stream:
                continue

            def make(ops):
                def body(eng):
                    for fn, waits, inc, by, gi in ops:
                        for (s, v) in waits:
                            eng.wait_ge(sems[s], v)
                        instr = fn(eng)
                        if inc is not None:
                            instr.then_inc(sems[inc], by)
                return body

            getattr(block, name)(make(stream))


def make_sems(ctx, nc, names):
    return {k: ctx.enter_context(nc.semaphore(k)) for k in names}


def build_l1():
    nc = bass.Bass("TRN2", target_bir_lowering=False, debug=False,
                   detect_race_conditions=False)
    TH = T + HALO

    xin = nc.dram_tensor("xin", [TH, D], F32, kind="ExternalInput").ap()
    w_in = nc.dram_tensor("w_in", [D, DG], F32, kind="ExternalInput").ap()
    b_in = nc.dram_tensor("b_in", [1, DG], F32, kind="ExternalInput").ap()
    ln_w = nc.dram_tensor("ln_w", [D, 1], F32, kind="ExternalInput").ap()
    ln_b = nc.dram_tensor("ln_b", [D, 1], F32, kind="ExternalInput").ap()
    conv_w = nc.dram_tensor("conv_w", [DI, 4], F32, kind="ExternalInput").ap()
    conv_b = nc.dram_tensor("conv_b", [DI, 1], F32, kind="ExternalInput").ap()
    w_x = nc.dram_tensor("w_x", [DI, 2 * NST], F32, kind="ExternalInput").ap()
    b_xv = nc.dram_tensor("b_xv", [2 * NST, 1], F32, kind="ExternalInput").ap()
    identb = nc.dram_tensor("identb", [128, 128], BF16, kind="ExternalInput").ap()
    ones1 = nc.dram_tensor("ones1", [1, 128], F32, kind="ExternalInput").ap()
    hmask = nc.dram_tensor("hmask", [128, 1], F32, kind="ExternalInput").ap()

    o_xc = nc.dram_tensor("o_xc", [DI, T], BF16, kind="ExternalOutput").ap()
    o_sg = nc.dram_tensor("o_sg", [DI, T], BF16, kind="ExternalOutput").ap()
    o_b = nc.dram_tensor("o_b", [NST, T], F32, kind="ExternalOutput").ap()
    o_c = nc.dram_tensor("o_c", [NST, T], F32, kind="ExternalOutput").ap()

    with contextlib.ExitStack() as ctx:
        sb = lambda n, s, d=F32: ctx.enter_context(nc.sbuf_tensor(n, s, d))
        ps = lambda n, s, d=F32: ctx.enter_context(nc.psum_tensor(n, s, d))

        s_x = [sb(f"s_x{i}", [128, D]) for i in range(2)]
        s_xh = sb("s_xh", [HALO, D])
        s_sq = sb("s_sq", [128, D])
        s_stat = sb("s_stat", [128, 2 * (NTT + 1)])
        s_mu = sb("s_mu", [128, NTT + 1])
        s_nv = sb("s_nv", [128, NTT + 1])
        s_sc = sb("s_sc", [128, NTT + 1])
        s_bcs = sb("s_bcs", [128, NTT + 1])
        s_xn = [sb(f"s_xn{i}", [128, D], BF16) for i in range(2)]
        s_xnh = sb("s_xnh", [HALO, D], BF16)
        s_id = sb("s_id", [128, 128], BF16)
        s_ones1 = sb("s_ones1", [1, 128])
        s_hm = sb("s_hm", [128, 1])
        s_xnT = [sb(f"s_xnT{k}", [128, HALO + T], BF16) for k in range(8)]
        SINGLE_WK = os.environ.get("L1_1WK", "0") == "1"
        s_wk = ([sb("s_wk0", [128, DG])] * 2) if SINGLE_WK else [sb(f"s_wk{k}", [128, DG]) for k in range(2)]
        s_wb = [sb(f"s_wb{k}", [128, DG], BF16) for k in range(8)]
        s_lnw = sb("s_lnw", [128, 8])
        s_lnb = sb("s_lnb", [128, 8])
        s_lnbb = sb("s_lnbb", [128, 8], BF16)
        s_binrow = sb("s_binrow", [1, DG])
        s_bias = sb("s_bias", [128, 32])
        s_cw = sb("s_cw", [128, 4 * NPT])
        s_cb = sb("s_cb", [128, NPT])
        s_xm = sb("s_xm", [128, HALO + T])
        s_xmh = sb("s_xmh", [128, NPT * HALO])
        s_acc = sb("s_acc", [128, T])
        s_xcp = sb("s_xcp", [128, T], BF16)
        s_sgp = [sb(f"s_sgp{i}", [128, T], BF16) for i in range(2)]
        s_wxb = sb("s_wxb", [128, 2 * NST * NPT], BF16)
        s_wxf = sb("s_wxf", [128, 2 * NST * NPT])
        s_bxv = sb("s_bxv", [2 * NST, 1])
        s_bcout = sb("s_bcout", [2 * NST, T])

        p_tr = ps("p_tr", [128, 512], BF16)
        p_mm = [ps(f"p_mm{i}", [128, 512]) for i in range(2)]
        p_misc = ps("p_misc", [128, 512])
        p_bias = p_misc[:, 0:32]
        p_bc = [ps(f"p_bc{i}", [2 * NST, 512]) for i in range(4)]

        sems = make_sems(ctx, nc, ("ld", "wld", "v", "s", "t", "g", "p"))
        block = ctx.enter_context(nc.Block())
        P = Prog()
        TRUNC = int(os.environ.get("L1_TRUNC", "99"))

        NC_MASK = int(os.environ.get("L1_NC", "31"))
        if NC_MASK & 1:
            P.dma("sync", lambda e: e.dma_start(out=s_id[:], in_=identb), inc="ld")
            P.dma("sync", lambda e: e.dma_start(out=s_ones1[:], in_=ones1), inc="ld")
            P.dma("sync", lambda e: e.dma_start(out=s_hm[:], in_=hmask), inc="ld")
        if NC_MASK & 2:
            for k in range(8):
                P.dma("sync", lambda e, k=k: e.dma_start(
                    out=s_lnw[:, k:k + 1], in_=ln_w[k * 128:(k + 1) * 128, :]), inc="ld")
                P.dma("sync", lambda e, k=k: e.dma_start(
                    out=s_lnb[:, k:k + 1], in_=ln_b[k * 128:(k + 1) * 128, :]), inc="ld")
        if NC_MASK & 4:
            P.dma("sync", lambda e: e.dma_start(out=s_binrow[:], in_=b_in), inc="ld")
        if NC_MASK & 8:
            for pt in range(NPT):
                P.dma("sync", lambda e, pt=pt: e.dma_start(
                    out=s_cw[:, 4 * pt:4 * pt + 4],
                    in_=conv_w[pt * 128:(pt + 1) * 128, :]), inc="ld")
                P.dma("sync", lambda e, pt=pt: e.dma_start(
                    out=s_cb[:, pt:pt + 1],
                    in_=conv_b[pt * 128:(pt + 1) * 128, :]), inc="ld")
                P.dma("sync", lambda e, pt=pt: e.dma_start(
                    out=s_wxf[:, 32 * pt:32 * pt + 32],
                    in_=w_x[pt * 128:(pt + 1) * 128, :]), inc="ld")
        if NC_MASK & 16:
            P.dma("sync", lambda e: e.dma_start(out=s_bxv[:], in_=b_xv), inc="ld")
            P.dma("sync", lambda e: e.dma_start(out=s_xh[:], in_=xin[0:HALO, :]), inc="ld")
        ld_const = P.val("ld")

        if os.environ.get("L1_NOWXB"):
            v_wx = 0
        else:
            P.op("vector", lambda e: e.tensor_copy(out=s_wxb[:], in_=s_wxf[:]),
                 waits=[("ld", ld_const)], inc="v")
            v_wx = P.val("v")

        if TRUNC < 2:
            P.dma("gpsimd", lambda e: e.dma_start(out=o_b, in_=s_stat[0:16, 0:T] if False else s_sc[0:16, 0:1].broadcast_to([16, T])), inc="g") if False else None
            P.emit(block, sems)
            return nc
        v_lnbb = 0
        cvt_done = []
        for k in (range(0) if os.environ.get("L1_NOW") else range(8)):
            stage = s_wk[k % 2]
            w = [("ld", ld_const)]
            kgap = 1 if SINGLE_WK else 2
            if k >= kgap:
                w.append(("v", cvt_done[k - kgap]))
            P.dma("sync", lambda e, k=k, st_=stage: e.dma_start(
                out=st_[:], in_=w_in[k * 128:(k + 1) * 128, :]), waits=w, inc="wld")
            P.op("vector", lambda e, k=k, st_=stage: e.tensor_scalar(
                out=s_wb[k][:], in0=st_[:], scalar1=s_lnw[:, k:k + 1], scalar2=None,
                op0=ALU.mult),
                waits=[("wld", 16 * (k + 1))], inc="v")
            for j in (range(0) if os.environ.get("L1_NOBROW") else range(8)):
                wj = [("wld", 16 * (k + 1))]
                if k == 0 and j == 0:
                    pass
                P.op("tensor", lambda e, k=k, j=j, st_=stage: e.matmul(
                    out=p_misc[0:1, :], lhsT=s_lnb[:, k:k + 1],
                    rhs=st_[:, j * 512:(j + 1) * 512],
                    start=True, stop=True), waits=wj + [("v", P.val("v"))], inc="t")
                P.op("vector", lambda e, j=j: e.tensor_tensor(
                    out=s_binrow[0:1, j * 512:(j + 1) * 512],
                    in0=s_binrow[0:1, j * 512:(j + 1) * 512],
                    in1=p_misc[0:1, :], op=ALU.add),
                    waits=[("t", P.val("t")), ("ld", ld_const)], inc="v")
            cvt_done.append(P.val("v"))
        v_wcvt = P.val("v")
        v_brow = v_wcvt
        if os.environ.get("L1_NOW"):
            cvt_done = [0] * 8

        for m in (range(0) if os.environ.get("L1_NOBIAS") else range(32)):
            P.op("tensor", lambda e, m=m: e.matmul(
                out=p_bias[:, m:m + 1], lhsT=s_binrow[0:1, m * 128:(m + 1) * 128],
                rhs=s_ones1[0:1, 0:1], start=True, stop=True),
                waits=[("v", v_brow)], inc="t")
        t_bias = P.val("t")
        if os.environ.get("L1_NOBIAS"):
            s_bias_done = 0
        else:
            s_bias_done = P.op("scalar", lambda e: e.activation(
                out=s_bias[:], in_=p_bias[:], func=AF.Identity),
                waits=[("t", t_bias)], inc="s")

        if TRUNC < 3:
            P.emit(block, sems)
            return nc
        L1_HALO = int(os.environ.get("L1_HALO", "1"))
        L1_TILES = int(os.environ.get("L1_TILES", str(NTT)))
        tile_tr_done = {}
        tile_evac_done = {}
        n_tr = 0

        def stats_ops(tile_idx, xtile, prows, ld_wait):
            si = tile_idx
            c_sq = P.op("scalar", lambda e: e.activation(
                out=s_sq[0:prows, :], in_=xtile, func=AF.Square,
                accum_out=s_stat[0:prows, 2 * si:2 * si + 1]),
                waits=[ld_wait], inc="s")
            c_sm = P.op("scalar", lambda e: e.activation(
                out=s_sq[0:prows, :], in_=xtile, func=AF.Identity,
                accum_out=s_stat[0:prows, 2 * si + 1:2 * si + 2]), inc="s")
            P.op("vector", lambda e: e.tensor_scalar(
                out=s_mu[0:prows, si:si + 1],
                in0=s_stat[0:prows, 2 * si + 1:2 * si + 2],
                scalar1=1.0 / D, scalar2=None, op0=ALU.mult),
                waits=[("s", c_sm)], inc="v")
            P.op("vector", lambda e: e.tensor_scalar(
                out=s_nv[0:prows, si:si + 1],
                in0=s_stat[0:prows, 2 * si:2 * si + 1],
                scalar1=1.0 / D, scalar2=None, op0=ALU.mult), inc="v")
            P.op("vector", lambda e: e.scalar_tensor_tensor(
                out=s_nv[0:prows, si:si + 1], in0=s_mu[0:prows, si:si + 1],
                scalar=s_mu[0:prows, si:si + 1], in1=s_nv[0:prows, si:si + 1],
                op0=ALU.mult, op1=ALU.subtract), inc="v")
            v_nv = P.val("v")
            P.op("vector", lambda e: e.tensor_scalar(
                out=s_sc[0:prows, si:si + 1], in0=s_nv[0:prows, si:si + 1],
                scalar1=-(1.0 + EPS), scalar2=EPS * EPS,
                op0=ALU.mult, op1=ALU.add), inc="v")
            v_nv2 = P.val("v")
            c_rt = P.op("scalar", lambda e: e.activation(
                out=s_sc[0:prows, si:si + 1], in_=s_sc[0:prows, si:si + 1],
                func=AF.Sqrt), waits=[("v", v_nv2)], inc="s")
            P.op("vector", lambda e: e.reciprocal(
                out=s_sc[0:prows, si:si + 1], in_=s_sc[0:prows, si:si + 1]),
                waits=[("s", c_rt)], inc="v")
            P.op("vector", lambda e: e.scalar_tensor_tensor(
                out=s_bcs[0:prows, si:si + 1], in0=s_mu[0:prows, si:si + 1],
                scalar=-1.0, in1=s_sc[0:prows, si:si + 1],
                op0=ALU.mult, op1=ALU.mult), inc="v")
            return P.val("v")

        if not L1_HALO:
            v_h = 0
        v_h = stats_ops(NTT, s_xh[:], HALO, ("ld", ld_const)) if L1_HALO else 0
        if L1_HALO:
            c_xnh = P.op("scalar", lambda e: e.activation(
                out=s_xnh[:], in_=s_xh[:], func=AF.Identity,
                scale=s_sc[0:HALO, NTT:NTT + 1], bias=s_bcs[0:HALO, NTT:NTT + 1]),
                waits=[("v", v_h)], inc="s")
            for k in range(8):
                n_tr += 1
                P.op("tensor", lambda e, k=k: e.transpose(
                    out=p_tr[:, 0:HALO], in_=s_xnh[:, k * 128:(k + 1) * 128],
                    identity=s_id[0:HALO, 0:HALO]),
                    waits=[("s", c_xnh), ("s", P.val("s"))], inc="t")
                P.op("scalar", lambda e, k=k: e.activation(
                    out=s_xnT[k][:, 0:HALO], in_=p_tr[:, 0:HALO], func=AF.Identity),
                    waits=[("t", P.val("t"))], inc="s")

        for i in range(L1_TILES):
            xb = s_x[i % 2]
            w = [("s", 0)]
            if i >= 2:
                w = [("s", tile_evac_done[i - 2])]
            P.dma("sync", lambda e, i=i, xb=xb: e.dma_start(
                out=xb[:], in_=xin[HALO + i * 128:HALO + (i + 1) * 128, :]),
                waits=w, inc="ld")
            L1_PARTS = int(os.environ.get("L1_PARTS", "9"))
            if L1_PARTS < 1:
                continue
            v_i = stats_ops(i, xb[:], 128, ("ld", P.val("ld")))
            if L1_PARTS < 2:
                continue
            xnb = s_xn[i % 2]
            wx = [("v", v_i)]
            if i >= 2:
                wx.append(("t", tile_tr_done[i - 2]))
            XNMODE = int(os.environ.get("L1_XNMODE", "0"))
            if XNMODE == 1:
                c_xn = P.op("scalar", lambda e, i=i, xb=xb: e.activation(
                    out=s_sq[:], in_=xb[:], func=AF.Identity,
                    scale=s_sc[:, i:i + 1], bias=s_bcs[:, i:i + 1]),
                    waits=wx, inc="s")
            elif XNMODE == 2:
                c_xn = P.op("scalar", lambda e, i=i, xb=xb, xnb=xnb: e.activation(
                    out=xnb[:], in_=xb[:], func=AF.Identity),
                    waits=wx, inc="s")
            elif XNMODE == 3:
                P.op("vector", lambda e, i=i, xb=xb: e.scalar_tensor_tensor(
                    out=s_sq[:], in0=xb[:], scalar=s_sc[:, i:i + 1],
                    in1=s_bcs[:, i:i + 1].broadcast_to([128, D]),
                    op0=ALU.mult, op1=ALU.add), waits=wx)
                c_xn = P.op("vector", lambda e, xnb=xnb: e.tensor_copy(
                    out=xnb[:], in_=s_sq[:]))
            else:
                c_xn = P.op("scalar", lambda e, i=i, xb=xb, xnb=xnb: e.activation(
                    out=xnb[:], in_=xb[:], func=AF.Identity,
                    scale=s_sc[:, i:i + 1], bias=s_bcs[:, i:i + 1]),
                    waits=wx, inc="s")
            for k in range(8):
                sl = 0
                wt = [("s", c_xn)]
                if n_tr >= 1:
                    wt.append(("s", tile_evac_done.get(("tr", n_tr - 1), 0)))
                n_tr += 1
                ct = P.op("tensor", lambda e, k=k, xnb=xnb, sl=sl: e.transpose(
                    out=p_tr[:, sl:sl + 128], in_=xnb[:, k * 128:(k + 1) * 128],
                    identity=s_id[:]), waits=wt, inc="t")
                ce = P.op("scalar", lambda e, k=k, i=i, sl=sl: e.activation(
                    out=s_xnT[k][:, HALO + i * 128:HALO + (i + 1) * 128],
                    in_=p_tr[:, sl:sl + 128], func=AF.Identity),
                    waits=[("t", ct)], inc="s")
                tile_evac_done[("tr", n_tr - 1)] = ce
            tile_tr_done[i] = P.val("t")
            tile_evac_done[i] = P.val("s")

        v_phaseA = P.val("v")
        s_phaseA = P.val("s")
        t_phaseA = P.val("t")

        mm_group = 0
        evac_group_done = []
        mm_group_t = []

        def mm_tile(m, c0, c1c, dest_fn, act_func, bias_col, extra_waits=()):
            nonlocal mm_group
            g = mm_group
            mm_group += 1
            pb = p_mm[g % 2]
            n = c1c - c0
            for k in range(8):
                w = [("s", s_phaseA), ("v", v_wcvt)]
                if k == 0 and g >= 2:
                    w.append(("s", evac_group_done[g - 2]))
                P.op("tensor", lambda e, k=k, m=m, pb=pb, c0=c0, c1c=c1c, n=n: e.matmul(
                    out=pb[:, 0:n], lhsT=s_wb[k][:, m * 128:(m + 1) * 128],
                    rhs=s_xnT[k][:, c0:c1c],
                    start=(k == 0), stop=(k == 7)), waits=w, inc="t")
            tmm = P.val("t")
            mm_group_t.append(tmm)
            ce = P.op("scalar", lambda e, pb=pb, n=n: e.activation(
                out=dest_fn(), in_=pb[:, 0:n], func=act_func,
                bias=bias_col), waits=[("t", tmm), ("s", s_bias_done),
                                       *extra_waits], inc="s")
            evac_group_done.append(ce)
            return ce

        for pt in range(NPT):
            mm_tile(pt, 0, HALO,
                    (lambda pt=pt: s_xmh[:, pt * HALO:(pt + 1) * HALO]),
                    AF.Identity, s_bias[:, pt:pt + 1])
        vmaskall = P.op("vector", lambda e: e.tensor_scalar(
            out=s_xmh[:], in0=s_xmh[:], scalar1=s_hm[:, 0:1], scalar2=None,
            op0=ALU.mult),
            waits=[("s", P.val("s")), ("ld", ld_const)], inc="v")

        if TRUNC < 4:
            P.emit(block, sems)
            return nc
        sg_spill_done = {}
        for mi, m in enumerate(range(16, 32)):
            pt = m - 16
            buf = s_sgp[pt % 2]
            ew = []
            if pt >= 2:
                ew.append(("g", sg_spill_done[pt - 2]))
            last = None
            for tb in range(4):
                last = mm_tile(m, HALO + tb * 512, HALO + (tb + 1) * 512,
                               (lambda buf=buf, tb=tb: buf[:, tb * 512:(tb + 1) * 512]),
                               AF.Silu, s_bias[:, m:m + 1], ew)
            cg = P.dma("gpsimd", lambda e, pt=pt, buf=buf: e.dma_start(
                out=o_sg[pt * 128:(pt + 1) * 128, :], in_=buf[:]),
                waits=[("s", last)], inc="g")
            sg_spill_done[pt] = cg

        if TRUNC < 5:
            P.emit(block, sems)
            return nc
        xc_spill_done = {}
        v_conv_done = {}
        wx_done = {}
        for pt in range(NPT):
            m = pt
            last = None
            for tb in range(4):
                ew = []
                if pt >= 1:
                    ew.append(("v", v_conv_done[pt - 1]))
                last = mm_tile(m, HALO + tb * 512, HALO + (tb + 1) * 512,
                               (lambda tb=tb: s_xm[:, HALO + tb * 512:HALO + (tb + 1) * 512]),
                               AF.Identity, s_bias[:, m:m + 1], ew)
            cmask = P.op("vector", lambda e, pt=pt: e.tensor_copy(
                out=s_xm[:, 0:HALO], in_=s_xmh[:, pt * HALO:(pt + 1) * HALO]),
                waits=[("v", vmaskall)] + ([("v", v_conv_done[pt - 1])] if pt >= 1 else []),
                inc="v")
            cw = s_cw
            c1 = P.op("vector", lambda e, pt=pt: e.scalar_tensor_tensor(
                out=s_acc[:], in0=s_xm[:, HALO:HALO + T],
                scalar=cw[:, 4 * pt + 3:4 * pt + 4],
                in1=s_cb[:, pt:pt + 1].broadcast_to([128, T]),
                op0=ALU.mult, op1=ALU.add),
                waits=[("s", last), ("v", cmask)], inc="v")
            for kk in range(3):
                c1 = P.op("vector", lambda e, pt=pt, kk=kk: e.scalar_tensor_tensor(
                    out=s_acc[:], in0=s_xm[:, 5 + kk:5 + kk + T],
                    scalar=cw[:, 4 * pt + kk:4 * pt + kk + 1],
                    in1=s_acc[:], op0=ALU.mult, op1=ALU.add), inc="v")
            v_conv_done[pt] = c1
            ws = [("v", c1)]
            if pt >= 1:
                ws.append(("g", xc_spill_done[pt - 1]))
                ws.append(("t", wx_done[pt - 1]))
            csil = P.op("scalar", lambda e: e.activation(
                out=s_xcp[:], in_=s_acc[:], func=AF.Silu), waits=ws, inc="s")
            cspill = P.dma("gpsimd", lambda e, pt=pt: e.dma_start(
                out=o_xc[pt * 128:(pt + 1) * 128, :], in_=s_xcp[:]),
                waits=[("s", csil)], inc="g")
            xc_spill_done[pt] = cspill
            for tb in (range(0) if os.environ.get("L1_NOWX") else range(4)):
                P.op("tensor", lambda e, pt=pt, tb=tb: e.matmul(
                    out=p_bc[tb][:, :],
                    lhsT=s_wxb[:, 32 * pt:32 * pt + 32],
                    rhs=s_xcp[:, tb * 512:(tb + 1) * 512],
                    start=(pt == 0), stop=(pt == NPT - 1)),
                    waits=[("s", csil), ("v", v_wx)], inc="t")
            wx_done[pt] = P.val("t")

        t_all = P.val("t")
        ev = 0
        for tb in (range(0) if os.environ.get("L1_NOWX") else range(4)):
            ev = P.op("scalar", lambda e, tb=tb: e.activation(
                out=s_bcout[:, tb * 512:(tb + 1) * 512], in_=p_bc[tb][:, :],
                func=AF.Identity, bias=s_bxv[:, 0:1]),
                waits=[("t", t_all), ("ld", ld_const)], inc="s")
        if os.environ.get("L1_NOWX"):
            ev = P.val("s")
        P.dma("gpsimd", lambda e: e.dma_start(out=o_b, in_=s_bcout[0:16, :]),
              waits=[("s", ev)], inc="g")
        P.dma("gpsimd", lambda e: e.dma_start(out=o_c, in_=s_bcout[16:32, :]),
              inc="g")
        g_final = P.val("g")
        P.op("gpsimd", lambda e: e.engine_nop(), waits=[("g", g_final)])

        P.emit(block, sems)
    return nc




def build_l2():
    nc = bass.Bass("TRN2", target_bir_lowering=False, debug=False,
                   detect_race_conditions=False)
    TH2 = TFULL // 2

    xc = nc.dram_tensor("xc", [DI, TFULL], BF16, kind="ExternalInput").ap()
    bh = nc.dram_tensor("bh", [NH, TFULL], F32, kind="ExternalInput").ap()
    ch = nc.dram_tensor("ch", [NH, TFULL], F32, kind="ExternalInput").ap()
    alog = nc.dram_tensor("alog", [DI, NH], F32, kind="ExternalInput").ap()
    oh = nc.dram_tensor("oh", [NH, NH * 128], BF16, kind="ExternalInput").ap()

    o_y = nc.dram_tensor("o_y", [DI, TFULL], BF16, kind="ExternalOutput").ap()

    with contextlib.ExitStack() as ctx:
        sb = lambda n, s, d=F32: ctx.enter_context(nc.sbuf_tensor(n, s, d))
        ps = lambda n, s, d=F32: ctx.enter_context(nc.psum_tensor(n, s, d))

        s_bh = sb("s_bh", [NH, TFULL])
        s_ch = sb("s_ch", [NH, TFULL])
        s_bhb = sb("s_bhb", [NH, TFULL], BF16)
        s_chb = sb("s_chb", [NH, TFULL], BF16)
        s_oh = sb("s_oh", [NH, NH * 128], BF16)
        s_alog = sb("s_alog", [128, NPT * NH])
        s_A = sb("s_A", [128, NPT * NH])
        s_bb = sb("s_bb", [128, NH * TH2], BF16)
        s_cc = sb("s_cc", [128, NH * TH2], BF16)
        s_xcp = [sb(f"s_xcp{i}", [128, TH2], BF16) for i in range(2)]
        s_u = sb("s_u", [128, TH2], BF16)
        s_ha = sb("s_ha", [128, TH2])
        s_g = [sb(f"s_g{i}", [128, TH2], BF16) for i in range(2)]
        s_y = sb("s_y", [128, TH2])
        s_yb = sb("s_yb", [128, TH2], BF16)
        s_hc = sb("s_hc", [128, NPT * NH])

        p_dup = [ps(f"p_dup{i}", [128, 512]) for i in range(2)]

        sems = make_sems(ctx, nc, ("ld", "xld", "v", "s", "t", "g", "p"))
        block = ctx.enter_context(nc.Block())
        P = Prog()

        P.dma("sync", lambda e: e.dma_start(out=s_bh[:], in_=bh), inc="ld")
        P.dma("sync", lambda e: e.dma_start(out=s_ch[:], in_=ch), inc="ld")
        P.dma("sync", lambda e: e.dma_start(out=s_oh[:], in_=oh), inc="ld")
        for pt in range(NPT):
            P.dma("sync", lambda e, pt=pt: e.dma_start(
                out=s_alog[:, pt * NH:(pt + 1) * NH],
                in_=alog[pt * 128:(pt + 1) * 128, :]), inc="ld")
        ld0 = P.val("ld")
        P.op("scalar", lambda e: e.activation(
            out=s_A[:], in_=s_alog[:], func=AF.Exp), waits=[("ld", ld0)], inc="s")
        sA = P.op("scalar", lambda e: e.activation(
            out=s_A[:], in_=s_A[:], func=AF.Exp, scale=-1.0), inc="s")
        P.op("vector", lambda e: e.tensor_copy(out=s_bhb[:], in_=s_bh[:]),
             waits=[("ld", ld0)], inc="v")
        vBC = P.op("vector", lambda e: e.tensor_copy(out=s_chb[:], in_=s_ch[:]),
                   inc="v")

        dup_evac = {}
        gdup = 0
        pool_done = {}
        gctr = 0
        vy_prev = 0
        dma_prev = 0
        xld_for = {}
        u_last = {}
        P.dma("sync", lambda e: e.dma_start(
            out=s_xcp[0][:], in_=xc[0:128, 0:TH2]), inc="xld")
        xld_for[0] = P.val("xld")

        for half in range(2):
            t0 = half * TH2
            v_before_dups = P.val("v")
            for n in range(NH):
                for which, src, dst in (("b", s_bhb, s_bb), ("c", s_chb, s_cc)):
                    for bk in range(4):
                        pd = p_dup[gdup % 2]
                        w = [("v", vBC)]
                        if gdup >= 2:
                            w.append(("s", dup_evac[gdup - 2]))
                        P.op("tensor", lambda e, n=n, bk=bk, pd=pd, t0=t0, src=src: e.matmul(
                            out=pd[:], lhsT=s_oh[:, n * 128:(n + 1) * 128],
                            rhs=src[:, t0 + bk * 512:t0 + (bk + 1) * 512],
                            start=True, stop=True), waits=w, inc="t")
                        we = [("t", P.val("t"))]
                        if half == 1:
                            we.append(("v", v_before_dups))
                        c1 = P.op("scalar", lambda e, n=n, bk=bk, pd=pd, dst=dst: e.activation(
                            out=dst[:, n * TH2 + bk * 512:n * TH2 + (bk + 1) * 512],
                            in_=pd[:], func=AF.Identity), waits=we, inc="s")
                        dup_evac[gdup] = c1
                        gdup += 1
            dup_done = P.val("s")

            for pt in range(NPT):
                q = half * NPT + pt
                xb_cp = s_xcp[q % 2]
                xld = xld_for[q]
                if q + 1 < 2 * NPT:
                    h2, p2 = divmod(q + 1, NPT)
                    wpre = [("v", u_last[q - 1])] if q >= 1 else []
                    P.dma("sync", lambda e, p2=p2, h2=h2, q=q: e.dma_start(
                        out=s_xcp[(q + 1) % 2][:],
                        in_=xc[p2 * 128:(p2 + 1) * 128,
                               h2 * TH2:(h2 + 1) * TH2]),
                        waits=wpre, inc="xld")
                    xld_for[q + 1] = P.val("xld")
                for n in range(NH):
                    ci = pt * NH + n
                    wu = [("xld", xld), ("s", dup_done)]
                    P.op("vector", lambda e, n=n, xb_cp=xb_cp: e.tensor_tensor(
                        out=s_u[:], in0=xb_cp[:],
                        in1=s_bb[:, n * TH2:(n + 1) * TH2], op=ALU.mult),
                        waits=wu, inc="v")
                    if n == NH - 1:
                        u_last[q] = P.val("v")
                    if half == 0:
                        init = 0.0
                    else:
                        init = s_hc[:, ci:ci + 1]
                    P.op("vector", lambda e, ci=ci, init=init: e.tensor_tensor_scan(
                        out=s_ha[:], data0=s_A[:, ci:ci + 1].broadcast_to([128, TH2]),
                        data1=s_u[:], initial=init,
                        op0=ALU.mult, op1=ALU.add),
                        waits=[("s", sA)], inc="v")
                    if half == 0:
                        P.op("vector", lambda e, ci=ci: e.tensor_copy(
                            out=s_hc[:, ci:ci + 1], in_=s_ha[:, TH2 - 1:TH2]), inc="v")
                    gb = s_g[gctr % 2]
                    wg = []
                    if gctr >= 2:
                        wg.append(("p", pool_done[gctr - 2]))
                    vg = P.op("vector", lambda e, n=n, gb=gb: e.tensor_tensor(
                        out=gb[:], in0=s_ha[:],
                        in1=s_cc[:, n * TH2:(n + 1) * TH2], op=ALU.mult),
                        waits=wg, inc="v")
                    wp = [("v", vg)]
                    if n == 0:
                        wp.append(("v", vy_prev))
                    if n == 0:
                        pc = P.op("gpsimd", lambda e, gb=gb: e.tensor_copy(
                            out=s_y[:], in_=gb[:]), waits=wp, inc="p")
                    else:
                        pc = P.op("gpsimd", lambda e, gb=gb: e.tensor_tensor(
                            out=s_y[:], in0=s_y[:], in1=gb[:], op=ALU.add),
                            waits=wp, inc="p")
                    pool_done[gctr] = pc
                    gctr += 1
                wy = [("p", pool_done[gctr - 1])]
                if dma_prev:
                    wy.append(("g", dma_prev))
                vy_prev = P.op("vector", lambda e: e.tensor_copy(
                    out=s_yb[:], in_=s_y[:]), waits=wy, inc="v")
                dma_prev = P.dma("gpsimd", lambda e, pt=pt, t0=t0: e.dma_start(
                    out=o_y[pt * 128:(pt + 1) * 128, t0:t0 + TH2], in_=s_yb[:]),
                    waits=[("v", vy_prev)], inc="g")

        g_final = P.val("g")
        P.op("gpsimd", lambda e: e.engine_nop(), waits=[("g", g_final)])
        P.emit(block, sems)
    return nc


def build_l3():
    nc = bass.Bass("TRN2", target_bir_lowering=False, debug=False,
                   detect_race_conditions=False)

    xin = nc.dram_tensor("xin", [T, D], F32, kind="ExternalInput").ap()
    p0 = nc.dram_tensor("p0", [DI, T], BF16, kind="ExternalInput").ap()
    p1 = nc.dram_tensor("p1", [DI, T], BF16, kind="ExternalInput").ap()
    xc = nc.dram_tensor("xc", [DI, T], BF16, kind="ExternalInput").ap()
    sg = nc.dram_tensor("sg", [DI, T], BF16, kind="ExternalInput").ap()
    dvec = nc.dram_tensor("dvec", [DI, 1], F32, kind="ExternalInput").ap()
    w_out = nc.dram_tensor("w_out", [DI, D], F32, kind="ExternalInput").ap()
    b_out = nc.dram_tensor("b_out", [1, D], F32, kind="ExternalInput").ap()
    g_row = nc.dram_tensor("g_row", [1, D], F32, kind="ExternalInput").ap()
    be_row = nc.dram_tensor("be_row", [1, D], F32, kind="ExternalInput").ap()
    ones1 = nc.dram_tensor("ones1", [1, 128], F32, kind="ExternalInput").ap()
    identf = nc.dram_tensor("identf", [128, 128], F32, kind="ExternalInput").ap()

    o_x1 = nc.dram_tensor("o_x1", [T, D], F32, kind="ExternalOutput").ap()
    o_lm = nc.dram_tensor("o_lm", [1, 1], F32, kind="ExternalOutput").ap()

    with contextlib.ExitStack() as ctx:
        sb = lambda n, s, d=F32: ctx.enter_context(nc.sbuf_tensor(n, s, d))
        ps = lambda n, s, d=F32: ctx.enter_context(nc.psum_tensor(n, s, d))

        s_p0 = sb("s_p0", [128, T], BF16)
        s_p1 = sb("s_p1", [128, T], BF16)
        s_xcp = sb("s_xcp", [128, T], BF16)
        s_sgp = sb("s_sgp", [128, T], BF16)
        s_d = sb("s_d", [128, NPT])
        s_y = sb("s_y", [128, T])
        s_yg = [sb(f"s_yg{i}", [128, T], BF16) for i in range(NPT)]
        s_wo = [sb(f"s_wo{i}", [128, D]) for i in range(2)]
        s_wob = [sb(f"s_wob{i}", [128, D], BF16) for i in range(NPT)]
        s_borow = sb("s_borow", [1, D])
        s_grow = sb("s_grow", [1, D])
        s_berow = sb("s_berow", [1, D])
        s_bo = sb("s_bo", [128, D])
        s_ga = sb("s_ga", [128, D])
        s_be = sb("s_be", [128, D])
        s_ones1 = sb("s_ones1", [1, 128])
        s_idf = sb("s_idf", [128, 128])
        s_x = [sb(f"s_x{i}", [128, D]) for i in range(2)]
        s_sq = sb("s_sq", [128, D])
        s_ln1 = sb("s_ln1", [128, D])
        s_x1 = [sb(f"s_x1{i}", [128, D]) for i in range(2)]
        s_z = sb("s_z", [128, D])
        s_stat = sb("s_stat", [128, 4 * NTT])
        s_mu = sb("s_mu", [128, 2 * NTT])
        s_nv = sb("s_nv", [128, 2 * NTT])
        s_sc = sb("s_sc", [128, 2 * NTT])
        s_bcs = sb("s_bcs", [128, 2 * NTT])
        s_mx = sb("s_mx", [128, 2])
        s_mrow = sb("s_mrow", [1, 128])
        s_lm = sb("s_lm", [1, 1])

        p_mm = [ps(f"p_mm{i}", [128, D]) for i in range(2)]
        p_b = ps("p_b", [128, D])
        p_r = ps("p_r", [1, 128])

        sems = make_sems(ctx, nc, ("ld", "wld", "xt", "v", "s", "t", "g", "p"))
        block = ctx.enter_context(nc.Block())
        P = Prog()

        P.dma("sync", lambda e: e.dma_start(out=s_borow[:], in_=b_out), inc="ld")
        P.dma("sync", lambda e: e.dma_start(out=s_grow[:], in_=g_row), inc="ld")
        P.dma("sync", lambda e: e.dma_start(out=s_berow[:], in_=be_row), inc="ld")
        P.dma("sync", lambda e: e.dma_start(out=s_ones1[:], in_=ones1), inc="ld")
        P.dma("sync", lambda e: e.dma_start(out=s_idf[:], in_=identf), inc="ld")
        for pt in range(NPT):
            P.dma("sync", lambda e, pt=pt: e.dma_start(
                out=s_d[:, pt:pt + 1], in_=dvec[pt * 128:(pt + 1) * 128, :]), inc="ld")
        ld0 = P.val("ld")

        prev_ev = 0
        for (row, dst) in ((s_borow, s_bo), (s_grow, s_ga), (s_berow, s_be)):
            for bk in range(2):
                P.op("tensor", lambda e, row=row, bk=bk: e.matmul(
                    out=p_b[:, bk * 512:(bk + 1) * 512], lhsT=s_ones1[0:1, :],
                    rhs=row[0:1, bk * 512:(bk + 1) * 512], start=True, stop=True),
                    waits=[("ld", ld0), ("s", prev_ev)], inc="t")
            prev_ev = P.op("scalar", lambda e, dst=dst: e.activation(
                out=dst[:], in_=p_b[:], func=AF.Identity),
                waits=[("t", P.val("t"))], inc="s")
        s_bcast = P.val("s")

        cvt = []
        for k in range(NPT):
            stg = s_wo[k % 2]
            w = []
            if k >= 2:
                w.append(("v", cvt[k - 2]))
            P.dma("sync", lambda e, k=k, stg=stg: e.dma_start(
                out=stg[:], in_=w_out[k * 128:(k + 1) * 128, :]), waits=w, inc="wld")
            c = P.op("vector", lambda e, k=k, stg=stg: e.tensor_copy(
                out=s_wob[k][:], in_=stg[:]), waits=[("wld", 16 * (k + 1))], inc="v")
            cvt.append(c)
        v_wcvt = P.val("v")

        yg_done = []
        for pt in range(NPT):
            w = []
            if pt >= 1:
                w.append(("v", yg_done[pt - 1]))
            P.dma("sync", lambda e, pt=pt: e.dma_start(
                out=s_p0[:], in_=p0[pt * 128:(pt + 1) * 128, :]), waits=w, inc="ld")
            P.dma("sync", lambda e, pt=pt: e.dma_start(
                out=s_p1[:], in_=p1[pt * 128:(pt + 1) * 128, :]), inc="ld")
            P.dma("sync", lambda e, pt=pt: e.dma_start(
                out=s_xcp[:], in_=xc[pt * 128:(pt + 1) * 128, :]), inc="ld")
            P.dma("sync", lambda e, pt=pt: e.dma_start(
                out=s_sgp[:], in_=sg[pt * 128:(pt + 1) * 128, :]), inc="ld")
            ldp = P.val("ld")
            P.op("vector", lambda e: e.tensor_tensor(
                out=s_y[:], in0=s_p0[:], in1=s_p1[:], op=ALU.add),
                waits=[("ld", ldp)], inc="v")
            P.op("vector", lambda e, pt=pt: e.scalar_tensor_tensor(
                out=s_y[:], in0=s_xcp[:], scalar=s_d[:, pt:pt + 1],
                in1=s_y[:], op0=ALU.mult, op1=ALU.add), inc="v")
            c = P.op("vector", lambda e, pt=pt: e.tensor_tensor(
                out=s_yg[pt][:], in0=s_y[:], in1=s_sgp[:], op=ALU.mult), inc="v")
            yg_done.append(c)
        v_yg = P.val("v")

        def stats2(xtile, sl, base_wait):
            c_sq = P.op("scalar", lambda e: e.activation(
                out=s_sq[:], in_=xtile, func=AF.Square,
                accum_out=s_stat[:, 2 * sl:2 * sl + 1]), waits=[base_wait], inc="s")
            c_sm = P.op("scalar", lambda e: e.activation(
                out=s_sq[:], in_=xtile, func=AF.Identity,
                accum_out=s_stat[:, 2 * sl + 1:2 * sl + 2]), inc="s")
            P.op("vector", lambda e: e.tensor_scalar(
                out=s_mu[:, sl:sl + 1], in0=s_stat[:, 2 * sl + 1:2 * sl + 2],
                scalar1=1.0 / D, scalar2=None, op0=ALU.mult),
                waits=[("s", c_sm)], inc="v")
            P.op("vector", lambda e: e.tensor_scalar(
                out=s_nv[:, sl:sl + 1], in0=s_stat[:, 2 * sl:2 * sl + 1],
                scalar1=1.0 / D, scalar2=None, op0=ALU.mult), inc="v")
            P.op("vector", lambda e: e.scalar_tensor_tensor(
                out=s_nv[:, sl:sl + 1], in0=s_mu[:, sl:sl + 1],
                scalar=s_mu[:, sl:sl + 1], in1=s_nv[:, sl:sl + 1],
                op0=ALU.mult, op1=ALU.subtract), inc="v")
            return P.val("v")

        def finish_scale(sl, combined):
            scale = -(1.0 + EPS) if combined else -1.0
            bias = EPS * EPS if combined else EPS
            P.op("vector", lambda e, sl=sl, scale=scale, bias=bias: e.tensor_scalar(
                out=s_sc[:, sl:sl + 1], in0=s_nv[:, sl:sl + 1],
                scalar1=scale, scalar2=bias, op0=ALU.mult, op1=ALU.add), inc="v")
            c_rt = P.op("scalar", lambda e, sl=sl: e.activation(
                out=s_sc[:, sl:sl + 1], in_=s_sc[:, sl:sl + 1],
                func=AF.Sqrt), waits=[("v", P.val("v"))], inc="s")
            P.op("vector", lambda e, sl=sl: e.reciprocal(
                out=s_sc[:, sl:sl + 1], in_=s_sc[:, sl:sl + 1]),
                waits=[("s", c_rt)], inc="v")
            P.op("vector", lambda e, sl=sl: e.scalar_tensor_tensor(
                out=s_bcs[:, sl:sl + 1], in0=s_mu[:, sl:sl + 1],
                scalar=-1.0, in1=s_sc[:, sl:sl + 1],
                op0=ALU.mult, op1=ALU.mult), inc="v")
            return P.val("v")

        ev_done = {}
        x_free = {}
        x1_free = {}
        mxv = 0
        for i in range(NTT):
            pb = p_mm[i % 2]
            for k in range(NPT):
                w = [("v", v_yg)]
                if k == 0 and i >= 2:
                    w.append(("v", ev_done[i - 2]))
                for bk in range(2):
                    P.op("tensor", lambda e, k=k, i=i, bk=bk, pb=pb: e.matmul(
                        out=pb[:, bk * 512:(bk + 1) * 512],
                        lhsT=s_yg[k][:, i * 128:(i + 1) * 128],
                        rhs=s_wob[k][:, bk * 512:(bk + 1) * 512],
                        start=(k == 0), stop=(k == NPT - 1)), waits=w, inc="t")
            t_mm = P.val("t")
            xb = s_x[i % 2]
            w = []
            if i >= 2:
                w.append(("v", x_free[i - 2]))
            P.dma("sync", lambda e, i=i, xb=xb: e.dma_start(
                out=xb[:], in_=xin[i * 128:(i + 1) * 128, :]), waits=w, inc="ld")
            ldx = P.val("ld")
            sl1 = 2 * i
            stats2(xb[:], sl1, ("ld", ldx))
            finish_scale(sl1, False)
            c_ln1 = P.op("scalar", lambda e, xb=xb, sl1=sl1: e.activation(
                out=s_ln1[:], in_=xb[:], func=AF.Identity,
                scale=s_sc[:, sl1:sl1 + 1], bias=s_bcs[:, sl1:sl1 + 1]),
                waits=[("v", P.val("v"))], inc="s")
            x1b = s_x1[i % 2]
            w = [("s", c_ln1)]
            if i >= 2:
                w.append(("v", x1_free[i - 2]))
                w.append(("g", 16 * (i - 1)))
            P.op("vector", lambda e, xb=xb: e.tensor_tensor(
                out=s_ln1[:], in0=s_ln1[:], in1=xb[:], op=ALU.add), waits=w, inc="v")
            P.op("vector", lambda e, pb=pb, x1b=x1b: e.tensor_tensor(
                out=x1b[:], in0=s_ln1[:], in1=pb[:], op=ALU.add),
                waits=[("t", t_mm)], inc="v")
            c_x1 = P.op("vector", lambda e, x1b=x1b: e.tensor_tensor(
                out=x1b[:], in0=x1b[:], in1=s_bo[:], op=ALU.add),
                waits=[("s", s_bcast)], inc="v")
            ev_done[i] = c_x1
            x_free[i] = c_x1
            P.dma("gpsimd", lambda e, i=i, x1b=x1b: e.dma_start(
                out=o_x1[i * 128:(i + 1) * 128, :], in_=x1b[:]),
                waits=[("v", c_x1)], inc="g")
            sl2 = 2 * i + 1
            stats2(x1b[:], sl2, ("v", c_x1))
            finish_scale(sl2, True)
            c_z = P.op("scalar", lambda e, x1b=x1b, sl2=sl2: e.activation(
                out=s_z[:], in_=x1b[:], func=AF.Identity,
                scale=s_sc[:, sl2:sl2 + 1], bias=s_bcs[:, sl2:sl2 + 1]),
                waits=[("v", P.val("v"))], inc="s")
            x1_free[i] = P.val("v") + 2
            P.op("vector", lambda e: e.tensor_tensor(
                out=s_z[:], in0=s_z[:], in1=s_ga[:], op=ALU.mult),
                waits=[("s", c_z)], inc="v")
            P.op("vector", lambda e: e.tensor_tensor(
                out=s_z[:], in0=s_z[:], in1=s_be[:], op=ALU.add), inc="v")
            P.op("vector", lambda e: e.tensor_reduce(
                out=s_mx[:, 1:2], in_=s_z[:], axis=AX.X, op=ALU.max,
                apply_absolute_value=True), inc="v")
            if i == 0:
                mxv = P.op("vector", lambda e: e.tensor_copy(
                    out=s_mx[:, 0:1], in_=s_mx[:, 1:2]), inc="v")
            else:
                mxv = P.op("vector", lambda e: e.tensor_tensor(
                    out=s_mx[:, 0:1], in0=s_mx[:, 0:1], in1=s_mx[:, 1:2],
                    op=ALU.max), inc="v")
            x1_free[i] = mxv

        P.op("tensor", lambda e: e.matmul(
            out=p_r[:], lhsT=s_mx[:, 0:1], rhs=s_idf[:], start=True, stop=True),
            waits=[("v", mxv)], inc="t")
        P.op("scalar", lambda e: e.activation(
            out=s_mrow[:], in_=p_r[:], func=AF.Identity),
            waits=[("t", P.val("t"))], inc="s")
        vlm = P.op("vector", lambda e: e.tensor_reduce(
            out=s_lm[:], in_=s_mrow[:], axis=AX.X, op=ALU.max),
            waits=[("s", P.val("s"))], inc="v")
        P.dma("gpsimd", lambda e: e.dma_start(out=o_lm, in_=s_lm[:]),
              waits=[("v", vlm)], inc="g")
        g_final = P.val("g")
        P.op("gpsimd", lambda e: e.engine_nop(), waits=[("g", g_final)])
        P.emit(block, sems)
    return nc


def build_l4():
    nc = bass.Bass("TRN2", target_bir_lowering=False, debug=False,
                   detect_race_conditions=False)

    x1 = nc.dram_tensor("x1", [T, D], F32, kind="ExternalInput").ap()
    wff = nc.dram_tensor("wff", [D, D], F32, kind="ExternalInput").ap()
    lmax8 = nc.dram_tensor("lmax8", [1, NCORES], F32, kind="ExternalInput").ap()
    g_row = nc.dram_tensor("g_row", [1, D], F32, kind="ExternalInput").ap()
    be_row = nc.dram_tensor("be_row", [1, D], F32, kind="ExternalInput").ap()
    ones1 = nc.dram_tensor("ones1", [1, 128], F32, kind="ExternalInput").ap()
    identf = nc.dram_tensor("identf", [128, 128], F32, kind="ExternalInput").ap()
    identb = nc.dram_tensor("identb", [128, 128], BF16, kind="ExternalInput").ap()

    o_x2 = nc.dram_tensor("o_x2", [T, D], F32, kind="ExternalOutput").ap()

    NKB = D // 128
    with contextlib.ExitStack() as ctx:
        sb = lambda n, s, d=F32: ctx.enter_context(nc.sbuf_tensor(n, s, d))
        ps = lambda n, s, d=F32: ctx.enter_context(nc.psum_tensor(n, s, d))

        s_w = [sb(f"s_w{i}", [128, D]) for i in range(2)]
        s_absw = [sb(f"s_absw{i}", [128, D]) for i in range(NKB)]
        s_sgnb = [sb(f"s_sgnb{i}", [128, D], BF16) for i in range(NKB)]
        s_scr = sb("s_scr", [128, D])
        s_mask = sb("s_mask", [128, D])
        s_m0 = sb("s_m0", [128, D], BF16)
        s_mT = [sb(f"s_mT{i}", [128, D], BF16) for i in range(NKB)]
        s_cols = sb("s_cols", [128, 3 * NKB])
        s_red = sb("s_red", [128, 4])
        s_row = sb("s_row", [1, 128])
        s_sca = sb("s_sca", [1, 12])
        s_dcol = sb("s_dcol", [128, 1])
        s_qcol = sb("s_qcol", [128, 1])
        s_acol = sb("s_acol", [128, 1])
        s_lmax = sb("s_lmax", [1, NCORES])
        s_grow = sb("s_grow", [1, D])
        s_berow = sb("s_berow", [1, D])
        s_ga = sb("s_ga", [128, D])
        s_be = sb("s_be", [128, D])
        s_ones1 = sb("s_ones1", [1, 128])
        s_idf = sb("s_idf", [128, 128])
        s_idb = sb("s_idb", [128, 128], BF16)
        s_x1 = [sb(f"s_x1{i}", [128, D]) for i in range(2)]
        s_sq = sb("s_sq", [128, D])
        s_stat = sb("s_stat", [128, 2 * NTT])
        s_mu = sb("s_mu", [128, NTT])
        s_nv = sb("s_nv", [128, NTT])
        s_sc = sb("s_sc", [128, NTT])
        s_bcs = sb("s_bcs", [128, NTT])
        s_z = sb("s_z", [128, D])
        s_a0 = sb("s_a0", [128, D])
        s_sgn = sb("s_sgn", [128, D])
        s_b = sb("s_b", [128, D])
        s_qtm = sb("s_qtm", [128, D], BF16)
        s_qfm = [sb(f"s_qfm{i}", [128, T], BF16) for i in range(NKB)]
        s_t2 = sb("s_t2", [128, D])

        p_b = ps("p_b", [128, D])
        p_c1 = p_b[:, 0:1]
        p_tr = ps("p_tr", [128, 512], BF16)
        p_tr2 = ps("p_tr2", [128, 512], BF16)
        p_trx = [p_tr, p_tr2]
        p_mm = [ps(f"p_mm{i}", [128, D]) for i in range(2)]
        p_r = p_b[0:1, 0:128]

        sems = make_sems(ctx, nc, ("ld", "wld", "v", "s", "t", "g", "p"))
        block = ctx.enter_context(nc.Block())
        P = Prog()

        P.dma("sync", lambda e: e.dma_start(out=s_lmax[:], in_=lmax8), inc="ld")
        P.dma("sync", lambda e: e.dma_start(out=s_grow[:], in_=g_row), inc="ld")
        P.dma("sync", lambda e: e.dma_start(out=s_berow[:], in_=be_row), inc="ld")
        P.dma("sync", lambda e: e.dma_start(out=s_ones1[:], in_=ones1), inc="ld")
        P.dma("sync", lambda e: e.dma_start(out=s_idf[:], in_=identf), inc="ld")
        P.dma("sync", lambda e: e.dma_start(out=s_idb[:], in_=identb), inc="ld")
        ld0 = P.val("ld")

        prev_ev = 0
        for (row, dst) in ((s_grow, s_ga), (s_berow, s_be)):
            for bk in range(2):
                P.op("tensor", lambda e, row=row, bk=bk: e.matmul(
                    out=p_b[:, bk * 512:(bk + 1) * 512], lhsT=s_ones1[0:1, :],
                    rhs=row[0:1, bk * 512:(bk + 1) * 512], start=True, stop=True),
                    waits=[("ld", ld0), ("s", prev_ev)], inc="t")
            prev_ev = P.op("scalar", lambda e, dst=dst: e.activation(
                out=dst[:], in_=p_b[:], func=AF.Identity),
                waits=[("t", P.val("t"))], inc="s")
        s_bcast = P.val("s")

        sgn_done = {}
        for t in range(NKB):
            stg = s_w[t % 2]
            w = []
            if t >= 2:
                w.append(("s", sgn_done[t - 2]))
            P.dma("sync", lambda e, t=t, stg=stg: e.dma_start(
                out=stg[:], in_=wff[t * 128:(t + 1) * 128, :]), waits=w, inc="wld")
            P.op("scalar", lambda e, t=t, stg=stg: e.activation(
                out=s_absw[t][:], in_=stg[:], func=AF.Abs,
                accum_out=s_cols[:, t:t + 1]),
                waits=[("wld", 16 * (t + 1))], inc="s")
            c = P.op("scalar", lambda e, t=t, stg=stg: e.activation(
                out=s_sgnb[t][:], in_=stg[:], func=AF.Sign), inc="s")
            sgn_done[t] = c
        s_wabs = P.val("s")

        def col_to_scalar(col_ap, dst_slot, op):
            ct = P.op("tensor", lambda e: e.matmul(
                out=p_r, lhsT=col_ap, rhs=s_idf[:], start=True, stop=True),
                waits=[("v", P.val("v")), ("s", P.val("s"))], inc="t")
            cs = P.op("scalar", lambda e: e.activation(
                out=s_row[:], in_=p_r, func=AF.Identity),
                waits=[("t", ct)], inc="s")
            return P.op("vector", lambda e: e.tensor_reduce(
                out=s_sca[0:1, dst_slot:dst_slot + 1], in_=s_row[:],
                axis=AX.X, op=op), waits=[("s", cs)], inc="v")

        def bcast_col(src_slot, dst):
            ct = P.op("tensor", lambda e: e.matmul(
                out=p_c1, lhsT=s_ones1[0:1, :],
                rhs=s_sca[0:1, src_slot:src_slot + 1], start=True, stop=True),
                waits=[("v", P.val("v")), ("s", P.val("s"))], inc="t")
            return P.op("scalar", lambda e, dst=dst: e.activation(
                out=dst[:], in_=p_c1[:], func=AF.Identity),
                waits=[("t", ct)], inc="s")

        P.op("vector", lambda e: e.tensor_reduce(
            out=s_red[:, 0:1], in_=s_cols[:, 0:NKB], axis=AX.X, op=ALU.add),
            waits=[("s", s_wabs)], inc="v")
        col_to_scalar(s_red[:, 0:1], 0, ALU.add)
        P.op("vector", lambda e: e.tensor_scalar(
            out=s_sca[0:1, 1:2], in0=s_sca[0:1, 0:1],
            scalar1=0.7 / (D * D), scalar2=None, op0=ALU.mult), inc="v")
        c_dcol = bcast_col(1, s_dcol)

        tr_ev = {}
        ntr = 0
        t_tr_prev = 0
        for t in range(NKB):
            vm = P.op("vector", lambda e, t=t: e.tensor_scalar(
                out=s_mask[:], in0=s_absw[t][:], scalar1=s_dcol[:, 0:1],
                scalar2=None, op0=ALU.is_gt),
                waits=[("s", c_dcol), ("t", t_tr_prev)], inc="v")
            P.op("vector", lambda e, t=t: e.tensor_tensor(
                out=s_scr[:], in0=s_absw[t][:], in1=s_mask[:], op=ALU.mult), inc="v")
            P.op("vector", lambda e, t=t: e.tensor_scalar(
                out=s_scr[:], in0=s_scr[:], scalar1=0.0, scalar2=0.0,
                op0=ALU.add, op1=ALU.add,
                accum_out=s_cols[:, NKB + t:NKB + t + 1]), inc="v")
            P.op("vector", lambda e, t=t: e.tensor_scalar(
                out=s_scr[:], in0=s_mask[:], scalar1=0.0, scalar2=0.0,
                op0=ALU.add, op1=ALU.add,
                accum_out=s_cols[:, 2 * NKB + t:2 * NKB + t + 1]), inc="v")
            vz = P.op("vector", lambda e, t=t: e.tensor_tensor(
                out=s_m0[:], in0=s_sgnb[t][:], in1=s_mask[:], op=ALU.mult), inc="v")
            for k in range(NKB):
                ptr = p_trx[ntr % 2]
                wt = [("v", vz)]
                if ntr >= 2:
                    wt.append(("s", tr_ev[ntr - 2]))
                ct = P.op("tensor", lambda e, k=k, ptr=ptr: e.transpose(
                    out=ptr[:, 0:128], in_=s_m0[:, k * 128:(k + 1) * 128],
                    identity=s_idb[:]), waits=wt, inc="t")
                ce = P.op("scalar", lambda e, k=k, t=t, ptr=ptr: e.activation(
                    out=s_mT[k][:, t * 128:(t + 1) * 128], in_=ptr[:, 0:128],
                    func=AF.Identity), waits=[("t", ct)], inc="s")
                tr_ev[ntr] = ce
                ntr += 1
            t_tr_prev = P.val("t")
        s_mT_done = P.val("s")

        P.op("vector", lambda e: e.tensor_reduce(
            out=s_red[:, 1:2], in_=s_cols[:, NKB:2 * NKB], axis=AX.X, op=ALU.add),
            waits=[("v", P.val("v"))], inc="v")
        P.op("vector", lambda e: e.tensor_reduce(
            out=s_red[:, 2:3], in_=s_cols[:, 2 * NKB:3 * NKB], axis=AX.X,
            op=ALU.add), inc="v")
        col_to_scalar(s_red[:, 1:2], 2, ALU.add)
        col_to_scalar(s_red[:, 2:3], 3, ALU.add)
        P.op("vector", lambda e: e.tensor_scalar(
            out=s_sca[0:1, 3:4], in0=s_sca[0:1, 3:4], scalar1=1.0,
            scalar2=None, op0=ALU.max), inc="v")
        P.op("vector", lambda e: e.reciprocal(
            out=s_sca[0:1, 4:5], in_=s_sca[0:1, 3:4]), inc="v")
        P.op("vector", lambda e: e.tensor_tensor(
            out=s_sca[0:1, 5:6], in0=s_sca[0:1, 2:3], in1=s_sca[0:1, 4:5],
            op=ALU.mult), inc="v")
        P.op("vector", lambda e: e.tensor_reduce(
            out=s_sca[0:1, 6:7], in_=s_lmax[:], axis=AX.X, op=ALU.max),
            waits=[("ld", ld0)], inc="v")
        P.op("vector", lambda e: e.tensor_scalar(
            out=s_sca[0:1, 6:7], in0=s_sca[0:1, 6:7], scalar1=1e-8,
            scalar2=None, op0=ALU.max), inc="v")
        P.op("vector", lambda e: e.reciprocal(
            out=s_sca[0:1, 7:8], in_=s_sca[0:1, 6:7]), inc="v")
        P.op("vector", lambda e: e.tensor_scalar(
            out=s_sca[0:1, 8:9], in0=s_sca[0:1, 7:8], scalar1=127.0,
            scalar2=None, op0=ALU.mult), inc="v")
        P.op("vector", lambda e: e.tensor_tensor(
            out=s_sca[0:1, 9:10], in0=s_sca[0:1, 5:6], in1=s_sca[0:1, 6:7],
            op=ALU.mult), inc="v")
        P.op("vector", lambda e: e.tensor_scalar(
            out=s_sca[0:1, 9:10], in0=s_sca[0:1, 9:10], scalar1=1.0 / 127.0,
            scalar2=None, op0=ALU.mult), inc="v")
        c_qcol = bcast_col(8, s_qcol)
        c_acol = bcast_col(9, s_acol)

        def stats2(xtile, sl, base_wait):
            c_sq = P.op("scalar", lambda e: e.activation(
                out=s_sq[:], in_=xtile, func=AF.Square,
                accum_out=s_stat[:, 2 * sl:2 * sl + 1]), waits=[base_wait], inc="s")
            c_sm = P.op("scalar", lambda e: e.activation(
                out=s_sq[:], in_=xtile, func=AF.Identity,
                accum_out=s_stat[:, 2 * sl + 1:2 * sl + 2]), inc="s")
            P.op("vector", lambda e: e.tensor_scalar(
                out=s_mu[:, sl:sl + 1], in0=s_stat[:, 2 * sl + 1:2 * sl + 2],
                scalar1=1.0 / D, scalar2=None, op0=ALU.mult),
                waits=[("s", c_sm)], inc="v")
            P.op("vector", lambda e: e.tensor_scalar(
                out=s_nv[:, sl:sl + 1], in0=s_stat[:, 2 * sl:2 * sl + 1],
                scalar1=1.0 / D, scalar2=None, op0=ALU.mult), inc="v")
            P.op("vector", lambda e: e.scalar_tensor_tensor(
                out=s_nv[:, sl:sl + 1], in0=s_mu[:, sl:sl + 1],
                scalar=s_mu[:, sl:sl + 1], in1=s_nv[:, sl:sl + 1],
                op0=ALU.mult, op1=ALU.subtract), inc="v")
            P.op("vector", lambda e, sl=sl: e.tensor_scalar(
                out=s_sc[:, sl:sl + 1], in0=s_nv[:, sl:sl + 1],
                scalar1=-(1.0 + EPS), scalar2=EPS * EPS,
                op0=ALU.mult, op1=ALU.add), inc="v")
            c_rt = P.op("scalar", lambda e, sl=sl: e.activation(
                out=s_sc[:, sl:sl + 1], in_=s_sc[:, sl:sl + 1],
                func=AF.Sqrt), waits=[("v", P.val("v"))], inc="s")
            P.op("vector", lambda e, sl=sl: e.reciprocal(
                out=s_sc[:, sl:sl + 1], in_=s_sc[:, sl:sl + 1]),
                waits=[("s", c_rt)], inc="v")
            P.op("vector", lambda e, sl=sl: e.scalar_tensor_tensor(
                out=s_bcs[:, sl:sl + 1], in0=s_mu[:, sl:sl + 1],
                scalar=-1.0, in1=s_sc[:, sl:sl + 1],
                op0=ALU.mult, op1=ALU.mult), inc="v")
            return P.val("v")

        x1_free = {}
        mm_free = {}
        qtr = 0
        qtr_ev = {}
        q_tr_prev = 0
        for i in range(NTT):
            xb = s_x1[i % 2]
            w = []
            if i >= 2:
                w.append(("v", x1_free[i - 2]))
            P.dma("sync", lambda e, i=i, xb=xb: e.dma_start(
                out=xb[:], in_=x1[i * 128:(i + 1) * 128, :]), waits=w, inc="ld")
            v_st = stats2(xb[:], i, ("ld", P.val("ld")))
            c_z = P.op("scalar", lambda e, xb=xb, i=i: e.activation(
                out=s_z[:], in_=xb[:], func=AF.Identity,
                scale=s_sc[:, i:i + 1], bias=s_bcs[:, i:i + 1]),
                waits=[("v", v_st)], inc="s")
            P.op("vector", lambda e: e.tensor_tensor(
                out=s_z[:], in0=s_z[:], in1=s_ga[:], op=ALU.mult),
                waits=[("s", c_z), ("s", s_bcast)], inc="v")
            P.op("vector", lambda e: e.tensor_tensor(
                out=s_z[:], in0=s_z[:], in1=s_be[:], op=ALU.add), inc="v")
            P.op("vector", lambda e: e.tensor_scalar(
                out=s_z[:], in0=s_z[:], scalar1=s_qcol[:, 0:1], scalar2=None,
                op0=ALU.mult), waits=[("s", c_qcol)], inc="v")
            P.op("vector", lambda e: e.tensor_scalar(
                out=s_z[:], in0=s_z[:], scalar1=127.0, scalar2=None,
                op0=ALU.min), inc="v")
            vz = P.op("vector", lambda e: e.tensor_scalar(
                out=s_z[:], in0=s_z[:], scalar1=-127.0, scalar2=None,
                op0=ALU.max), inc="v")
            c_a0 = P.op("scalar", lambda e: e.activation(
                out=s_a0[:], in_=s_z[:], func=AF.Abs),
                waits=[("v", vz), ("t", q_tr_prev)], inc="s")
            c_sg = P.op("scalar", lambda e: e.activation(
                out=s_sgn[:], in_=s_z[:], func=AF.Sign), inc="s")
            P.op("vector", lambda e: e.tensor_copy(
                out=s_z[:], in_=s_a0[:]), waits=[("s", c_sg)], inc="v")
            for kb in range(6, -1, -1):
                P.op("vector", lambda e, kb=kb: e.tensor_scalar(
                    out=s_b[:], in0=s_z[:], scalar1=(2.0 ** kb) - 0.5,
                    scalar2=float(2 ** kb), op0=ALU.is_ge, op1=ALU.mult), inc="v")
                P.op("vector", lambda e: e.tensor_tensor(
                    out=s_z[:], in0=s_z[:], in1=s_b[:], op=ALU.subtract), inc="v")
            P.op("vector", lambda e: e.tensor_tensor(
                out=s_b[:], in0=s_a0[:], in1=s_z[:], op=ALU.subtract), inc="v")
            vq = P.op("vector", lambda e: e.tensor_tensor(
                out=s_qtm[:], in0=s_b[:], in1=s_sgn[:], op=ALU.mult),
                waits=[("t", q_tr_prev)], inc="v")
            for k in range(NKB):
                ptr = p_trx[qtr % 2]
                wt = [("v", vq), ("s", s_mT_done)]
                if qtr >= 2:
                    wt.append(("s", qtr_ev[qtr - 2]))
                ct = P.op("tensor", lambda e, k=k, ptr=ptr: e.transpose(
                    out=ptr[:, 0:128], in_=s_qtm[:, k * 128:(k + 1) * 128],
                    identity=s_idb[:]), waits=wt, inc="t")
                ce = P.op("scalar", lambda e, k=k, i=i, ptr=ptr: e.activation(
                    out=s_qfm[k][:, i * 128:(i + 1) * 128], in_=ptr[:, 0:128],
                    func=AF.Identity), waits=[("t", ct)], inc="s")
                qtr_ev[qtr] = ce
                qtr += 1
            q_tr_prev = P.val("t")
            q_done = P.val("s")
            pb = p_mm[i % 2]
            for k in range(NKB):
                w = [("s", q_done)]
                if k == 0 and i >= 2:
                    w.append(("s", mm_free[i - 2]))
                for bk in range(2):
                    P.op("tensor", lambda e, k=k, i=i, bk=bk, pb=pb: e.matmul(
                        out=pb[:, bk * 512:(bk + 1) * 512],
                        lhsT=s_qfm[k][:, i * 128:(i + 1) * 128],
                        rhs=s_mT[k][:, bk * 512:(bk + 1) * 512],
                        start=(k == 0), stop=(k == NKB - 1)), waits=w, inc="t")
            t_mm = P.val("t")
            wt2 = [("t", t_mm), ("s", c_acol)]
            if i >= 1:
                wt2.append(("g", 16 * i))
            c_t2 = P.op("scalar", lambda e, pb=pb: e.activation(
                out=s_t2[:], in_=pb[:], func=AF.Identity,
                scale=s_acol[:, 0:1]), waits=wt2, inc="s")
            mm_free[i] = c_t2
            vx2 = P.op("vector", lambda e, xb=xb: e.tensor_tensor(
                out=s_t2[:], in0=s_t2[:], in1=xb[:], op=ALU.add),
                waits=[("s", c_t2)], inc="v")
            x1_free[i] = vx2
            P.dma("gpsimd", lambda e, i=i: e.dma_start(
                out=o_x2[i * 128:(i + 1) * 128, :], in_=s_t2[:]),
                waits=[("v", vx2)], inc="g")

        g_final = P.val("g")
        P.op("gpsimd", lambda e: e.engine_nop(), waits=[("g", g_final)])
        P.emit(block, sems)
    return nc


_NC_CACHE = {}


def _get(name, builder):
    if name not in _NC_CACHE:
        _NC_CACHE[name] = builder()
    return _NC_CACHE[name]


def kernel(L0_attn_x, L0_attn_ln_w, L0_attn_ln_b, L0_attn_W_in, L0_attn_b_in,
           L0_attn_conv_w, L0_attn_conv_b, L0_attn_A_log, L0_attn_D,
           L0_attn_W_x, L0_attn_b_x, L0_attn_W_out, L0_attn_b_out,
           L0_ffn_weight, L0_ffn_gamma, L0_ffn_beta):
    f32 = np.float32
    x = np.ascontiguousarray(np.asarray(L0_attn_x, f32))
    w_in = np.ascontiguousarray(np.asarray(L0_attn_W_in, f32))
    b_in = np.asarray(L0_attn_b_in, f32).reshape(1, DG)
    ln_w = np.asarray(L0_attn_ln_w, f32).reshape(D, 1)
    ln_b = np.asarray(L0_attn_ln_b, f32).reshape(D, 1)
    conv_w = np.ascontiguousarray(np.asarray(L0_attn_conv_w, f32).reshape(DI, 4))
    conv_b = np.asarray(L0_attn_conv_b, f32).reshape(DI, 1)
    w_x = np.ascontiguousarray(np.asarray(L0_attn_W_x, f32))
    b_x = np.asarray(L0_attn_b_x, f32).reshape(2 * NST, 1)
    a_log = np.ascontiguousarray(np.asarray(L0_attn_A_log, f32))
    dvec = np.asarray(L0_attn_D, f32).reshape(DI, 1)
    w_out = np.ascontiguousarray(np.asarray(L0_attn_W_out, f32))
    b_out = np.asarray(L0_attn_b_out, f32).reshape(1, D)
    wff = np.ascontiguousarray(np.asarray(L0_ffn_weight, f32))
    g_row = np.asarray(L0_ffn_gamma, f32).reshape(1, D)
    be_row = np.asarray(L0_ffn_beta, f32).reshape(1, D)

    identb = np.eye(128, dtype=BF)
    identf = np.eye(128, dtype=f32)
    ones1 = np.ones((1, 128), f32)
    oh = np.zeros((NH, NH * 128), BF)
    for n in range(NH):
        oh[n, n * 128:(n + 1) * 128] = 1
    zeros_halo = np.zeros((HALO, D), f32)

    cores = list(range(NCORES))

    in1 = []
    for c in cores:
        b, h = c // 2, c % 2
        if h == 0:
            xs = np.concatenate([zeros_halo, x[b, 0:T]], 0)
        else:
            xs = np.ascontiguousarray(x[b, T - HALO:2 * T])
        hm = np.ones((128, 1), f32) if h == 1 else np.zeros((128, 1), f32)
        in1.append(dict(xin=xs, w_in=w_in, b_in=b_in, ln_w=ln_w, ln_b=ln_b,
                        conv_w=conv_w, conv_b=conv_b, w_x=w_x,
                        b_xv=b_x, identb=identb, ones1=ones1, hmask=hm))
    r1 = run_bass_kernel_spmd(_get("l1", build_l1), in1, core_ids=cores).results

    xc_full = [np.concatenate([np.asarray(r1[2 * b]["o_xc"]),
                               np.asarray(r1[2 * b + 1]["o_xc"])], 1)
               for b in range(BATCH)]
    b_full = [np.concatenate([np.asarray(r1[2 * b]["o_b"]),
                              np.asarray(r1[2 * b + 1]["o_b"])], 1)
              for b in range(BATCH)]
    c_full = [np.concatenate([np.asarray(r1[2 * b]["o_c"]),
                              np.asarray(r1[2 * b + 1]["o_c"])], 1)
              for b in range(BATCH)]
    in2 = []
    for c in cores:
        b, nh = c // 2, c % 2
        in2.append(dict(xc=xc_full[b],
                        bh=np.ascontiguousarray(b_full[b][nh * NH:(nh + 1) * NH]),
                        ch=np.ascontiguousarray(c_full[b][nh * NH:(nh + 1) * NH]),
                        alog=np.ascontiguousarray(a_log[:, nh * NH:(nh + 1) * NH]),
                        oh=oh))
    r2 = run_bass_kernel_spmd(_get("l2", build_l2), in2, core_ids=cores).results

    in3 = []
    for c in cores:
        b, h = c // 2, c % 2
        sl = np.s_[:, h * T:(h + 1) * T]
        in3.append(dict(
            xin=np.ascontiguousarray(x[b, h * T:(h + 1) * T]),
            p0=np.ascontiguousarray(np.asarray(r2[2 * b]["o_y"])[sl]),
            p1=np.ascontiguousarray(np.asarray(r2[2 * b + 1]["o_y"])[sl]),
            xc=np.asarray(r1[c]["o_xc"]),
            sg=np.asarray(r1[c]["o_sg"]),
            dvec=dvec, w_out=w_out, b_out=b_out,
            g_row=g_row, be_row=be_row, ones1=ones1, identf=identf))
    r3 = run_bass_kernel_spmd(_get("l3", build_l3), in3, core_ids=cores).results

    lmax8 = np.concatenate([np.asarray(r3[c]["o_lm"]).reshape(1, 1)
                            for c in cores], 1)
    in4 = []
    for c in cores:
        in4.append(dict(x1=np.asarray(r3[c]["o_x1"]), wff=wff, lmax8=lmax8,
                        g_row=g_row, be_row=be_row, ones1=ones1,
                        identf=identf, identb=identb))
    r4 = run_bass_kernel_spmd(_get("l4", build_l4), in4, core_ids=cores).results

    out = np.empty((BATCH, TFULL, D), f32)
    for c in cores:
        b, h = c // 2, c % 2
        out[b, h * T:(h + 1) * T] = np.asarray(r4[c]["o_x2"])
    if os.environ.get("KDEBUG"):
        _DBG.update(r1=r1, r2=r2, r3=r3, r4=r4, lmax8=lmax8)
    return out


_DBG = {}


# version 04
# speedup vs baseline: 1.0879x; 1.0488x over previous
import contextlib
import os
import numpy as np
import ml_dtypes
import concourse.bass as bass
import concourse.mybir as mybir
from concourse.bass_utils import run_bass_kernel_spmd

F32 = mybir.dt.float32
BF16 = mybir.dt.bfloat16
AF = mybir.ActivationFunctionType
ALU = mybir.AluOpType
AX = mybir.AxisListType

NCORES = 8
BATCH = 4
TFULL = 4096
T = 2048
HALO = 8
D = 1024
DI = 2048
DG = 4096
NST = 16
NH = 8
EPS = 1e-5
NPT = DI // 128
NTT = T // 128
BF = ml_dtypes.bfloat16


class Prog:

    ENGINES = ("sync", "vector", "scalar", "tensor", "gpsimd")

    AUTO_SEM = {"vector": "v", "scalar": "s", "gpsimd": "p"}

    def __init__(self):
        self.streams = {e: [] for e in self.ENGINES}
        self.counts = {}
        self.last_op = {}

    def op(self, eng, fn, waits=(), inc=None, inc_by=1):
        w = [x for x in waits if x[1] > 0]
        if eng in self.AUTO_SEM:
            if inc is None:
                inc = self.AUTO_SEM[eng]
            prev = self.last_op.get(eng)
            if prev is not None:
                w.append(prev)
        self._gidx = getattr(self, "_gidx", 0) + 1
        self.streams[eng].append((fn, w, inc, inc_by, self._gidx))
        if inc is not None:
            self.counts[inc] = self.counts.get(inc, 0) + inc_by
            self.last_op[eng] = (inc, self.counts[inc])
        return self.counts.get(inc, 0)

    def dma(self, eng, fn, waits=(), inc=None):
        w = list(waits)
        if inc is not None:
            prev = self.counts.get(inc, 0)
            if prev:
                w.append((inc, prev))
        return self.op(eng, fn, w, inc, 16)

    def val(self, sem):
        return self.counts.get(sem, 0)

    def bar(self, sem):
        return (sem, self.val(sem))

    def emit(self, block, sems):
        import os as _os
        K = int(_os.environ.get("BISECT_K", "10**9".replace("*", "")) or 10**9)
        K = int(_os.environ.get("BISECT_K", str(10**9)))
        for name in self.ENGINES:
            stream = [t for t in self.streams[name] if t[4] <= K]
            if not stream:
                continue

            def make(ops):
                def body(eng):
                    for fn, waits, inc, by, gi in ops:
                        for (s, v) in waits:
                            eng.wait_ge(sems[s], v)
                        instr = fn(eng)
                        if inc is not None:
                            instr.then_inc(sems[inc], by)
                return body

            getattr(block, name)(make(stream))


def make_sems(ctx, nc, names):
    return {k: ctx.enter_context(nc.semaphore(k)) for k in names}


def build_l1():
    nc = bass.Bass("TRN2", target_bir_lowering=False, debug=False,
                   detect_race_conditions=False)
    TH = T + HALO

    xin = nc.dram_tensor("xin", [TH, D], F32, kind="ExternalInput").ap()
    w_in = nc.dram_tensor("w_in", [D, DG], F32, kind="ExternalInput").ap()
    b_in = nc.dram_tensor("b_in", [1, DG], F32, kind="ExternalInput").ap()
    ln_w = nc.dram_tensor("ln_w", [D, 1], F32, kind="ExternalInput").ap()
    ln_b = nc.dram_tensor("ln_b", [D, 1], F32, kind="ExternalInput").ap()
    conv_w = nc.dram_tensor("conv_w", [DI, 4], F32, kind="ExternalInput").ap()
    conv_b = nc.dram_tensor("conv_b", [DI, 1], F32, kind="ExternalInput").ap()
    w_x = nc.dram_tensor("w_x", [DI, 2 * NST], F32, kind="ExternalInput").ap()
    b_xv = nc.dram_tensor("b_xv", [2 * NST, 1], F32, kind="ExternalInput").ap()
    identb = nc.dram_tensor("identb", [128, 128], BF16, kind="ExternalInput").ap()
    ones1 = nc.dram_tensor("ones1", [1, 128], F32, kind="ExternalInput").ap()
    hmask = nc.dram_tensor("hmask", [128, 1], F32, kind="ExternalInput").ap()

    o_xc = nc.dram_tensor("o_xc", [DI, T], BF16, kind="ExternalOutput").ap()
    o_sg = nc.dram_tensor("o_sg", [DI, T], BF16, kind="ExternalOutput").ap()
    o_b = nc.dram_tensor("o_b", [NST, T], F32, kind="ExternalOutput").ap()
    o_c = nc.dram_tensor("o_c", [NST, T], F32, kind="ExternalOutput").ap()

    with contextlib.ExitStack() as ctx:
        sb = lambda n, s, d=F32: ctx.enter_context(nc.sbuf_tensor(n, s, d))
        ps = lambda n, s, d=F32: ctx.enter_context(nc.psum_tensor(n, s, d))

        s_x = [sb(f"s_x{i}", [128, D]) for i in range(2)]
        s_xh = sb("s_xh", [HALO, D])
        s_sq = sb("s_sq", [128, D])
        s_stat = sb("s_stat", [128, 2 * (NTT + 1)])
        s_mu = sb("s_mu", [128, NTT + 1])
        s_nv = sb("s_nv", [128, NTT + 1])
        s_sc = sb("s_sc", [128, NTT + 1])
        s_bcs = sb("s_bcs", [128, NTT + 1])
        s_xn = [sb(f"s_xn{i}", [128, D], BF16) for i in range(2)]
        s_xnh = sb("s_xnh", [HALO, D], BF16)
        s_id = sb("s_id", [128, 128], BF16)
        s_ones1 = sb("s_ones1", [1, 128])
        s_hm = sb("s_hm", [128, 1])
        s_xnT = [sb(f"s_xnT{k}", [128, HALO + T], BF16) for k in range(8)]
        SINGLE_WK = os.environ.get("L1_1WK", "0") == "1"
        s_wk = ([sb("s_wk0", [128, DG])] * 2) if SINGLE_WK else [sb(f"s_wk{k}", [128, DG]) for k in range(2)]
        s_wb = [sb(f"s_wb{k}", [128, DG], BF16) for k in range(8)]
        s_lnw = sb("s_lnw", [128, 8])
        s_lnb = sb("s_lnb", [128, 8])
        s_lnbb = sb("s_lnbb", [128, 8], BF16)
        s_binrow = sb("s_binrow", [1, DG])
        s_bias = sb("s_bias", [128, 32])
        s_cw = sb("s_cw", [128, 4 * NPT])
        s_cb = sb("s_cb", [128, NPT])
        s_xm = sb("s_xm", [128, HALO + T])
        s_xmh = sb("s_xmh", [128, NPT * HALO])
        s_acc = sb("s_acc", [128, T])
        s_xcp = sb("s_xcp", [128, T], BF16)
        s_sgp = [sb(f"s_sgp{i}", [128, T], BF16) for i in range(2)]
        s_wxb = sb("s_wxb", [128, 2 * NST * NPT], BF16)
        s_wxf = sb("s_wxf", [128, 2 * NST * NPT])
        s_bxv = sb("s_bxv", [2 * NST, 1])
        s_bcout = sb("s_bcout", [2 * NST, T])

        p_tr = ps("p_tr", [128, 512], BF16)
        p_mm = [ps(f"p_mm{i}", [128, 512]) for i in range(2)]
        p_misc = ps("p_misc", [128, 512])
        p_bias = p_misc[:, 0:32]
        p_bc = [ps(f"p_bc{i}", [2 * NST, 512]) for i in range(4)]

        sems = make_sems(ctx, nc, ("ld", "wld", "v", "s", "t", "g", "p"))
        block = ctx.enter_context(nc.Block())
        P = Prog()
        TRUNC = int(os.environ.get("L1_TRUNC", "99"))

        NC_MASK = int(os.environ.get("L1_NC", "31"))
        if NC_MASK & 1:
            P.dma("sync", lambda e: e.dma_start(out=s_id[:], in_=identb), inc="ld")
            P.dma("sync", lambda e: e.dma_start(out=s_ones1[:], in_=ones1), inc="ld")
            P.dma("sync", lambda e: e.dma_start(out=s_hm[:], in_=hmask), inc="ld")
        if NC_MASK & 2:
            for k in range(8):
                P.dma("sync", lambda e, k=k: e.dma_start(
                    out=s_lnw[:, k:k + 1], in_=ln_w[k * 128:(k + 1) * 128, :]), inc="ld")
                P.dma("sync", lambda e, k=k: e.dma_start(
                    out=s_lnb[:, k:k + 1], in_=ln_b[k * 128:(k + 1) * 128, :]), inc="ld")
        if NC_MASK & 4:
            P.dma("sync", lambda e: e.dma_start(out=s_binrow[:], in_=b_in), inc="ld")
        if NC_MASK & 8:
            for pt in range(NPT):
                P.dma("sync", lambda e, pt=pt: e.dma_start(
                    out=s_cw[:, 4 * pt:4 * pt + 4],
                    in_=conv_w[pt * 128:(pt + 1) * 128, :]), inc="ld")
                P.dma("sync", lambda e, pt=pt: e.dma_start(
                    out=s_cb[:, pt:pt + 1],
                    in_=conv_b[pt * 128:(pt + 1) * 128, :]), inc="ld")
                P.dma("sync", lambda e, pt=pt: e.dma_start(
                    out=s_wxf[:, 32 * pt:32 * pt + 32],
                    in_=w_x[pt * 128:(pt + 1) * 128, :]), inc="ld")
        if NC_MASK & 16:
            P.dma("sync", lambda e: e.dma_start(out=s_bxv[:], in_=b_xv), inc="ld")
            P.dma("sync", lambda e: e.dma_start(out=s_xh[:], in_=xin[0:HALO, :]), inc="ld")
        ld_const = P.val("ld")

        if os.environ.get("L1_NOWXB"):
            v_wx = 0
        else:
            P.op("vector", lambda e: e.tensor_copy(out=s_wxb[:], in_=s_wxf[:]),
                 waits=[("ld", ld_const)], inc="v")
            v_wx = P.val("v")

        if TRUNC < 2:
            P.dma("gpsimd", lambda e: e.dma_start(out=o_b, in_=s_stat[0:16, 0:T] if False else s_sc[0:16, 0:1].broadcast_to([16, T])), inc="g") if False else None
            P.emit(block, sems)
            return nc
        v_lnbb = 0
        cvt_done = []
        for k in (range(0) if os.environ.get("L1_NOW") else range(8)):
            stage = s_wk[k % 2]
            w = [("ld", ld_const)]
            kgap = 1 if SINGLE_WK else 2
            if k >= kgap:
                w.append(("v", cvt_done[k - kgap]))
            P.dma("sync", lambda e, k=k, st_=stage: e.dma_start(
                out=st_[:], in_=w_in[k * 128:(k + 1) * 128, :]), waits=w, inc="wld")
            P.op("vector", lambda e, k=k, st_=stage: e.tensor_scalar(
                out=s_wb[k][:], in0=st_[:], scalar1=s_lnw[:, k:k + 1], scalar2=None,
                op0=ALU.mult),
                waits=[("wld", 16 * (k + 1))], inc="v")
            for j in (range(0) if os.environ.get("L1_NOBROW") else range(8)):
                wj = [("wld", 16 * (k + 1))]
                if k == 0 and j == 0:
                    pass
                P.op("tensor", lambda e, k=k, j=j, st_=stage: e.matmul(
                    out=p_misc[0:1, :], lhsT=s_lnb[:, k:k + 1],
                    rhs=st_[:, j * 512:(j + 1) * 512],
                    start=True, stop=True), waits=wj + [("v", P.val("v"))], inc="t")
                P.op("vector", lambda e, j=j: e.tensor_tensor(
                    out=s_binrow[0:1, j * 512:(j + 1) * 512],
                    in0=s_binrow[0:1, j * 512:(j + 1) * 512],
                    in1=p_misc[0:1, :], op=ALU.add),
                    waits=[("t", P.val("t")), ("ld", ld_const)], inc="v")
            cvt_done.append(P.val("v"))
        v_wcvt = P.val("v")
        v_brow = v_wcvt
        if os.environ.get("L1_NOW"):
            cvt_done = [0] * 8

        for m in (range(0) if os.environ.get("L1_NOBIAS") else range(32)):
            P.op("tensor", lambda e, m=m: e.matmul(
                out=p_bias[:, m:m + 1], lhsT=s_binrow[0:1, m * 128:(m + 1) * 128],
                rhs=s_ones1[0:1, 0:1], start=True, stop=True),
                waits=[("v", v_brow)], inc="t")
        t_bias = P.val("t")
        if os.environ.get("L1_NOBIAS"):
            s_bias_done = 0
        else:
            s_bias_done = P.op("scalar", lambda e: e.activation(
                out=s_bias[:], in_=p_bias[:], func=AF.Identity),
                waits=[("t", t_bias)], inc="s")

        if TRUNC < 3:
            P.emit(block, sems)
            return nc
        L1_HALO = int(os.environ.get("L1_HALO", "1"))
        L1_TILES = int(os.environ.get("L1_TILES", str(NTT)))
        tile_tr_done = {}
        tile_evac_done = {}
        n_tr = 0

        def stats_ops(tile_idx, xtile, prows, ld_wait):
            si = tile_idx
            c_sq = P.op("scalar", lambda e: e.activation(
                out=s_sq[0:prows, :], in_=xtile, func=AF.Square,
                accum_out=s_stat[0:prows, 2 * si:2 * si + 1]),
                waits=[ld_wait], inc="s")
            c_sm = P.op("scalar", lambda e: e.activation(
                out=s_sq[0:prows, :], in_=xtile, func=AF.Identity,
                accum_out=s_stat[0:prows, 2 * si + 1:2 * si + 2]), inc="s")
            P.op("vector", lambda e: e.tensor_scalar(
                out=s_mu[0:prows, si:si + 1],
                in0=s_stat[0:prows, 2 * si + 1:2 * si + 2],
                scalar1=1.0 / D, scalar2=None, op0=ALU.mult),
                waits=[("s", c_sm)], inc="v")
            P.op("vector", lambda e: e.tensor_scalar(
                out=s_nv[0:prows, si:si + 1],
                in0=s_stat[0:prows, 2 * si:2 * si + 1],
                scalar1=1.0 / D, scalar2=None, op0=ALU.mult), inc="v")
            P.op("vector", lambda e: e.scalar_tensor_tensor(
                out=s_nv[0:prows, si:si + 1], in0=s_mu[0:prows, si:si + 1],
                scalar=s_mu[0:prows, si:si + 1], in1=s_nv[0:prows, si:si + 1],
                op0=ALU.mult, op1=ALU.subtract), inc="v")
            v_nv = P.val("v")
            P.op("vector", lambda e: e.tensor_scalar(
                out=s_sc[0:prows, si:si + 1], in0=s_nv[0:prows, si:si + 1],
                scalar1=-(1.0 + EPS), scalar2=EPS * EPS,
                op0=ALU.mult, op1=ALU.add), inc="v")
            v_nv2 = P.val("v")
            c_rt = P.op("scalar", lambda e: e.activation(
                out=s_sc[0:prows, si:si + 1], in_=s_sc[0:prows, si:si + 1],
                func=AF.Sqrt), waits=[("v", v_nv2)], inc="s")
            P.op("vector", lambda e: e.reciprocal(
                out=s_sc[0:prows, si:si + 1], in_=s_sc[0:prows, si:si + 1]),
                waits=[("s", c_rt)], inc="v")
            P.op("vector", lambda e: e.scalar_tensor_tensor(
                out=s_bcs[0:prows, si:si + 1], in0=s_mu[0:prows, si:si + 1],
                scalar=-1.0, in1=s_sc[0:prows, si:si + 1],
                op0=ALU.mult, op1=ALU.mult), inc="v")
            return P.val("v")

        if not L1_HALO:
            v_h = 0
        v_h = stats_ops(NTT, s_xh[:], HALO, ("ld", ld_const)) if L1_HALO else 0
        if L1_HALO:
            c_xnh = P.op("scalar", lambda e: e.activation(
                out=s_xnh[:], in_=s_xh[:], func=AF.Identity,
                scale=s_sc[0:HALO, NTT:NTT + 1], bias=s_bcs[0:HALO, NTT:NTT + 1]),
                waits=[("v", v_h)], inc="s")
            for k in range(8):
                n_tr += 1
                P.op("tensor", lambda e, k=k: e.transpose(
                    out=p_tr[:, 0:HALO], in_=s_xnh[:, k * 128:(k + 1) * 128],
                    identity=s_id[0:HALO, 0:HALO]),
                    waits=[("s", c_xnh), ("s", P.val("s"))], inc="t")
                P.op("scalar", lambda e, k=k: e.activation(
                    out=s_xnT[k][:, 0:HALO], in_=p_tr[:, 0:HALO], func=AF.Identity),
                    waits=[("t", P.val("t"))], inc="s")

        for i in range(L1_TILES):
            xb = s_x[i % 2]
            w = [("s", 0)]
            if i >= 2:
                w = [("s", tile_evac_done[i - 2])]
            P.dma("sync", lambda e, i=i, xb=xb: e.dma_start(
                out=xb[:], in_=xin[HALO + i * 128:HALO + (i + 1) * 128, :]),
                waits=w, inc="ld")
            L1_PARTS = int(os.environ.get("L1_PARTS", "9"))
            if L1_PARTS < 1:
                continue
            v_i = stats_ops(i, xb[:], 128, ("ld", P.val("ld")))
            if L1_PARTS < 2:
                continue
            xnb = s_xn[i % 2]
            wx = [("v", v_i)]
            if i >= 2:
                wx.append(("t", tile_tr_done[i - 2]))
            XNMODE = int(os.environ.get("L1_XNMODE", "0"))
            if XNMODE == 1:
                c_xn = P.op("scalar", lambda e, i=i, xb=xb: e.activation(
                    out=s_sq[:], in_=xb[:], func=AF.Identity,
                    scale=s_sc[:, i:i + 1], bias=s_bcs[:, i:i + 1]),
                    waits=wx, inc="s")
            elif XNMODE == 2:
                c_xn = P.op("scalar", lambda e, i=i, xb=xb, xnb=xnb: e.activation(
                    out=xnb[:], in_=xb[:], func=AF.Identity),
                    waits=wx, inc="s")
            elif XNMODE == 3:
                P.op("vector", lambda e, i=i, xb=xb: e.scalar_tensor_tensor(
                    out=s_sq[:], in0=xb[:], scalar=s_sc[:, i:i + 1],
                    in1=s_bcs[:, i:i + 1].broadcast_to([128, D]),
                    op0=ALU.mult, op1=ALU.add), waits=wx)
                c_xn = P.op("vector", lambda e, xnb=xnb: e.tensor_copy(
                    out=xnb[:], in_=s_sq[:]))
            else:
                c_xn = P.op("scalar", lambda e, i=i, xb=xb, xnb=xnb: e.activation(
                    out=xnb[:], in_=xb[:], func=AF.Identity,
                    scale=s_sc[:, i:i + 1], bias=s_bcs[:, i:i + 1]),
                    waits=wx, inc="s")
            for k in range(8):
                sl = 0
                wt = [("s", c_xn)]
                if n_tr >= 1:
                    wt.append(("s", tile_evac_done.get(("tr", n_tr - 1), 0)))
                n_tr += 1
                ct = P.op("tensor", lambda e, k=k, xnb=xnb, sl=sl: e.transpose(
                    out=p_tr[:, sl:sl + 128], in_=xnb[:, k * 128:(k + 1) * 128],
                    identity=s_id[:]), waits=wt, inc="t")
                ce = P.op("scalar", lambda e, k=k, i=i, sl=sl: e.activation(
                    out=s_xnT[k][:, HALO + i * 128:HALO + (i + 1) * 128],
                    in_=p_tr[:, sl:sl + 128], func=AF.Identity),
                    waits=[("t", ct)], inc="s")
                tile_evac_done[("tr", n_tr - 1)] = ce
            tile_tr_done[i] = P.val("t")
            tile_evac_done[i] = P.val("s")

        v_phaseA = P.val("v")
        s_phaseA = P.val("s")
        t_phaseA = P.val("t")

        mm_group = 0
        evac_group_done = []
        mm_group_t = []

        def mm_tile(m, c0, c1c, dest_fn, act_func, bias_col, extra_waits=()):
            nonlocal mm_group
            g = mm_group
            mm_group += 1
            pb = p_mm[g % 2]
            n = c1c - c0
            for k in range(8):
                w = [("s", s_phaseA), ("v", v_wcvt)]
                if k == 0 and g >= 2:
                    w.append(("s", evac_group_done[g - 2]))
                P.op("tensor", lambda e, k=k, m=m, pb=pb, c0=c0, c1c=c1c, n=n: e.matmul(
                    out=pb[:, 0:n], lhsT=s_wb[k][:, m * 128:(m + 1) * 128],
                    rhs=s_xnT[k][:, c0:c1c],
                    start=(k == 0), stop=(k == 7)), waits=w, inc="t")
            tmm = P.val("t")
            mm_group_t.append(tmm)
            ce = P.op("scalar", lambda e, pb=pb, n=n: e.activation(
                out=dest_fn(), in_=pb[:, 0:n], func=act_func,
                bias=bias_col), waits=[("t", tmm), ("s", s_bias_done),
                                       *extra_waits], inc="s")
            evac_group_done.append(ce)
            return ce

        for pt in range(NPT):
            mm_tile(pt, 0, HALO,
                    (lambda pt=pt: s_xmh[:, pt * HALO:(pt + 1) * HALO]),
                    AF.Identity, s_bias[:, pt:pt + 1])
        vmaskall = P.op("vector", lambda e: e.tensor_scalar(
            out=s_xmh[:], in0=s_xmh[:], scalar1=s_hm[:, 0:1], scalar2=None,
            op0=ALU.mult),
            waits=[("s", P.val("s")), ("ld", ld_const)], inc="v")

        if TRUNC < 4:
            P.emit(block, sems)
            return nc
        sg_spill_done = {}
        for mi, m in enumerate(range(16, 32)):
            pt = m - 16
            buf = s_sgp[pt % 2]
            ew = []
            if pt >= 2:
                ew.append(("g", sg_spill_done[pt - 2]))
            last = None
            for tb in range(4):
                last = mm_tile(m, HALO + tb * 512, HALO + (tb + 1) * 512,
                               (lambda buf=buf, tb=tb: buf[:, tb * 512:(tb + 1) * 512]),
                               AF.Silu, s_bias[:, m:m + 1], ew)
            cg = P.dma("gpsimd", lambda e, pt=pt, buf=buf: e.dma_start(
                out=o_sg[pt * 128:(pt + 1) * 128, :], in_=buf[:]),
                waits=[("s", last)], inc="g")
            sg_spill_done[pt] = cg

        if TRUNC < 5:
            P.emit(block, sems)
            return nc
        xc_spill_done = {}
        v_conv_done = {}
        wx_done = {}
        for pt in range(NPT):
            m = pt
            last = None
            for tb in range(4):
                ew = []
                if pt >= 1:
                    ew.append(("v", v_conv_done[pt - 1]))
                last = mm_tile(m, HALO + tb * 512, HALO + (tb + 1) * 512,
                               (lambda tb=tb: s_xm[:, HALO + tb * 512:HALO + (tb + 1) * 512]),
                               AF.Identity, s_bias[:, m:m + 1], ew)
            cmask = P.op("vector", lambda e, pt=pt: e.tensor_copy(
                out=s_xm[:, 0:HALO], in_=s_xmh[:, pt * HALO:(pt + 1) * HALO]),
                waits=[("v", vmaskall)] + ([("v", v_conv_done[pt - 1])] if pt >= 1 else []),
                inc="v")
            cw = s_cw
            c1 = P.op("vector", lambda e, pt=pt: e.scalar_tensor_tensor(
                out=s_acc[:], in0=s_xm[:, HALO:HALO + T],
                scalar=cw[:, 4 * pt + 3:4 * pt + 4],
                in1=s_cb[:, pt:pt + 1].broadcast_to([128, T]),
                op0=ALU.mult, op1=ALU.add),
                waits=[("s", last), ("v", cmask)], inc="v")
            for kk in range(3):
                c1 = P.op("vector", lambda e, pt=pt, kk=kk: e.scalar_tensor_tensor(
                    out=s_acc[:], in0=s_xm[:, 5 + kk:5 + kk + T],
                    scalar=cw[:, 4 * pt + kk:4 * pt + kk + 1],
                    in1=s_acc[:], op0=ALU.mult, op1=ALU.add), inc="v")
            v_conv_done[pt] = c1
            ws = [("v", c1)]
            if pt >= 1:
                ws.append(("g", xc_spill_done[pt - 1]))
                ws.append(("t", wx_done[pt - 1]))
            csil = P.op("scalar", lambda e: e.activation(
                out=s_xcp[:], in_=s_acc[:], func=AF.Silu), waits=ws, inc="s")
            cspill = P.dma("gpsimd", lambda e, pt=pt: e.dma_start(
                out=o_xc[pt * 128:(pt + 1) * 128, :], in_=s_xcp[:]),
                waits=[("s", csil)], inc="g")
            xc_spill_done[pt] = cspill
            for tb in (range(0) if os.environ.get("L1_NOWX") else range(4)):
                P.op("tensor", lambda e, pt=pt, tb=tb: e.matmul(
                    out=p_bc[tb][:, :],
                    lhsT=s_wxb[:, 32 * pt:32 * pt + 32],
                    rhs=s_xcp[:, tb * 512:(tb + 1) * 512],
                    start=(pt == 0), stop=(pt == NPT - 1)),
                    waits=[("s", csil), ("v", v_wx)], inc="t")
            wx_done[pt] = P.val("t")

        t_all = P.val("t")
        ev = 0
        for tb in (range(0) if os.environ.get("L1_NOWX") else range(4)):
            ev = P.op("scalar", lambda e, tb=tb: e.activation(
                out=s_bcout[:, tb * 512:(tb + 1) * 512], in_=p_bc[tb][:, :],
                func=AF.Identity, bias=s_bxv[:, 0:1]),
                waits=[("t", t_all), ("ld", ld_const)], inc="s")
        if os.environ.get("L1_NOWX"):
            ev = P.val("s")
        P.dma("gpsimd", lambda e: e.dma_start(out=o_b, in_=s_bcout[0:16, :]),
              waits=[("s", ev)], inc="g")
        P.dma("gpsimd", lambda e: e.dma_start(out=o_c, in_=s_bcout[16:32, :]),
              inc="g")
        g_final = P.val("g")
        P.op("gpsimd", lambda e: e.engine_nop(), waits=[("g", g_final)])

        P.emit(block, sems)
    return nc




def build_l2():
    nc = bass.Bass("TRN2", target_bir_lowering=False, debug=False,
                   detect_race_conditions=False)
    TH2 = TFULL // 2

    xc = nc.dram_tensor("xc", [DI, TFULL], BF16, kind="ExternalInput").ap()
    bh = nc.dram_tensor("bh", [NH, TFULL], F32, kind="ExternalInput").ap()
    ch = nc.dram_tensor("ch", [NH, TFULL], F32, kind="ExternalInput").ap()
    alog = nc.dram_tensor("alog", [DI, NH], F32, kind="ExternalInput").ap()
    oh = nc.dram_tensor("oh", [NH, NH * 128], BF16, kind="ExternalInput").ap()

    o_y = nc.dram_tensor("o_y", [DI, TFULL], BF16, kind="ExternalOutput").ap()

    with contextlib.ExitStack() as ctx:
        sb = lambda n, s, d=F32: ctx.enter_context(nc.sbuf_tensor(n, s, d))
        ps = lambda n, s, d=F32: ctx.enter_context(nc.psum_tensor(n, s, d))

        s_bh = sb("s_bh", [NH, TFULL])
        s_ch = sb("s_ch", [NH, TFULL])
        s_bhb = sb("s_bhb", [NH, TFULL], BF16)
        s_chb = sb("s_chb", [NH, TFULL], BF16)
        s_oh = sb("s_oh", [NH, NH * 128], BF16)
        s_alog = sb("s_alog", [128, NPT * NH])
        s_A = sb("s_A", [128, NPT * NH])
        s_bb = sb("s_bb", [128, NH * TH2], BF16)
        s_cc = sb("s_cc", [128, NH * TH2], BF16)
        s_xcp = [sb(f"s_xcp{i}", [128, TH2], BF16) for i in range(2)]
        s_u = sb("s_u", [128, TH2], BF16)
        s_ha = sb("s_ha", [128, TH2], BF16)
        s_g = [sb(f"s_g{i}", [128, TH2], BF16) for i in range(2)]
        s_y = sb("s_y", [128, TH2])
        s_yb = sb("s_yb", [128, TH2], BF16)
        s_hc = sb("s_hc", [128, NPT * NH])

        p_dup = [ps(f"p_dup{i}", [128, 512]) for i in range(2)]

        sems = make_sems(ctx, nc, ("ld", "xld", "v", "s", "t", "g", "p"))
        block = ctx.enter_context(nc.Block())
        P = Prog()

        P.dma("sync", lambda e: e.dma_start(out=s_bh[:], in_=bh), inc="ld")
        P.dma("sync", lambda e: e.dma_start(out=s_ch[:], in_=ch), inc="ld")
        P.dma("sync", lambda e: e.dma_start(out=s_oh[:], in_=oh), inc="ld")
        for pt in range(NPT):
            P.dma("sync", lambda e, pt=pt: e.dma_start(
                out=s_alog[:, pt * NH:(pt + 1) * NH],
                in_=alog[pt * 128:(pt + 1) * 128, :]), inc="ld")
        ld0 = P.val("ld")
        P.op("scalar", lambda e: e.activation(
            out=s_A[:], in_=s_alog[:], func=AF.Exp), waits=[("ld", ld0)], inc="s")
        sA = P.op("scalar", lambda e: e.activation(
            out=s_A[:], in_=s_A[:], func=AF.Exp, scale=-1.0), inc="s")
        P.op("vector", lambda e: e.tensor_copy(out=s_bhb[:], in_=s_bh[:]),
             waits=[("ld", ld0)], inc="v")
        vBC = P.op("vector", lambda e: e.tensor_copy(out=s_chb[:], in_=s_ch[:]),
                   inc="v")

        dup_evac = {}
        gdup = 0
        pool_done = {}
        gctr = 0
        vy_prev = 0
        dma_prev = 0
        xld_for = {}
        u_last = {}
        P.dma("sync", lambda e: e.dma_start(
            out=s_xcp[0][:], in_=xc[0:128, 0:TH2]), inc="xld")
        xld_for[0] = P.val("xld")

        for half in range(2):
            t0 = half * TH2
            v_before_dups = P.val("v")
            for n in range(NH):
                for which, src, dst in (("b", s_bhb, s_bb), ("c", s_chb, s_cc)):
                    for bk in range(4):
                        pd = p_dup[gdup % 2]
                        w = [("v", vBC)]
                        if gdup >= 2:
                            w.append(("s", dup_evac[gdup - 2]))
                        P.op("tensor", lambda e, n=n, bk=bk, pd=pd, t0=t0, src=src: e.matmul(
                            out=pd[:], lhsT=s_oh[:, n * 128:(n + 1) * 128],
                            rhs=src[:, t0 + bk * 512:t0 + (bk + 1) * 512],
                            start=True, stop=True), waits=w, inc="t")
                        we = [("t", P.val("t"))]
                        if half == 1:
                            we.append(("v", v_before_dups))
                        c1 = P.op("scalar", lambda e, n=n, bk=bk, pd=pd, dst=dst: e.activation(
                            out=dst[:, n * TH2 + bk * 512:n * TH2 + (bk + 1) * 512],
                            in_=pd[:], func=AF.Identity), waits=we, inc="s")
                        dup_evac[gdup] = c1
                        gdup += 1
            dup_done = P.val("s")

            for pt in range(NPT):
                q = half * NPT + pt
                xb_cp = s_xcp[q % 2]
                xld = xld_for[q]
                if q + 1 < 2 * NPT:
                    h2, p2 = divmod(q + 1, NPT)
                    wpre = [("v", u_last[q - 1])] if q >= 1 else []
                    P.dma("sync", lambda e, p2=p2, h2=h2, q=q: e.dma_start(
                        out=s_xcp[(q + 1) % 2][:],
                        in_=xc[p2 * 128:(p2 + 1) * 128,
                               h2 * TH2:(h2 + 1) * TH2]),
                        waits=wpre, inc="xld")
                    xld_for[q + 1] = P.val("xld")
                for n in range(NH):
                    ci = pt * NH + n
                    wu = [("xld", xld), ("s", dup_done)]
                    P.op("vector", lambda e, n=n, xb_cp=xb_cp: e.tensor_tensor(
                        out=s_u[:], in0=xb_cp[:],
                        in1=s_bb[:, n * TH2:(n + 1) * TH2], op=ALU.mult),
                        waits=wu, inc="v")
                    if n == NH - 1:
                        u_last[q] = P.val("v")
                    if half == 0:
                        init = 0.0
                    else:
                        init = s_hc[:, ci:ci + 1]
                    P.op("vector", lambda e, ci=ci, init=init: e.tensor_tensor_scan(
                        out=s_ha[:], data0=s_A[:, ci:ci + 1].broadcast_to([128, TH2]),
                        data1=s_u[:], initial=init,
                        op0=ALU.mult, op1=ALU.add),
                        waits=[("s", sA)], inc="v")
                    if half == 0:
                        P.op("vector", lambda e, ci=ci: e.tensor_copy(
                            out=s_hc[:, ci:ci + 1], in_=s_ha[:, TH2 - 1:TH2]), inc="v")
                    gb = s_g[gctr % 2]
                    wg = []
                    if gctr >= 2:
                        wg.append(("p", pool_done[gctr - 2]))
                    vg = P.op("vector", lambda e, n=n, gb=gb: e.tensor_tensor(
                        out=gb[:], in0=s_ha[:],
                        in1=s_cc[:, n * TH2:(n + 1) * TH2], op=ALU.mult),
                        waits=wg, inc="v")
                    wp = [("v", vg)]
                    if n == 0:
                        wp.append(("v", vy_prev))
                    if n == 0:
                        pc = P.op("gpsimd", lambda e, gb=gb: e.tensor_copy(
                            out=s_y[:], in_=gb[:]), waits=wp, inc="p")
                    else:
                        pc = P.op("gpsimd", lambda e, gb=gb: e.tensor_tensor(
                            out=s_y[:], in0=s_y[:], in1=gb[:], op=ALU.add),
                            waits=wp, inc="p")
                    pool_done[gctr] = pc
                    gctr += 1
                wy = [("p", pool_done[gctr - 1])]
                if dma_prev:
                    wy.append(("g", dma_prev))
                vy_prev = P.op("vector", lambda e: e.tensor_copy(
                    out=s_yb[:], in_=s_y[:]), waits=wy, inc="v")
                dma_prev = P.dma("gpsimd", lambda e, pt=pt, t0=t0: e.dma_start(
                    out=o_y[pt * 128:(pt + 1) * 128, t0:t0 + TH2], in_=s_yb[:]),
                    waits=[("v", vy_prev)], inc="g")

        g_final = P.val("g")
        P.op("gpsimd", lambda e: e.engine_nop(), waits=[("g", g_final)])
        P.emit(block, sems)
    return nc


def build_l3():
    nc = bass.Bass("TRN2", target_bir_lowering=False, debug=False,
                   detect_race_conditions=False)

    xin = nc.dram_tensor("xin", [T, D], F32, kind="ExternalInput").ap()
    p0 = nc.dram_tensor("p0", [DI, T], BF16, kind="ExternalInput").ap()
    p1 = nc.dram_tensor("p1", [DI, T], BF16, kind="ExternalInput").ap()
    xc = nc.dram_tensor("xc", [DI, T], BF16, kind="ExternalInput").ap()
    sg = nc.dram_tensor("sg", [DI, T], BF16, kind="ExternalInput").ap()
    dvec = nc.dram_tensor("dvec", [DI, 1], F32, kind="ExternalInput").ap()
    w_out = nc.dram_tensor("w_out", [DI, D], F32, kind="ExternalInput").ap()
    b_out = nc.dram_tensor("b_out", [1, D], F32, kind="ExternalInput").ap()
    g_row = nc.dram_tensor("g_row", [1, D], F32, kind="ExternalInput").ap()
    be_row = nc.dram_tensor("be_row", [1, D], F32, kind="ExternalInput").ap()
    ones1 = nc.dram_tensor("ones1", [1, 128], F32, kind="ExternalInput").ap()
    identf = nc.dram_tensor("identf", [128, 128], F32, kind="ExternalInput").ap()

    o_x1 = nc.dram_tensor("o_x1", [T, D], F32, kind="ExternalOutput").ap()
    o_lm = nc.dram_tensor("o_lm", [1, 1], F32, kind="ExternalOutput").ap()

    with contextlib.ExitStack() as ctx:
        sb = lambda n, s, d=F32: ctx.enter_context(nc.sbuf_tensor(n, s, d))
        ps = lambda n, s, d=F32: ctx.enter_context(nc.psum_tensor(n, s, d))

        s_p0 = sb("s_p0", [128, T], BF16)
        s_p1 = sb("s_p1", [128, T], BF16)
        s_xcp = sb("s_xcp", [128, T], BF16)
        s_sgp = sb("s_sgp", [128, T], BF16)
        s_d = sb("s_d", [128, NPT])
        s_y = sb("s_y", [128, T])
        s_yg = [sb(f"s_yg{i}", [128, T], BF16) for i in range(NPT)]
        s_wo = [sb(f"s_wo{i}", [128, D]) for i in range(2)]
        s_wob = [sb(f"s_wob{i}", [128, D], BF16) for i in range(NPT)]
        s_borow = sb("s_borow", [1, D])
        s_grow = sb("s_grow", [1, D])
        s_berow = sb("s_berow", [1, D])
        s_bo = sb("s_bo", [128, D])
        s_ga = sb("s_ga", [128, D])
        s_be = sb("s_be", [128, D])
        s_ones1 = sb("s_ones1", [1, 128])
        s_idf = sb("s_idf", [128, 128])
        s_x = [sb(f"s_x{i}", [128, D]) for i in range(2)]
        s_sq = sb("s_sq", [128, D])
        s_ln1 = sb("s_ln1", [128, D])
        s_x1 = [sb(f"s_x1{i}", [128, D]) for i in range(2)]
        s_z = sb("s_z", [128, D])
        s_stat = sb("s_stat", [128, 4 * NTT])
        s_mu = sb("s_mu", [128, 2 * NTT])
        s_nv = sb("s_nv", [128, 2 * NTT])
        s_sc = sb("s_sc", [128, 2 * NTT])
        s_bcs = sb("s_bcs", [128, 2 * NTT])
        s_mx = sb("s_mx", [128, 2])
        s_mrow = sb("s_mrow", [1, 128])
        s_lm = sb("s_lm", [1, 1])

        p_mm = [ps(f"p_mm{i}", [128, D]) for i in range(2)]
        p_b = ps("p_b", [128, D])
        p_r = ps("p_r", [1, 128])

        sems = make_sems(ctx, nc, ("ld", "wld", "xt", "v", "s", "t", "g", "p"))
        block = ctx.enter_context(nc.Block())
        P = Prog()

        P.dma("sync", lambda e: e.dma_start(out=s_borow[:], in_=b_out), inc="ld")
        P.dma("sync", lambda e: e.dma_start(out=s_grow[:], in_=g_row), inc="ld")
        P.dma("sync", lambda e: e.dma_start(out=s_berow[:], in_=be_row), inc="ld")
        P.dma("sync", lambda e: e.dma_start(out=s_ones1[:], in_=ones1), inc="ld")
        P.dma("sync", lambda e: e.dma_start(out=s_idf[:], in_=identf), inc="ld")
        for pt in range(NPT):
            P.dma("sync", lambda e, pt=pt: e.dma_start(
                out=s_d[:, pt:pt + 1], in_=dvec[pt * 128:(pt + 1) * 128, :]), inc="ld")
        ld0 = P.val("ld")

        prev_ev = 0
        for (row, dst) in ((s_borow, s_bo), (s_grow, s_ga), (s_berow, s_be)):
            for bk in range(2):
                P.op("tensor", lambda e, row=row, bk=bk: e.matmul(
                    out=p_b[:, bk * 512:(bk + 1) * 512], lhsT=s_ones1[0:1, :],
                    rhs=row[0:1, bk * 512:(bk + 1) * 512], start=True, stop=True),
                    waits=[("ld", ld0), ("s", prev_ev)], inc="t")
            prev_ev = P.op("scalar", lambda e, dst=dst: e.activation(
                out=dst[:], in_=p_b[:], func=AF.Identity),
                waits=[("t", P.val("t"))], inc="s")
        s_bcast = P.val("s")

        cvt = []
        for k in range(NPT):
            stg = s_wo[k % 2]
            w = []
            if k >= 2:
                w.append(("v", cvt[k - 2]))
            P.dma("sync", lambda e, k=k, stg=stg: e.dma_start(
                out=stg[:], in_=w_out[k * 128:(k + 1) * 128, :]), waits=w, inc="wld")
            c = P.op("vector", lambda e, k=k, stg=stg: e.tensor_copy(
                out=s_wob[k][:], in_=stg[:]), waits=[("wld", 16 * (k + 1))], inc="v")
            cvt.append(c)
        v_wcvt = P.val("v")

        yg_done = []
        for pt in range(NPT):
            w = []
            if pt >= 1:
                w.append(("v", yg_done[pt - 1]))
            P.dma("sync", lambda e, pt=pt: e.dma_start(
                out=s_p0[:], in_=p0[pt * 128:(pt + 1) * 128, :]), waits=w, inc="ld")
            P.dma("sync", lambda e, pt=pt: e.dma_start(
                out=s_p1[:], in_=p1[pt * 128:(pt + 1) * 128, :]), inc="ld")
            P.dma("sync", lambda e, pt=pt: e.dma_start(
                out=s_xcp[:], in_=xc[pt * 128:(pt + 1) * 128, :]), inc="ld")
            P.dma("sync", lambda e, pt=pt: e.dma_start(
                out=s_sgp[:], in_=sg[pt * 128:(pt + 1) * 128, :]), inc="ld")
            ldp = P.val("ld")
            P.op("vector", lambda e: e.tensor_tensor(
                out=s_y[:], in0=s_p0[:], in1=s_p1[:], op=ALU.add),
                waits=[("ld", ldp)], inc="v")
            P.op("vector", lambda e, pt=pt: e.scalar_tensor_tensor(
                out=s_y[:], in0=s_xcp[:], scalar=s_d[:, pt:pt + 1],
                in1=s_y[:], op0=ALU.mult, op1=ALU.add), inc="v")
            c = P.op("vector", lambda e, pt=pt: e.tensor_tensor(
                out=s_yg[pt][:], in0=s_y[:], in1=s_sgp[:], op=ALU.mult), inc="v")
            yg_done.append(c)
        v_yg = P.val("v")

        def stats2(xtile, sl, base_wait):
            c_sq = P.op("scalar", lambda e: e.activation(
                out=s_sq[:], in_=xtile, func=AF.Square,
                accum_out=s_stat[:, 2 * sl:2 * sl + 1]), waits=[base_wait], inc="s")
            c_sm = P.op("scalar", lambda e: e.activation(
                out=s_sq[:], in_=xtile, func=AF.Identity,
                accum_out=s_stat[:, 2 * sl + 1:2 * sl + 2]), inc="s")
            P.op("vector", lambda e: e.tensor_scalar(
                out=s_mu[:, sl:sl + 1], in0=s_stat[:, 2 * sl + 1:2 * sl + 2],
                scalar1=1.0 / D, scalar2=None, op0=ALU.mult),
                waits=[("s", c_sm)], inc="v")
            P.op("vector", lambda e: e.tensor_scalar(
                out=s_nv[:, sl:sl + 1], in0=s_stat[:, 2 * sl:2 * sl + 1],
                scalar1=1.0 / D, scalar2=None, op0=ALU.mult), inc="v")
            P.op("vector", lambda e: e.scalar_tensor_tensor(
                out=s_nv[:, sl:sl + 1], in0=s_mu[:, sl:sl + 1],
                scalar=s_mu[:, sl:sl + 1], in1=s_nv[:, sl:sl + 1],
                op0=ALU.mult, op1=ALU.subtract), inc="v")
            return P.val("v")

        def finish_scale(sl, combined):
            scale = -(1.0 + EPS) if combined else -1.0
            bias = EPS * EPS if combined else EPS
            P.op("vector", lambda e, sl=sl, scale=scale, bias=bias: e.tensor_scalar(
                out=s_sc[:, sl:sl + 1], in0=s_nv[:, sl:sl + 1],
                scalar1=scale, scalar2=bias, op0=ALU.mult, op1=ALU.add), inc="v")
            c_rt = P.op("scalar", lambda e, sl=sl: e.activation(
                out=s_sc[:, sl:sl + 1], in_=s_sc[:, sl:sl + 1],
                func=AF.Sqrt), waits=[("v", P.val("v"))], inc="s")
            P.op("vector", lambda e, sl=sl: e.reciprocal(
                out=s_sc[:, sl:sl + 1], in_=s_sc[:, sl:sl + 1]),
                waits=[("s", c_rt)], inc="v")
            P.op("vector", lambda e, sl=sl: e.scalar_tensor_tensor(
                out=s_bcs[:, sl:sl + 1], in0=s_mu[:, sl:sl + 1],
                scalar=-1.0, in1=s_sc[:, sl:sl + 1],
                op0=ALU.mult, op1=ALU.mult), inc="v")
            return P.val("v")

        ev_done = {}
        x_free = {}
        x1_free = {}
        mxv = 0
        for i in range(NTT):
            pb = p_mm[i % 2]
            for k in range(NPT):
                w = [("v", v_yg)]
                if k == 0 and i >= 2:
                    w.append(("v", ev_done[i - 2]))
                for bk in range(2):
                    P.op("tensor", lambda e, k=k, i=i, bk=bk, pb=pb: e.matmul(
                        out=pb[:, bk * 512:(bk + 1) * 512],
                        lhsT=s_yg[k][:, i * 128:(i + 1) * 128],
                        rhs=s_wob[k][:, bk * 512:(bk + 1) * 512],
                        start=(k == 0), stop=(k == NPT - 1)), waits=w, inc="t")
            t_mm = P.val("t")
            xb = s_x[i % 2]
            w = []
            if i >= 2:
                w.append(("v", x_free[i - 2]))
            P.dma("sync", lambda e, i=i, xb=xb: e.dma_start(
                out=xb[:], in_=xin[i * 128:(i + 1) * 128, :]), waits=w, inc="ld")
            ldx = P.val("ld")
            sl1 = 2 * i
            stats2(xb[:], sl1, ("ld", ldx))
            finish_scale(sl1, False)
            c_ln1 = P.op("scalar", lambda e, xb=xb, sl1=sl1: e.activation(
                out=s_ln1[:], in_=xb[:], func=AF.Identity,
                scale=s_sc[:, sl1:sl1 + 1], bias=s_bcs[:, sl1:sl1 + 1]),
                waits=[("v", P.val("v"))], inc="s")
            x1b = s_x1[i % 2]
            w = [("s", c_ln1)]
            if i >= 2:
                w.append(("v", x1_free[i - 2]))
                w.append(("g", 16 * (i - 1)))
            P.op("vector", lambda e, xb=xb: e.tensor_tensor(
                out=s_ln1[:], in0=s_ln1[:], in1=xb[:], op=ALU.add), waits=w, inc="v")
            P.op("vector", lambda e, pb=pb, x1b=x1b: e.tensor_tensor(
                out=x1b[:], in0=s_ln1[:], in1=pb[:], op=ALU.add),
                waits=[("t", t_mm)], inc="v")
            c_x1 = P.op("vector", lambda e, x1b=x1b: e.tensor_tensor(
                out=x1b[:], in0=x1b[:], in1=s_bo[:], op=ALU.add),
                waits=[("s", s_bcast)], inc="v")
            ev_done[i] = c_x1
            x_free[i] = c_x1
            P.dma("gpsimd", lambda e, i=i, x1b=x1b: e.dma_start(
                out=o_x1[i * 128:(i + 1) * 128, :], in_=x1b[:]),
                waits=[("v", c_x1)], inc="g")
            sl2 = 2 * i + 1
            stats2(x1b[:], sl2, ("v", c_x1))
            finish_scale(sl2, True)
            c_z = P.op("scalar", lambda e, x1b=x1b, sl2=sl2: e.activation(
                out=s_z[:], in_=x1b[:], func=AF.Identity,
                scale=s_sc[:, sl2:sl2 + 1], bias=s_bcs[:, sl2:sl2 + 1]),
                waits=[("v", P.val("v"))], inc="s")
            x1_free[i] = P.val("v") + 2
            P.op("vector", lambda e: e.tensor_tensor(
                out=s_z[:], in0=s_z[:], in1=s_ga[:], op=ALU.mult),
                waits=[("s", c_z)], inc="v")
            P.op("vector", lambda e: e.tensor_tensor(
                out=s_z[:], in0=s_z[:], in1=s_be[:], op=ALU.add), inc="v")
            P.op("vector", lambda e: e.tensor_reduce(
                out=s_mx[:, 1:2], in_=s_z[:], axis=AX.X, op=ALU.max,
                apply_absolute_value=True), inc="v")
            if i == 0:
                mxv = P.op("vector", lambda e: e.tensor_copy(
                    out=s_mx[:, 0:1], in_=s_mx[:, 1:2]), inc="v")
            else:
                mxv = P.op("vector", lambda e: e.tensor_tensor(
                    out=s_mx[:, 0:1], in0=s_mx[:, 0:1], in1=s_mx[:, 1:2],
                    op=ALU.max), inc="v")
            x1_free[i] = mxv

        P.op("tensor", lambda e: e.matmul(
            out=p_r[:], lhsT=s_mx[:, 0:1], rhs=s_idf[:], start=True, stop=True),
            waits=[("v", mxv)], inc="t")
        P.op("scalar", lambda e: e.activation(
            out=s_mrow[:], in_=p_r[:], func=AF.Identity),
            waits=[("t", P.val("t"))], inc="s")
        vlm = P.op("vector", lambda e: e.tensor_reduce(
            out=s_lm[:], in_=s_mrow[:], axis=AX.X, op=ALU.max),
            waits=[("s", P.val("s"))], inc="v")
        P.dma("gpsimd", lambda e: e.dma_start(out=o_lm, in_=s_lm[:]),
              waits=[("v", vlm)], inc="g")
        g_final = P.val("g")
        P.op("gpsimd", lambda e: e.engine_nop(), waits=[("g", g_final)])
        P.emit(block, sems)
    return nc


def build_l4():
    nc = bass.Bass("TRN2", target_bir_lowering=False, debug=False,
                   detect_race_conditions=False)

    x1 = nc.dram_tensor("x1", [T, D], F32, kind="ExternalInput").ap()
    wff = nc.dram_tensor("wff", [D, D], F32, kind="ExternalInput").ap()
    lmax8 = nc.dram_tensor("lmax8", [1, NCORES], F32, kind="ExternalInput").ap()
    g_row = nc.dram_tensor("g_row", [1, D], F32, kind="ExternalInput").ap()
    be_row = nc.dram_tensor("be_row", [1, D], F32, kind="ExternalInput").ap()
    ones1 = nc.dram_tensor("ones1", [1, 128], F32, kind="ExternalInput").ap()
    identf = nc.dram_tensor("identf", [128, 128], F32, kind="ExternalInput").ap()
    identb = nc.dram_tensor("identb", [128, 128], BF16, kind="ExternalInput").ap()

    o_x2 = nc.dram_tensor("o_x2", [T, D], F32, kind="ExternalOutput").ap()

    NKB = D // 128
    with contextlib.ExitStack() as ctx:
        sb = lambda n, s, d=F32: ctx.enter_context(nc.sbuf_tensor(n, s, d))
        ps = lambda n, s, d=F32: ctx.enter_context(nc.psum_tensor(n, s, d))

        s_w = [sb(f"s_w{i}", [128, D]) for i in range(2)]
        s_absw = [sb(f"s_absw{i}", [128, D]) for i in range(NKB)]
        s_sgnb = [sb(f"s_sgnb{i}", [128, D], BF16) for i in range(NKB)]
        s_scr = sb("s_scr", [128, D])
        s_mask = sb("s_mask", [128, D])
        s_m0 = sb("s_m0", [128, D], BF16)
        s_mT = [sb(f"s_mT{i}", [128, D], BF16) for i in range(NKB)]
        s_cols = sb("s_cols", [128, 3 * NKB])
        s_red = sb("s_red", [128, 4])
        s_row = sb("s_row", [1, 128])
        s_sca = sb("s_sca", [1, 12])
        s_dcol = sb("s_dcol", [128, 1])
        s_qcol = sb("s_qcol", [128, 1])
        s_acol = sb("s_acol", [128, 1])
        s_lmax = sb("s_lmax", [1, NCORES])
        s_grow = sb("s_grow", [1, D])
        s_berow = sb("s_berow", [1, D])
        s_ga = sb("s_ga", [128, D])
        s_be = sb("s_be", [128, D])
        s_ones1 = sb("s_ones1", [1, 128])
        s_idf = sb("s_idf", [128, 128])
        s_idb = sb("s_idb", [128, 128], BF16)
        s_x1 = [sb(f"s_x1{i}", [128, D]) for i in range(2)]
        s_sq = sb("s_sq", [128, D])
        s_stat = sb("s_stat", [128, 2 * NTT])
        s_mu = sb("s_mu", [128, NTT])
        s_nv = sb("s_nv", [128, NTT])
        s_sc = sb("s_sc", [128, NTT])
        s_bcs = sb("s_bcs", [128, NTT])
        s_z = sb("s_z", [128, D])
        s_a0 = sb("s_a0", [128, D])
        s_sgn = sb("s_sgn", [128, D])
        s_b = sb("s_b", [128, D])
        s_qtm = sb("s_qtm", [128, D], BF16)
        s_qfm = [sb(f"s_qfm{i}", [128, T], BF16) for i in range(NKB)]
        s_t2 = sb("s_t2", [128, D])

        p_b = ps("p_b", [128, D])
        p_c1 = p_b[:, 0:1]
        p_tr = ps("p_tr", [128, 512], BF16)
        p_tr2 = ps("p_tr2", [128, 512], BF16)
        p_trx = [p_tr, p_tr2]
        p_mm = [ps(f"p_mm{i}", [128, D]) for i in range(2)]
        p_r = p_b[0:1, 0:128]

        sems = make_sems(ctx, nc, ("ld", "wld", "v", "s", "t", "g", "p"))
        block = ctx.enter_context(nc.Block())
        P = Prog()

        P.dma("sync", lambda e: e.dma_start(out=s_lmax[:], in_=lmax8), inc="ld")
        P.dma("sync", lambda e: e.dma_start(out=s_grow[:], in_=g_row), inc="ld")
        P.dma("sync", lambda e: e.dma_start(out=s_berow[:], in_=be_row), inc="ld")
        P.dma("sync", lambda e: e.dma_start(out=s_ones1[:], in_=ones1), inc="ld")
        P.dma("sync", lambda e: e.dma_start(out=s_idf[:], in_=identf), inc="ld")
        P.dma("sync", lambda e: e.dma_start(out=s_idb[:], in_=identb), inc="ld")
        ld0 = P.val("ld")

        prev_ev = 0
        for (row, dst) in ((s_grow, s_ga), (s_berow, s_be)):
            for bk in range(2):
                P.op("tensor", lambda e, row=row, bk=bk: e.matmul(
                    out=p_b[:, bk * 512:(bk + 1) * 512], lhsT=s_ones1[0:1, :],
                    rhs=row[0:1, bk * 512:(bk + 1) * 512], start=True, stop=True),
                    waits=[("ld", ld0), ("s", prev_ev)], inc="t")
            prev_ev = P.op("scalar", lambda e, dst=dst: e.activation(
                out=dst[:], in_=p_b[:], func=AF.Identity),
                waits=[("t", P.val("t"))], inc="s")
        s_bcast = P.val("s")

        sgn_done = {}
        for t in range(NKB):
            stg = s_w[t % 2]
            w = []
            if t >= 2:
                w.append(("s", sgn_done[t - 2]))
            P.dma("sync", lambda e, t=t, stg=stg: e.dma_start(
                out=stg[:], in_=wff[t * 128:(t + 1) * 128, :]), waits=w, inc="wld")
            P.op("scalar", lambda e, t=t, stg=stg: e.activation(
                out=s_absw[t][:], in_=stg[:], func=AF.Abs,
                accum_out=s_cols[:, t:t + 1]),
                waits=[("wld", 16 * (t + 1))], inc="s")
            c = P.op("scalar", lambda e, t=t, stg=stg: e.activation(
                out=s_sgnb[t][:], in_=stg[:], func=AF.Sign), inc="s")
            sgn_done[t] = c
        s_wabs = P.val("s")

        def col_to_scalar(col_ap, dst_slot, op):
            ct = P.op("tensor", lambda e: e.matmul(
                out=p_r, lhsT=col_ap, rhs=s_idf[:], start=True, stop=True),
                waits=[("v", P.val("v")), ("s", P.val("s"))], inc="t")
            cs = P.op("scalar", lambda e: e.activation(
                out=s_row[:], in_=p_r, func=AF.Identity),
                waits=[("t", ct)], inc="s")
            return P.op("vector", lambda e: e.tensor_reduce(
                out=s_sca[0:1, dst_slot:dst_slot + 1], in_=s_row[:],
                axis=AX.X, op=op), waits=[("s", cs)], inc="v")

        def bcast_col(src_slot, dst):
            ct = P.op("tensor", lambda e: e.matmul(
                out=p_c1, lhsT=s_ones1[0:1, :],
                rhs=s_sca[0:1, src_slot:src_slot + 1], start=True, stop=True),
                waits=[("v", P.val("v")), ("s", P.val("s"))], inc="t")
            return P.op("scalar", lambda e, dst=dst: e.activation(
                out=dst[:], in_=p_c1[:], func=AF.Identity),
                waits=[("t", ct)], inc="s")

        P.op("vector", lambda e: e.tensor_reduce(
            out=s_red[:, 0:1], in_=s_cols[:, 0:NKB], axis=AX.X, op=ALU.add),
            waits=[("s", s_wabs)], inc="v")
        col_to_scalar(s_red[:, 0:1], 0, ALU.add)
        P.op("vector", lambda e: e.tensor_scalar(
            out=s_sca[0:1, 1:2], in0=s_sca[0:1, 0:1],
            scalar1=0.7 / (D * D), scalar2=None, op0=ALU.mult), inc="v")
        c_dcol = bcast_col(1, s_dcol)

        tr_ev = {}
        ntr = 0
        t_tr_prev = 0
        for t in range(NKB):
            vm = P.op("vector", lambda e, t=t: e.tensor_scalar(
                out=s_mask[:], in0=s_absw[t][:], scalar1=s_dcol[:, 0:1],
                scalar2=None, op0=ALU.is_gt),
                waits=[("s", c_dcol), ("t", t_tr_prev)], inc="v")
            P.op("vector", lambda e, t=t: e.tensor_tensor(
                out=s_scr[:], in0=s_absw[t][:], in1=s_mask[:], op=ALU.mult), inc="v")
            P.op("vector", lambda e, t=t: e.tensor_scalar(
                out=s_scr[:], in0=s_scr[:], scalar1=0.0, scalar2=0.0,
                op0=ALU.add, op1=ALU.add,
                accum_out=s_cols[:, NKB + t:NKB + t + 1]), inc="v")
            P.op("vector", lambda e, t=t: e.tensor_scalar(
                out=s_scr[:], in0=s_mask[:], scalar1=0.0, scalar2=0.0,
                op0=ALU.add, op1=ALU.add,
                accum_out=s_cols[:, 2 * NKB + t:2 * NKB + t + 1]), inc="v")
            vz = P.op("vector", lambda e, t=t: e.tensor_tensor(
                out=s_m0[:], in0=s_sgnb[t][:], in1=s_mask[:], op=ALU.mult), inc="v")
            for k in range(NKB):
                ptr = p_trx[ntr % 2]
                wt = [("v", vz)]
                if ntr >= 2:
                    wt.append(("s", tr_ev[ntr - 2]))
                ct = P.op("tensor", lambda e, k=k, ptr=ptr: e.transpose(
                    out=ptr[:, 0:128], in_=s_m0[:, k * 128:(k + 1) * 128],
                    identity=s_idb[:]), waits=wt, inc="t")
                ce = P.op("scalar", lambda e, k=k, t=t, ptr=ptr: e.activation(
                    out=s_mT[k][:, t * 128:(t + 1) * 128], in_=ptr[:, 0:128],
                    func=AF.Identity), waits=[("t", ct)], inc="s")
                tr_ev[ntr] = ce
                ntr += 1
            t_tr_prev = P.val("t")
        s_mT_done = P.val("s")

        P.op("vector", lambda e: e.tensor_reduce(
            out=s_red[:, 1:2], in_=s_cols[:, NKB:2 * NKB], axis=AX.X, op=ALU.add),
            waits=[("v", P.val("v"))], inc="v")
        P.op("vector", lambda e: e.tensor_reduce(
            out=s_red[:, 2:3], in_=s_cols[:, 2 * NKB:3 * NKB], axis=AX.X,
            op=ALU.add), inc="v")
        col_to_scalar(s_red[:, 1:2], 2, ALU.add)
        col_to_scalar(s_red[:, 2:3], 3, ALU.add)
        P.op("vector", lambda e: e.tensor_scalar(
            out=s_sca[0:1, 3:4], in0=s_sca[0:1, 3:4], scalar1=1.0,
            scalar2=None, op0=ALU.max), inc="v")
        P.op("vector", lambda e: e.reciprocal(
            out=s_sca[0:1, 4:5], in_=s_sca[0:1, 3:4]), inc="v")
        P.op("vector", lambda e: e.tensor_tensor(
            out=s_sca[0:1, 5:6], in0=s_sca[0:1, 2:3], in1=s_sca[0:1, 4:5],
            op=ALU.mult), inc="v")
        P.op("vector", lambda e: e.tensor_reduce(
            out=s_sca[0:1, 6:7], in_=s_lmax[:], axis=AX.X, op=ALU.max),
            waits=[("ld", ld0)], inc="v")
        P.op("vector", lambda e: e.tensor_scalar(
            out=s_sca[0:1, 6:7], in0=s_sca[0:1, 6:7], scalar1=1e-8,
            scalar2=None, op0=ALU.max), inc="v")
        P.op("vector", lambda e: e.reciprocal(
            out=s_sca[0:1, 7:8], in_=s_sca[0:1, 6:7]), inc="v")
        P.op("vector", lambda e: e.tensor_scalar(
            out=s_sca[0:1, 8:9], in0=s_sca[0:1, 7:8], scalar1=127.0,
            scalar2=None, op0=ALU.mult), inc="v")
        P.op("vector", lambda e: e.tensor_tensor(
            out=s_sca[0:1, 9:10], in0=s_sca[0:1, 5:6], in1=s_sca[0:1, 6:7],
            op=ALU.mult), inc="v")
        P.op("vector", lambda e: e.tensor_scalar(
            out=s_sca[0:1, 9:10], in0=s_sca[0:1, 9:10], scalar1=1.0 / 127.0,
            scalar2=None, op0=ALU.mult), inc="v")
        c_qcol = bcast_col(8, s_qcol)
        c_acol = bcast_col(9, s_acol)

        def stats2(xtile, sl, base_wait):
            c_sq = P.op("scalar", lambda e: e.activation(
                out=s_sq[:], in_=xtile, func=AF.Square,
                accum_out=s_stat[:, 2 * sl:2 * sl + 1]), waits=[base_wait], inc="s")
            c_sm = P.op("scalar", lambda e: e.activation(
                out=s_sq[:], in_=xtile, func=AF.Identity,
                accum_out=s_stat[:, 2 * sl + 1:2 * sl + 2]), inc="s")
            P.op("vector", lambda e: e.tensor_scalar(
                out=s_mu[:, sl:sl + 1], in0=s_stat[:, 2 * sl + 1:2 * sl + 2],
                scalar1=1.0 / D, scalar2=None, op0=ALU.mult),
                waits=[("s", c_sm)], inc="v")
            P.op("vector", lambda e: e.tensor_scalar(
                out=s_nv[:, sl:sl + 1], in0=s_stat[:, 2 * sl:2 * sl + 1],
                scalar1=1.0 / D, scalar2=None, op0=ALU.mult), inc="v")
            P.op("vector", lambda e: e.scalar_tensor_tensor(
                out=s_nv[:, sl:sl + 1], in0=s_mu[:, sl:sl + 1],
                scalar=s_mu[:, sl:sl + 1], in1=s_nv[:, sl:sl + 1],
                op0=ALU.mult, op1=ALU.subtract), inc="v")
            P.op("vector", lambda e, sl=sl: e.tensor_scalar(
                out=s_sc[:, sl:sl + 1], in0=s_nv[:, sl:sl + 1],
                scalar1=-(1.0 + EPS), scalar2=EPS * EPS,
                op0=ALU.mult, op1=ALU.add), inc="v")
            c_rt = P.op("scalar", lambda e, sl=sl: e.activation(
                out=s_sc[:, sl:sl + 1], in_=s_sc[:, sl:sl + 1],
                func=AF.Sqrt), waits=[("v", P.val("v"))], inc="s")
            P.op("vector", lambda e, sl=sl: e.reciprocal(
                out=s_sc[:, sl:sl + 1], in_=s_sc[:, sl:sl + 1]),
                waits=[("s", c_rt)], inc="v")
            P.op("vector", lambda e, sl=sl: e.scalar_tensor_tensor(
                out=s_bcs[:, sl:sl + 1], in0=s_mu[:, sl:sl + 1],
                scalar=-1.0, in1=s_sc[:, sl:sl + 1],
                op0=ALU.mult, op1=ALU.mult), inc="v")
            return P.val("v")

        x1_free = {}
        mm_free = {}
        qtr = 0
        qtr_ev = {}
        q_tr_prev = 0
        for i in range(NTT):
            xb = s_x1[i % 2]
            w = []
            if i >= 2:
                w.append(("v", x1_free[i - 2]))
            P.dma("sync", lambda e, i=i, xb=xb: e.dma_start(
                out=xb[:], in_=x1[i * 128:(i + 1) * 128, :]), waits=w, inc="ld")
            v_st = stats2(xb[:], i, ("ld", P.val("ld")))
            c_z = P.op("scalar", lambda e, xb=xb, i=i: e.activation(
                out=s_z[:], in_=xb[:], func=AF.Identity,
                scale=s_sc[:, i:i + 1], bias=s_bcs[:, i:i + 1]),
                waits=[("v", v_st)], inc="s")
            P.op("vector", lambda e: e.tensor_tensor(
                out=s_z[:], in0=s_z[:], in1=s_ga[:], op=ALU.mult),
                waits=[("s", c_z), ("s", s_bcast)], inc="v")
            P.op("vector", lambda e: e.tensor_tensor(
                out=s_z[:], in0=s_z[:], in1=s_be[:], op=ALU.add), inc="v")
            P.op("vector", lambda e: e.tensor_scalar(
                out=s_z[:], in0=s_z[:], scalar1=s_qcol[:, 0:1], scalar2=None,
                op0=ALU.mult), waits=[("s", c_qcol)], inc="v")
            P.op("vector", lambda e: e.tensor_scalar(
                out=s_z[:], in0=s_z[:], scalar1=127.0, scalar2=None,
                op0=ALU.min), inc="v")
            vz = P.op("vector", lambda e: e.tensor_scalar(
                out=s_z[:], in0=s_z[:], scalar1=-127.0, scalar2=None,
                op0=ALU.max), inc="v")
            c_a0 = P.op("scalar", lambda e: e.activation(
                out=s_a0[:], in_=s_z[:], func=AF.Abs),
                waits=[("v", vz), ("t", q_tr_prev)], inc="s")
            c_sg = P.op("scalar", lambda e: e.activation(
                out=s_sgn[:], in_=s_z[:], func=AF.Sign), inc="s")
            P.op("vector", lambda e: e.tensor_copy(
                out=s_z[:], in_=s_a0[:]), waits=[("s", c_sg)], inc="v")
            for kb in range(6, -1, -1):
                P.op("vector", lambda e, kb=kb: e.tensor_scalar(
                    out=s_b[:], in0=s_z[:], scalar1=(2.0 ** kb) - 0.5,
                    scalar2=float(2 ** kb), op0=ALU.is_ge, op1=ALU.mult), inc="v")
                P.op("vector", lambda e: e.tensor_tensor(
                    out=s_z[:], in0=s_z[:], in1=s_b[:], op=ALU.subtract), inc="v")
            P.op("vector", lambda e: e.tensor_tensor(
                out=s_b[:], in0=s_a0[:], in1=s_z[:], op=ALU.subtract), inc="v")
            vq = P.op("vector", lambda e: e.tensor_tensor(
                out=s_qtm[:], in0=s_b[:], in1=s_sgn[:], op=ALU.mult),
                waits=[("t", q_tr_prev)], inc="v")
            for k in range(NKB):
                ptr = p_trx[qtr % 2]
                wt = [("v", vq), ("s", s_mT_done)]
                if qtr >= 2:
                    wt.append(("s", qtr_ev[qtr - 2]))
                ct = P.op("tensor", lambda e, k=k, ptr=ptr: e.transpose(
                    out=ptr[:, 0:128], in_=s_qtm[:, k * 128:(k + 1) * 128],
                    identity=s_idb[:]), waits=wt, inc="t")
                ce = P.op("scalar", lambda e, k=k, i=i, ptr=ptr: e.activation(
                    out=s_qfm[k][:, i * 128:(i + 1) * 128], in_=ptr[:, 0:128],
                    func=AF.Identity), waits=[("t", ct)], inc="s")
                qtr_ev[qtr] = ce
                qtr += 1
            q_tr_prev = P.val("t")
            q_done = P.val("s")
            pb = p_mm[i % 2]
            for k in range(NKB):
                w = [("s", q_done)]
                if k == 0 and i >= 2:
                    w.append(("s", mm_free[i - 2]))
                for bk in range(2):
                    P.op("tensor", lambda e, k=k, i=i, bk=bk, pb=pb: e.matmul(
                        out=pb[:, bk * 512:(bk + 1) * 512],
                        lhsT=s_qfm[k][:, i * 128:(i + 1) * 128],
                        rhs=s_mT[k][:, bk * 512:(bk + 1) * 512],
                        start=(k == 0), stop=(k == NKB - 1)), waits=w, inc="t")
            t_mm = P.val("t")
            wt2 = [("t", t_mm), ("s", c_acol)]
            if i >= 1:
                wt2.append(("g", 16 * i))
            c_t2 = P.op("scalar", lambda e, pb=pb: e.activation(
                out=s_t2[:], in_=pb[:], func=AF.Identity,
                scale=s_acol[:, 0:1]), waits=wt2, inc="s")
            mm_free[i] = c_t2
            vx2 = P.op("vector", lambda e, xb=xb: e.tensor_tensor(
                out=s_t2[:], in0=s_t2[:], in1=xb[:], op=ALU.add),
                waits=[("s", c_t2)], inc="v")
            x1_free[i] = vx2
            P.dma("gpsimd", lambda e, i=i: e.dma_start(
                out=o_x2[i * 128:(i + 1) * 128, :], in_=s_t2[:]),
                waits=[("v", vx2)], inc="g")

        g_final = P.val("g")
        P.op("gpsimd", lambda e: e.engine_nop(), waits=[("g", g_final)])
        P.emit(block, sems)
    return nc


_NC_CACHE = {}


def _get(name, builder):
    if name not in _NC_CACHE:
        _NC_CACHE[name] = builder()
    return _NC_CACHE[name]


def kernel(L0_attn_x, L0_attn_ln_w, L0_attn_ln_b, L0_attn_W_in, L0_attn_b_in,
           L0_attn_conv_w, L0_attn_conv_b, L0_attn_A_log, L0_attn_D,
           L0_attn_W_x, L0_attn_b_x, L0_attn_W_out, L0_attn_b_out,
           L0_ffn_weight, L0_ffn_gamma, L0_ffn_beta):
    f32 = np.float32
    x = np.ascontiguousarray(np.asarray(L0_attn_x, f32))
    w_in = np.ascontiguousarray(np.asarray(L0_attn_W_in, f32))
    b_in = np.asarray(L0_attn_b_in, f32).reshape(1, DG)
    ln_w = np.asarray(L0_attn_ln_w, f32).reshape(D, 1)
    ln_b = np.asarray(L0_attn_ln_b, f32).reshape(D, 1)
    conv_w = np.ascontiguousarray(np.asarray(L0_attn_conv_w, f32).reshape(DI, 4))
    conv_b = np.asarray(L0_attn_conv_b, f32).reshape(DI, 1)
    w_x = np.ascontiguousarray(np.asarray(L0_attn_W_x, f32))
    b_x = np.asarray(L0_attn_b_x, f32).reshape(2 * NST, 1)
    a_log = np.ascontiguousarray(np.asarray(L0_attn_A_log, f32))
    dvec = np.asarray(L0_attn_D, f32).reshape(DI, 1)
    w_out = np.ascontiguousarray(np.asarray(L0_attn_W_out, f32))
    b_out = np.asarray(L0_attn_b_out, f32).reshape(1, D)
    wff = np.ascontiguousarray(np.asarray(L0_ffn_weight, f32))
    g_row = np.asarray(L0_ffn_gamma, f32).reshape(1, D)
    be_row = np.asarray(L0_ffn_beta, f32).reshape(1, D)

    identb = np.eye(128, dtype=BF)
    identf = np.eye(128, dtype=f32)
    ones1 = np.ones((1, 128), f32)
    oh = np.zeros((NH, NH * 128), BF)
    for n in range(NH):
        oh[n, n * 128:(n + 1) * 128] = 1
    zeros_halo = np.zeros((HALO, D), f32)

    cores = list(range(NCORES))

    in1 = []
    for c in cores:
        b, h = c // 2, c % 2
        if h == 0:
            xs = np.concatenate([zeros_halo, x[b, 0:T]], 0)
        else:
            xs = np.ascontiguousarray(x[b, T - HALO:2 * T])
        hm = np.ones((128, 1), f32) if h == 1 else np.zeros((128, 1), f32)
        in1.append(dict(xin=xs, w_in=w_in, b_in=b_in, ln_w=ln_w, ln_b=ln_b,
                        conv_w=conv_w, conv_b=conv_b, w_x=w_x,
                        b_xv=b_x, identb=identb, ones1=ones1, hmask=hm))
    r1 = run_bass_kernel_spmd(_get("l1", build_l1), in1, core_ids=cores).results

    xc_full = [np.concatenate([np.asarray(r1[2 * b]["o_xc"]),
                               np.asarray(r1[2 * b + 1]["o_xc"])], 1)
               for b in range(BATCH)]
    b_full = [np.concatenate([np.asarray(r1[2 * b]["o_b"]),
                              np.asarray(r1[2 * b + 1]["o_b"])], 1)
              for b in range(BATCH)]
    c_full = [np.concatenate([np.asarray(r1[2 * b]["o_c"]),
                              np.asarray(r1[2 * b + 1]["o_c"])], 1)
              for b in range(BATCH)]
    in2 = []
    for c in cores:
        b, nh = c // 2, c % 2
        in2.append(dict(xc=xc_full[b],
                        bh=np.ascontiguousarray(b_full[b][nh * NH:(nh + 1) * NH]),
                        ch=np.ascontiguousarray(c_full[b][nh * NH:(nh + 1) * NH]),
                        alog=np.ascontiguousarray(a_log[:, nh * NH:(nh + 1) * NH]),
                        oh=oh))
    r2 = run_bass_kernel_spmd(_get("l2", build_l2), in2, core_ids=cores).results

    in3 = []
    for c in cores:
        b, h = c // 2, c % 2
        sl = np.s_[:, h * T:(h + 1) * T]
        in3.append(dict(
            xin=np.ascontiguousarray(x[b, h * T:(h + 1) * T]),
            p0=np.ascontiguousarray(np.asarray(r2[2 * b]["o_y"])[sl]),
            p1=np.ascontiguousarray(np.asarray(r2[2 * b + 1]["o_y"])[sl]),
            xc=np.asarray(r1[c]["o_xc"]),
            sg=np.asarray(r1[c]["o_sg"]),
            dvec=dvec, w_out=w_out, b_out=b_out,
            g_row=g_row, be_row=be_row, ones1=ones1, identf=identf))
    r3 = run_bass_kernel_spmd(_get("l3", build_l3), in3, core_ids=cores).results

    lmax8 = np.concatenate([np.asarray(r3[c]["o_lm"]).reshape(1, 1)
                            for c in cores], 1)
    in4 = []
    for c in cores:
        in4.append(dict(x1=np.asarray(r3[c]["o_x1"]), wff=wff, lmax8=lmax8,
                        g_row=g_row, be_row=be_row, ones1=ones1,
                        identf=identf, identb=identb))
    r4 = run_bass_kernel_spmd(_get("l4", build_l4), in4, core_ids=cores).results

    out = np.empty((BATCH, TFULL, D), f32)
    for c in cores:
        b, h = c // 2, c % 2
        out[b, h * T:(h + 1) * T] = np.asarray(r4[c]["o_x2"])
    if os.environ.get("KDEBUG"):
        _DBG.update(r1=r1, r2=r2, r3=r3, r4=r4, lmax8=lmax8)
    return out


_DBG = {}
